# version 01
# baseline (speedup 1.0000x reference)
import numpy as np
import ml_dtypes
import concourse.bass as bass
import concourse.mybir as mybir
from concourse.bass_utils import run_bass_kernel_spmd

F32 = mybir.dt.float32
BF16 = mybir.dt.bfloat16
AF = mybir.ActivationFunctionType
ALU = mybir.AluOpType
AXX = mybir.AxisListType.X

MAGIC = float(np.float32(1.5 * 2 ** 23))
T = 2048
C = 1024
NCC = 8
NTC = 16
II = 2816
NIC = 22
NHEAD = 16
EPS = 1e-5

ENGS = ("t", "v", "s", "g", "y")


class B:

    def __init__(self, nc):
        self.nc = nc
        self.q = {e: [] for e in ENGS}
        self.sems = {}
        self.cnt = {}
        self.guards = []

    def sem(self, name):
        g = self.nc.semaphore(name)
        h = g.__enter__()
        self.guards.append(g)
        self.sems[name] = h
        self.cnt[name] = 0
        return name

    def emit(self, e, fn):
        self.q[e].append(fn)

    def inc(self, e, mk, sem, k=1):
        h = self.sems[sem]
        self.emit(e, lambda eng, mk=mk, h=h, k=k: mk(eng).then_inc(h, k))
        self.cnt[sem] += k
        return self.cnt[sem]

    def run(self, e, mk):
        self.emit(e, mk)

    def wait(self, e, sem, val=None):
        h = self.sems[sem]
        v = self.cnt[sem] if val is None else val
        if v:
            self.emit(e, lambda eng, h=h, v=v: eng.wait_ge(h, v))

    def mark(self, sem):
        return self.cnt[sem]

    def barrier(self):
        snap = {s: self.cnt[s] for s in self.sems}
        for e in ENGS:
            for s, v in snap.items():
                if v:
                    self.wait(e, s, v)


def build_nc(debug=False):
    nc = bass.Bass("TRN2", target_bir_lowering=False)

    d_x = nc.dram_tensor("x", [T, C], F32, kind="ExternalInput")
    d_wqkv = nc.dram_tensor("wqkv", [C, 3 * C], F32, kind="ExternalInput")
    d_bqkv = nc.dram_tensor("bqkv", [3 * C], F32, kind="ExternalInput")
    d_wout = nc.dram_tensor("wout", [C, C], F32, kind="ExternalInput")
    d_bout = nc.dram_tensor("bout", [C], F32, kind="ExternalInput")
    d_clipx = nc.dram_tensor("clipx", [1], F32, kind="ExternalInput")
    d_clipd = nc.dram_tensor("clipd", [1], F32, kind="ExternalInput")
    d_wg = nc.dram_tensor("wg", [II, C], F32, kind="ExternalInput")
    d_wu = nc.dram_tensor("wu", [II, C], F32, kind="ExternalInput")
    d_wd = nc.dram_tensor("wd", [C, II], F32, kind="ExternalInput")
    d_ident = nc.dram_tensor("cident", [128, 128], F32, kind="ExternalInput")
    d_cos = nc.dram_tensor("ccos", [128, T], F32, kind="ExternalInput")
    d_sin = nc.dram_tensor("csin", [128, T], F32, kind="ExternalInput")
    d_crows = nc.dram_tensor("crows", [128, 384], F32, kind="ExternalInput")
    d_out = nc.dram_tensor("out", [T, C], F32, kind="ExternalOutput")

    d_wgt = nc.dram_tensor("wgt_s", [NIC, 128, C], BF16)
    d_wut = nc.dram_tensor("wut_s", [NIC, 128, C], BF16)
    d_wdt = nc.dram_tensor("wdt_s", [NCC, 128, II], BF16)
    d_ccin = nc.dram_tensor("ccin", [1, 8], F32)
    d_ccout = nc.dram_tensor("ccout", [1, 8], F32)

    dbg = {}
    if debug:
        dbg["x1"] = nc.dram_tensor("dbg_x1", [T, C], F32, kind="ExternalOutput")
        dbg["x2"] = nc.dram_tensor("dbg_x2", [T, C], F32, kind="ExternalOutput")
        dbg["sc"] = nc.dram_tensor("dbg_sc", [1, 64], F32, kind="ExternalOutput")

    guards = []

    def sb(name, shape, dt):
        g = nc.sbuf_tensor(name, shape, dt)
        t = g.__enter__()
        guards.append(g)
        return t

    def psu(name, shape):
        g = nc.psum_tensor(name, shape, F32)
        t = g.__enter__()
        guards.append(g)
        return t

    a_x = sb("a_x", [128, NCC * T], BF16)
    a_b = sb("a_b", [128, NCC * T], BF16)
    a_big = sb("a_big", [128, 49408], BF16)
    a_scr = sb("a_scr", [128, 11264], BF16)
    a_f32 = sb("a_f32", [128, 3840], F32)
    a_col = sb("a_col", [128, 256], F32)
    a_row = sb("a_row", [128, 256], F32)
    a_rowb = sb("a_rowb", [128, 1024], BF16)
    a_id = sb("a_id", [128, 128], F32)
    a_idb = sb("a_idb", [128, 128], BF16)
    a_crw = sb("a_crw", [128, 384], F32)
    a_crwb = sb("a_crwb", [128, 384], BF16)
    a_bq = sb("a_bq", [128, 24], F32)
    a_bo = sb("a_bo", [128, 8], F32)
    pA = psu("pA", [128, 2048])
    pB = psu("pB", [128, 2048])

    def xfm(g):
        return a_x[:, g * T:(g + 1) * T]

    def bfm(g):
        return a_b[:, g * T:(g + 1) * T]

    def tcode(i):
        return a_big[:, i * T:(i + 1) * T]

    def qfm(c):
        return a_big[:, c * T:(c + 1) * T]

    def kfm(c):
        return a_big[:, (8 + c) * T:(9 + c) * T]

    def vaug(tc):
        o = 16 * T + tc * (NHEAD * 65)
        return a_big[:, o:o + NHEAD * 65]

    S_BC0 = 0
    S_BC1 = 2048
    S_ST0 = 4096
    S_ST1 = 5120
    S_ST2 = 6144
    S_PT = 7168

    def scr(o, n):
        return a_scr[:, o:o + n]

    FU = 0
    FW = 1792

    C_SG, C_SU, C_SD = 0, 32, 64
    C_MX = 80
    C_KG = 112
    C_RC = 114
    C_FIN = 116
    C_T1, C_T2 = 120, 121
    C_C = 124
    C_INV = 128
    C_F3 = 136
    C_SX = 144
    C_BIAS = 152
    C_MQ = 176

    R_SGX = 0
    R_SGD = 4
    R_LM = 8
    R_GM = 12
    R_SX = 16
    R_INV = 20
    R_SD = 24
    R_F2 = 28
    R_F3 = 32
    R_MQ = 40
    R_MK = 64
    R_BE = 88
    R_CC = 128
    R_TMP = 160

    RB_Z = 0

    k = B(nc)
    sSY = k.sem("sSY")
    sGP = k.sem("sGP")
    sSC = k.sem("sSC")
    sT = k.sem("sT")
    sV = k.sem("sV")
    sS = k.sem("sS")
    sG = k.sem("sG")
    sCC = k.sem("sCC")

    def dma(queue, out, in_, slow=False):
        sem = {"y": sSY, "g": sGP, "s": sSC}[queue]
        return k.inc(queue, lambda e, o=out, i=in_, sl=slow: e.dma_start(
            out=o, in_=i, allow_slow_non_contiguous=sl), sem, 16)

    def mm(out, lhsT, rhs, start, stop, sem=None):
        def go(e, o=out, l=lhsT, r=rhs, st=start, sp=stop):
            return e.matmul(o, l, r, start=st, stop=sp)
        if sem:
            return k.inc("t", go, sem)
        k.run("t", go)

    def vo(e, mk, sem=None):
        if sem:
            return k.inc(e, mk, sem)
        k.run(e, mk)

    dma("y", a_id[:, :], d_ident[:, :])
    dma("y", a_crw[:, :], d_crows[:, :])
    dma("g", a_b.ap().rearrange("p (t c) -> p t c", t=NTC),
        d_x.ap().rearrange("(t p) c -> p t c", p=128))
    dma("y", a_bq[:, :], d_bqkv.ap().rearrange("(f p) -> p f", p=128), slow=True)
    dma("y", a_bo[:, :], d_bout.ap().rearrange("(f p) -> p f", p=128), slow=True)
    dma("y", a_f32[0:1, 0:1024], d_bqkv.ap()[2048:3072].rearrange("(a f) -> a f", a=1))
    dma("y", a_row[0:1, R_SGX:R_SGX + 1], d_clipx.ap().rearrange("(b a) -> b a", b=1))
    dma("y", a_row[0:1, R_SGD:R_SGD + 1], d_clipd.ap().rearrange("(b a) -> b a", b=1))

    k.wait("v", sSY)
    k.wait("v", sGP)
    vo("v", lambda e: e.tensor_copy(a_idb[:, :], a_id[:, :]))
    vo("v", lambda e: e.tensor_copy(a_crwb[:, :], a_crw[:, :]))
    vo("v", lambda e: e.memset(a_col[:, C_C:C_C + 1], MAGIC))
    vo("v", lambda e: e.tensor_copy(a_rowb[0:1, 0:1024], a_f32[0:1, 0:1024]),
       sem=sV)
    k.wait("s", sSY)
    vo("s", lambda e: e.activation(a_row[0:1, R_SGX:R_SGX + 1],
                                   a_row[0:1, R_SGX:R_SGX + 1], AF.Sigmoid))
    vo("s", lambda e: e.activation(a_row[0:1, R_SGD:R_SGD + 1],
                                   a_row[0:1, R_SGD:R_SGD + 1], AF.Sigmoid), sem=sS)

    k.wait("t", sV)
    k.wait("t", sGP)
    for sl in range(2):
        mm(pA[:, sl * 512:(sl + 1) * 512], a_crwb[0:1, 0:128],
           a_rowb[0:1, sl * 512:(sl + 1) * 512], start=True, stop=True,
           sem=sT if sl == 1 else None)
    k.wait("v", sT)
    BVB = a_rowb[:, 0:1024]
    vo("v", lambda e: e.tensor_copy(BVB, pA[:, 0:1024]), sem=sV)

    for tc in range(NTC):
        for half in range(2):
            pp = (pA if half == 0 else pB)[:, 1024 + (tc % 2) * 512:
                                           1024 + (tc % 2) * 512 + 512]
            for j in range(4):
                g = half * 4 + j
                mm(pp[:, j * 128:(j + 1) * 128],
                   a_b[:, tc * C + g * 128:tc * C + (g + 1) * 128],
                   a_idb[:, :], start=True, stop=True,
                   sem=sT if j == 3 else None)
        k.wait("s", sT)
        for half in range(2):
            pp = (pA if half == 0 else pB)[:, 1024 + (tc % 2) * 512:
                                           1024 + (tc % 2) * 512 + 512]
            def ev(e, tc=tc, half=half, pp=pp):
                outs = a_x.ap().rearrange("p (g t) -> p g t", g=NCC)
                return e.copy(outs[:, half * 4:(half + 1) * 4,
                                   tc * 128:(tc + 1) * 128], pp)
            vo("s", ev, sem=sS)
        if tc >= 1:
            k.wait("t", sS)
    k.barrier()

    def ln_norm(codes_max):
        onescol = a_crwb[:, 300:301]
        for g in range(NCC):
            sq = scr(S_ST0, T) if g % 2 == 0 else scr(S_BC1, T)
            if g >= 2:
                k.wait("v", sT)
            vo("v", lambda e, g=g, sq=sq: e.tensor_tensor(sq, xfm(g), xfm(g),
                                                          ALU.mult), sem=sV)
            k.wait("t", sV)
            for sl in range(4):
                mm(pA[0:1, sl * 512:(sl + 1) * 512], onescol,
                   xfm(g)[:, sl * 512:(sl + 1) * 512], start=(g == 0),
                   stop=(g == NCC - 1))
                mm(pB[0:1, sl * 512:(sl + 1) * 512], onescol,
                   sq[:, sl * 512:(sl + 1) * 512], start=(g == 0),
                   stop=(g == NCC - 1),
                   sem=sT if sl == 3 else None)
        k.wait("v", sT)
        R_M = a_f32[0:1, 0:T]
        PSUMR = pA[0:1, :]
        PSQR = pB[0:1, :]
        RBM = a_scr[0:1, S_ST0:S_ST0 + T]
        RBS = a_scr[0:1, S_BC1:S_BC1 + T]
        vo("v", lambda e: e.tensor_scalar(R_M, PSUMR, 1.0 / C, None, ALU.mult))
        vo("v", lambda e: e.tensor_copy(RBM, R_M))
        vo("v", lambda e: e.tensor_scalar(PSQR, PSQR, 1.0 / C, None, ALU.mult))
        vo("v", lambda e: e.tensor_tensor(R_M, R_M, R_M, ALU.mult))
        vo("v", lambda e: e.tensor_tensor(PSQR, PSQR, R_M, ALU.subtract))
        vo("v", lambda e: e.tensor_scalar(PSQR, PSQR, EPS, None, ALU.add))
        vo("v", lambda e: e.reciprocal(R_M, PSQR), sem=sV)
        k.run("v", lambda e: e.drain())
        k.wait("s", sV)
        vo("s", lambda e: e.activation(R_M, R_M, AF.Sqrt), sem=sS)
        k.wait("v", sS)
        vo("v", lambda e: e.tensor_copy(RBS, R_M), sem=sV)
        k.wait("t", sV)
        orow = a_crwb[0:1, 0:128]
        for sl in range(4):
            mm(pA[:, sl * 512:(sl + 1) * 512], orow, RBM[:, sl * 512:(sl + 1) * 512],
               start=True, stop=True)
            mm(pB[:, sl * 512:(sl + 1) * 512], orow, RBS[:, sl * 512:(sl + 1) * 512],
               start=True, stop=True, sem=sT if sl == 3 else None)
        k.wait("v", sT)
        MB = scr(S_BC0, T)
        RSB = scr(S_BC1, T)
        vo("v", lambda e: e.tensor_copy(MB, pA[:, :]))
        vo("v", lambda e: e.tensor_copy(RSB, pB[:, :]), sem=sV)
        for g in range(NCC):
            vo("v", lambda e, g=g: e.tensor_tensor(bfm(g), xfm(g), MB, ALU.subtract))
            vo("v", lambda e, g=g: e.tensor_tensor(bfm(g), bfm(g), RSB, ALU.mult),
               sem=sV)
            if codes_max:
                vo("v", lambda e, g=g: e.tensor_reduce(
                    a_col[:, C_MX + g:C_MX + g + 1], bfm(g), AXX, ALU.max,
                    apply_absolute_value=True), sem=sV)
        if codes_max:
            k.run("v", lambda e: e.drain())

    def global_max(ncols, scale_cols=None, sfact_row=None):
        if scale_cols is not None:
            for i in range(ncols):
                vo("v", lambda e, i=i: e.tensor_tensor(
                    a_col[:, C_MX + i:C_MX + i + 1],
                    a_col[:, C_MX + i:C_MX + i + 1],
                    a_col[:, scale_cols + i:scale_cols + i + 1], ALU.mult))
        step = 1
        while step < ncols:
            for i in range(0, ncols, 2 * step):
                if i + step < ncols:
                    vo("v", lambda e, i=i, s=step: e.tensor_tensor(
                        a_col[:, C_MX + i:C_MX + i + 1],
                        a_col[:, C_MX + i:C_MX + i + 1],
                        a_col[:, C_MX + i + s:C_MX + i + s + 1], ALU.max))
            step *= 2
        vo("v", lambda e: e.drain(), sem=sV)
        k.wait("t", sV)
        k.inc("t", lambda e: e.transpose(pA[0:1, 0:128], a_col[:, C_MX:C_MX + 1],
                                         a_id[:, :]), sT)
        k.wait("v", sT)
        vo("v", lambda e: e.tensor_reduce(a_row[0:1, R_LM:R_LM + 1],
                                          pA[0:1, 0:128], AXX, ALU.max), sem=sV)
        k.run("v", lambda e: e.drain())
        if sfact_row is not None:
            vo("v", lambda e: e.tensor_tensor(
                a_row[0:1, R_LM:R_LM + 1], a_row[0:1, R_LM:R_LM + 1],
                a_row[0:1, sfact_row:sfact_row + 1], ALU.mult), sem=sV)
            k.run("v", lambda e: e.drain())
        vo("v", lambda e: e.memset(a_row[0:1, R_CC:R_CC + 8], 0.0))
        vo("v", lambda e: e.tensor_copy(a_row[0:1, R_CC:R_CC + 1],
                                        a_row[0:1, R_LM:R_LM + 1]), sem=sV)
        k.wait("y", sV)
        k.wait("y", sCC)
        dma("y", d_ccin[:, :], a_row[0:1, R_CC:R_CC + 8])
        k.wait("g", sSY)
        k.inc("g", lambda e: e.collective_compute(
            "AllReduce", ALU.max, replica_groups=[list(range(8))],
            ins=[d_ccin[:, :]], outs=[d_ccout[:, :]]), sCC)
        k.wait("g", sCC)
        dma("g", a_row[0:1, R_GM:R_GM + 1], d_ccout[0:1, 0:1])
        k.wait("v", sGP)

    def bcast_scalar(rowoff, coloff):
        k.wait("t", sV)
        mm(pB[:, 0:1], a_crw[0:1, 0:128], a_row[0:1, rowoff:rowoff + 1],
           start=True, stop=True, sem=sT)
        k.wait("v", sT)
        vo("v", lambda e, c=coloff: e.tensor_copy(a_col[:, c:c + 1], pB[:, 0:1]),
           sem=sV)

    def scalar_chain(dst_sd, dst_inv, sig_row):
        vo("v", lambda e, d=dst_sd: e.tensor_tensor(
            a_row[0:1, d:d + 1], a_row[0:1, R_GM:R_GM + 1],
            a_row[0:1, sig_row:sig_row + 1], ALU.mult), sem=sV)
        k.run("v", lambda e: e.drain())
        vo("v", lambda e, d=dst_sd, i=dst_inv: e.reciprocal(
            a_row[0:1, i:i + 1], a_row[0:1, d:d + 1]))
        k.run("v", lambda e: e.drain())
        vo("v", lambda e, i=dst_inv: e.tensor_scalar(
            a_row[0:1, i:i + 1], a_row[0:1, i:i + 1], 127.0, None, ALU.mult))
        vo("v", lambda e, d=dst_sd: e.tensor_scalar(
            a_row[0:1, d:d + 1], a_row[0:1, d:d + 1], 1.0 / 127.0, None,
            ALU.mult), sem=sV)
        k.run("v", lambda e: e.drain())

    def quant_chunks(nch, src_fn, scale_col, out_fn):
        for i in range(nch):
            sc = scale_col(i) if callable(scale_col) else scale_col
            u = a_f32[:, FU:FU + T]
            k.wait("s", sV)
            vo("s", lambda e, i=i, sc=sc, u=u: e.activation(
                u, src_fn(i), AF.Identity, bias=a_col[:, C_C:C_C + 1], scale=sc),
               sem=sS)
            k.wait("v", sS)
            vo("v", lambda e, u=u: e.tensor_scalar(u, u, -MAGIC, 127.0,
                                                   ALU.add, ALU.min))
            vo("v", lambda e, i=i, u=u: e.tensor_scalar(out_fn(i), u, -128.0,
                                                        None, ALU.max), sem=sV)

    def wquant_rows(wf, scol_ap, wint, width):
        rmax = a_col[:, C_T1:C_T1 + 1]
        rinv = a_col[:, C_T2:C_T2 + 1]
        vo("v", lambda e: e.tensor_reduce(rmax, wf, AXX, ALU.max,
                                          apply_absolute_value=True), sem=sV)
        k.run("v", lambda e: e.drain())
        vo("v", lambda e: e.tensor_scalar(scol_ap, rmax, 1.0 / 127.0, None,
                                          ALU.mult))
        vo("v", lambda e: e.reciprocal(rinv, rmax))
        k.run("v", lambda e: e.drain())
        vo("v", lambda e: e.tensor_scalar(rinv, rinv, 127.0, None, ALU.mult))
        k.run("v", lambda e: e.drain())
        for part in range((width + 1023) // 1024):
            lo = part * 1024
            hi = min(width, lo + 1024)
            u = a_f32[:, FU:FU + (hi - lo)]
            vo("v", lambda e, lo=lo, hi=hi, u=u, wf=wf: e.tensor_scalar(
                u, wf[:, lo:hi], rinv, MAGIC, ALU.mult, ALU.add))
            vo("v", lambda e, lo=lo, hi=hi, u=u, wint=wint: e.tensor_scalar(
                wint[:, lo:hi], u, -MAGIC, None, ALU.add), sem=sV)

    def transpose_tiles(wint, ntile, dststage, pbank):
        k.wait("t", sV)
        k.wait("t", sS)
        for i in range(ntile):
            pp = pbank[:, (i % 4) * 128:(i % 4) * 128 + 128]
            mm(pp, wint[:, i * 128:(i + 1) * 128], a_idb[:, :], start=True,
               stop=True, sem=sT if (i % 4 == 3 or i == ntile - 1) else None)
            if i % 4 == 3 or i == ntile - 1:
                k.wait("s", sT)
                base = (i // 4) * 4
                ngrp = i - base + 1
                vo("s", lambda e, base=base, ngrp=ngrp, d=dststage, p=pbank: e.copy(
                    d[:, base * 128:(base + ngrp) * 128], p[:, 0:ngrp * 128]),
                   sem=sS)
                if i != ntile - 1:
                    k.wait("t", sS)

    def emit_ffn(save, dbg_tag=None):
        ln_norm(codes_max=True)
        global_max(NCC)
        scalar_chain(R_SX, R_INV, R_SGX)
        bcast_scalar(R_INV, C_INV)
        bcast_scalar(R_SX, C_SX)
        quant_chunks(NCC, bfm, a_col[:, C_INV:C_INV + 1], bfm)
        k.barrier()
        if debug and dbg_tag == 1:
            k.wait("y", sV)
            dma("y", dbg["sc"][0:1, 0:16], a_row[0:1, 0:16])
            dma("y", dbg["sc"][0:1, 16:32], a_row[0:1, 16:32])
        for i in range(NIC):
            gst = scr(S_ST0, 1024)
            ust = scr(S_ST1, 1024)
            k.wait("y", sT)
            k.wait("y", sV)
            k.wait("s", sSY)
            if save:
                wfg = a_f32[:, FW:FW + 1024]
                wfu = a_f32[:, FW + 1024:FW + 2048]
                dma("y", wfg, d_wg[i * 128:(i + 1) * 128, :])
                dma("y", wfu, d_wu[i * 128:(i + 1) * 128, :])
                k.wait("v", sSY)
                wint = scr(S_ST2, 1024)
                wquant_rows(wfg, a_col[:, C_SG + i:C_SG + i + 1], wint, 1024)
                transpose_tiles(wint, 8, gst, pA[:, 0:512])
                wquant_rows(wfu, a_col[:, C_SU + i:C_SU + i + 1], wint, 1024)
                transpose_tiles(wint, 8, ust, pA[:, 512:1024])
                k.wait("y", sS)
                dma("y", d_wgt[i, :, :], gst)
                dma("y", d_wut[i, :, :], ust)
            else:
                dma("y", gst, d_wgt[i, :, :])
                dma("y", ust, d_wut[i, :, :])
                k.wait("t", sSY)
            vo("v", lambda e, i=i: e.tensor_tensor(
                a_col[:, C_KG:C_KG + 1], a_col[:, C_SG + i:C_SG + i + 1],
                a_col[:, C_SX:C_SX + 1], ALU.mult), sem=sV)
            k.run("v", lambda e: e.drain())
            k.wait("t", sV)
            k.wait("t", sS)
            for th in range(2):
                pg = (pA if th == 0 else pB)[:, 0:1024]
                pu = (pA if th == 0 else pB)[:, 1024:2048]
                for h in range(NCC):
                    for sl in range(2):
                        nsl = th * 1024 + sl * 512
                        mm(pg[:, sl * 512:(sl + 1) * 512],
                           gst[:, h * 128:(h + 1) * 128],
                           bfm(h)[:, nsl:nsl + 512], start=(h == 0),
                           stop=(h == NCC - 1))
                        mm(pu[:, sl * 512:(sl + 1) * 512],
                           ust[:, h * 128:(h + 1) * 128],
                           bfm(h)[:, nsl:nsl + 512], start=(h == 0),
                           stop=(h == NCC - 1),
                           sem=sT if (h == NCC - 1 and sl == 1) else None)
                k.wait("s", sT)
                k.wait("s", sV)
                sil = scr(S_ST2, 1024)
                vo("s", lambda e, pg=pg, sil=sil: e.activation(
                    sil, pg, AF.Silu, scale=a_col[:, C_KG:C_KG + 1]), sem=sS)
                k.wait("v", sS)
                vo("v", lambda e, i=i, th=th, pu=pu, sil=sil: e.scalar_tensor_tensor(
                    tcode(i)[:, th * 1024:(th + 1) * 1024], pu, 1.0, sil,
                    ALU.mult, ALU.mult), sem=sV)
            vo("v", lambda e, i=i: e.tensor_reduce(
                a_col[:, C_MX + i:C_MX + i + 1], tcode(i), AXX, ALU.max,
                apply_absolute_value=True), sem=sV)
            k.run("v", lambda e: e.drain())
        k.barrier()
        global_max(NIC, scale_cols=C_SU, sfact_row=R_SX)
        scalar_chain(R_SD, R_TMP, R_SGD)
        vo("v", lambda e: e.tensor_tensor(a_row[0:1, R_F2:R_F2 + 1],
                                          a_row[0:1, R_SX:R_SX + 1],
                                          a_row[0:1, R_TMP:R_TMP + 1], ALU.mult))
        vo("v", lambda e: e.tensor_scalar(a_row[0:1, R_F3:R_F3 + 1],
                                          a_row[0:1, R_SD:R_SD + 1], 0.5, None,
                                          ALU.mult), sem=sV)
        k.run("v", lambda e: e.drain())
        bcast_scalar(R_F2, C_INV)
        bcast_scalar(R_F3, C_F3)
        k.barrier()
        if debug and dbg_tag == 1:
            k.wait("y", sV)
            dma("y", dbg["sc"][0:1, 32:64], a_row[0:1, 0:32])
        def rc_col(i):
            vo("v", lambda e, i=i: e.tensor_tensor(
                a_col[:, C_RC:C_RC + 1], a_col[:, C_SU + i:C_SU + i + 1],
                a_col[:, C_INV:C_INV + 1], ALU.mult), sem=sV)
            k.run("v", lambda e: e.drain())
            return a_col[:, C_RC:C_RC + 1]
        quant_chunks(NIC, tcode, rc_col, tcode)
        k.barrier()
        for h in range(NCC):
            dstg = scr(S_ST0, II)
            pdn = pA if h % 2 == 0 else pB
            k.wait("y", sT)
            k.wait("y", sV)
            k.wait("s", sSY)
            if save:
                wfh = a_f32[:, 0:II]
                dma("y", wfh, d_wd[h * 128:(h + 1) * 128, :])
                k.wait("v", sSY)
                wint = scr(S_PT, II)
                rmax = a_col[:, C_T1:C_T1 + 1]
                rinv = a_col[:, C_T2:C_T2 + 1]
                vo("v", lambda e, wfh=wfh: e.tensor_reduce(
                    rmax, wfh, AXX, ALU.max, apply_absolute_value=True), sem=sV)
                k.run("v", lambda e: e.drain())
                vo("v", lambda e, h=h: e.tensor_scalar(
                    a_col[:, C_SD + h:C_SD + h + 1], rmax, 1.0 / 127.0, None,
                    ALU.mult))
                vo("v", lambda e: e.reciprocal(rinv, rmax))
                k.run("v", lambda e: e.drain())
                vo("v", lambda e: e.tensor_scalar(rinv, rinv, 127.0, None,
                                                  ALU.mult))
                k.run("v", lambda e: e.drain())
                u2 = a_f32[:, II:II + 1024]
                for part in range(3):
                    lo = part * 1024
                    hi = min(II, lo + 1024)
                    vo("v", lambda e, lo=lo, hi=hi, u2=u2, wfh=wfh: e.tensor_scalar(
                        u2[:, 0:hi - lo], wfh[:, lo:hi], rinv, MAGIC, ALU.mult,
                        ALU.add))
                    vo("v", lambda e, lo=lo, hi=hi, u2=u2, wint=wint: e.tensor_scalar(
                        wint[:, lo:hi], u2[:, 0:hi - lo], -MAGIC, None, ALU.add),
                        sem=sV)
                transpose_tiles(wint, NIC, dstg, pdn[:, 1024:1536])
                k.wait("y", sS)
                dma("y", d_wdt[h, :, :], dstg)
            else:
                dma("y", dstg, d_wdt[h, :, :])
                k.wait("t", sSY)
            vo("v", lambda e, h=h: e.tensor_tensor(
                a_col[:, C_FIN:C_FIN + 1], a_col[:, C_SD + h:C_SD + h + 1],
                a_col[:, C_F3:C_F3 + 1], ALU.mult), sem=sV)
            k.run("v", lambda e: e.drain())
            k.wait("t", sV)
            k.wait("t", sS)
            for i in range(NIC):
                for sl in range(4):
                    mm(pdn[:, sl * 512:(sl + 1) * 512],
                       dstg[:, i * 128:(i + 1) * 128],
                       tcode(i)[:, sl * 512:(sl + 1) * 512], start=(i == 0),
                       stop=(i == NIC - 1),
                       sem=sT if (i == NIC - 1 and sl == 3) else None)
            k.wait("v", sT)
            vo("v", lambda e, h=h, pdn=pdn: e.scalar_tensor_tensor(
                xfm(h), pdn[:, :], a_col[:, C_FIN:C_FIN + 1], xfm(h),
                ALU.mult, ALU.add), sem=sV)
        k.barrier()

    def emit_attn():
        ln_norm(codes_max=False)
        k.barrier()
        for f in range(16):
            wt = scr(S_ST0 + (f % 2) * 1024, 1024)
            if f >= 2:
                k.wait("g", sT)
            dma("g", wt.rearrange("p (c j) -> p c j", c=8),
                d_wqkv.ap()[:, f * 128:(f + 1) * 128]
                .rearrange("(c p) j -> p c j", p=128))
            k.wait("t", sGP)
            pq = pA if f % 2 == 0 else pB
            for c in range(NCC):
                for sl in range(4):
                    mm(pq[:, sl * 512:(sl + 1) * 512], wt[:, c * 128:(c + 1) * 128],
                       bfm(c)[:, sl * 512:(sl + 1) * 512], start=(c == 0),
                       stop=(c == NCC - 1),
                       sem=sT if (c == NCC - 1 and sl == 3) else None)
            k.wait("s", sT)
            dst = qfm(f) if f < 8 else kfm(f - 8)
            vo("s", lambda e, dst=dst, pq=pq, f=f: e.activation(
                dst, pq[:, :], AF.Identity, bias=a_bq[:, f:f + 1], scale=1.0),
               sem=sS)
            k.wait("t", sS)
        k.barrier()
        vo("v", lambda e: e.memset(a_big[:, 16 * T:16 * T + NTC * NHEAD * 65],
                                   1.0), sem=sV)
        WV = a_scr[:, S_BC1:S_BC1 + 8192]
        dma("g", WV.rearrange("p (c j) -> p c j", c=8),
            d_wqkv.ap()[:, 2048:3072].rearrange("(c p) j -> p c j", p=128))
        k.wait("t", sGP)
        k.wait("t", sV)
        for tc in range(NTC):
            pv = (pA if tc % 2 == 0 else pB)[:, 0:1024]
            for c in range(NCC):
                for sl in range(2):
                    mm(pv[:, sl * 512:(sl + 1) * 512],
                       bfm(c)[:, tc * 128:(tc + 1) * 128],
                       WV[:, c * 1024 + sl * 512:c * 1024 + (sl + 1) * 512],
                       start=(c == 0), stop=(c == NCC - 1),
                       sem=sT if (c == NCC - 1 and sl == 1) else None)
            k.wait("v", sT)
            def vev(e, tc=tc, pv=pv):
                outp = vaug(tc).rearrange("p (h j) -> p h j", j=65)[:, :, 0:64]
                return e.scalar_tensor_tensor(
                    outp, pv, 1.0, BVB.rearrange("p (h j) -> p h j", j=64),
                    ALU.mult, ALU.add)
            vo("v", vev, sem=sV)
            k.wait("t", sV)
        k.barrier()
        A_COS = scr(S_PT, T)
        A_SIN = scr(S_PT + T, T)
        dma("g", A_COS, d_cos[:, :])
        dma("g", A_SIN, d_sin[:, :])
        k.wait("v", sGP)
        for c in range(16):
            src = qfm(c) if c < 8 else kfm(c - 8)
            prm = scr(S_ST0, T)
            t1 = scr(S_ST2, T)
            k.wait("g", sV)
            vo("g", lambda e, src=src, prm=prm: e.tensor_copy(prm[0:32, :],
                                                              src[32:64, :]))
            vo("g", lambda e, src=src, prm=prm: e.tensor_copy(prm[32:64, :],
                                                              src[0:32, :]))
            vo("g", lambda e, src=src, prm=prm: e.tensor_copy(prm[64:96, :],
                                                              src[96:128, :]))
            vo("g", lambda e, src=src, prm=prm: e.tensor_copy(prm[96:128, :],
                                                              src[64:96, :]),
               sem=sG)
            k.wait("v", sG)
            vo("v", lambda e, src=src, t1=t1: e.tensor_tensor(
                t1, src, A_COS, ALU.mult))
            vo("v", lambda e, prm=prm: e.tensor_tensor(prm, prm, A_SIN,
                                                       ALU.mult))
            vo("v", lambda e, src=src, t1=t1, prm=prm: e.tensor_tensor(
                src, t1, prm, ALU.add), sem=sV)
            vo("v", lambda e, c=c: e.tensor_reduce(
                a_col[:, C_MQ + c:C_MQ + c + 1], src, AXX, ALU.max,
                apply_absolute_value=True), sem=sV)
            k.run("v", lambda e: e.drain())
        k.barrier()
        k.inc("t", lambda e: e.transpose(pA[0:16, 0:128],
                                         a_col[:, C_MQ:C_MQ + 16], a_id[:, :]), sT)
        k.wait("v", sT)
        r1c = a_f32[0:16, 3072:3073]
        r2c = a_f32[0:16, 3073:3074]
        vo("v", lambda e: e.tensor_reduce(r1c, pA[0:16, 0:64], AXX, ALU.max))
        vo("v", lambda e: e.tensor_reduce(r2c, pA[0:16, 64:128], AXX, ALU.max),
           sem=sV)
        k.run("v", lambda e: e.drain())
        k.wait("t", sV)
        k.inc("t", lambda e: e.transpose(pB[0:1, 0:16], r1c, a_id[0:16, 0:16]), sT)
        k.inc("t", lambda e: e.transpose(pB[0:1, 16:32], r2c, a_id[0:16, 0:16]), sT)
        k.wait("v", sT)
        r1row = a_f32[0:1, 3074:3090]
        r2row = a_f32[0:1, 3090:3106]
        vo("v", lambda e: e.tensor_copy(r1row, pB[0:1, 0:16]))
        vo("v", lambda e: e.tensor_copy(r2row, pB[0:1, 16:32]), sem=sV)
        BE = a_row[0:1, R_BE:R_BE + 16]
        bev = BE.rearrange("p (c t) -> p c t", t=2)

        def r3(ap):
            return ap.rearrange("p (c o) -> p c o", o=1)
        vo("v", lambda e: e.tensor_tensor(bev[:, :, 0:1], r3(r1row[:, 0:8]),
                                          r3(r1row[:, 8:16]), ALU.mult))
        vo("v", lambda e: e.tensor_tensor(bev[:, :, 1:2], r3(r2row[:, 0:8]),
                                          r3(r2row[:, 8:16]), ALU.mult), sem=sV)
        k.run("v", lambda e: e.drain())
        vo("v", lambda e: e.tensor_scalar(BE, BE, -8.0, 40.0, ALU.mult, ALU.add),
           sem=sV)
        k.run("v", lambda e: e.drain())
        k.wait("t", sV)
        for h in range(NHEAD):
            mm(pB[:, h:h + 1], a_crw[0:1, 0:128],
               a_row[0:1, R_BE + h:R_BE + h + 1], start=True, stop=True,
               sem=sT if h == NHEAD - 1 else None)
        k.wait("v", sT)
        vo("v", lambda e: e.tensor_copy(a_col[:, C_BIAS:C_BIAS + 16],
                                        pB[:, 0:16]), sem=sV)
        k.barrier()
        nphase = 0
        for hp in range(8):
            for qq in range(4):
                par = nphase % 2
                avA = pB[0:65, par * 1024:par * 1024 + 512]
                avB = pB[0:65, par * 1024 + 512:par * 1024 + 1024]
                hA, hB = 2 * hp, 2 * hp + 1
                expmk = []
                prev_vmk = k.mark(sV)
                for kt in range(NTC):
                    pp = kt % 2
                    SA = pA[:, pp * 512:(pp + 1) * 512]
                    SB = pA[:, 1024 + pp * 512:1024 + (pp + 1) * 512]
                    if kt >= 2:
                        k.wait("t", sS, expmk[kt - 2])
                    if kt == 0:
                        k.wait("t", sV, prev_vmk)
                    mm(SA, kfm(hp)[0:64, kt * 128:(kt + 1) * 128],
                       qfm(hp)[0:64, qq * 512:(qq + 1) * 512], start=True,
                       stop=True)
                    mm(SB, kfm(hp)[64:128, kt * 128:(kt + 1) * 128],
                       qfm(hp)[64:128, qq * 512:(qq + 1) * 512], start=True,
                       stop=True, sem=sT)
                    k.wait("s", sT)
                    ptA = scr(S_PT + (kt % 4) * 1024, 512)
                    ptB = scr(S_PT + (kt % 4) * 1024 + 512, 512)
                    vo("s", lambda e, SA=SA, ptA=ptA, hA=hA: e.activation(
                        ptA, SA, AF.Exp,
                        bias=a_col[:, C_BIAS + hA:C_BIAS + hA + 1], scale=0.125))
                    em = vo("s", lambda e, SB=SB, ptB=ptB, hB=hB: e.activation(
                        ptB, SB, AF.Exp,
                        bias=a_col[:, C_BIAS + hB:C_BIAS + hB + 1], scale=0.125),
                        sem=sS)
                    expmk.append(em)
                    k.wait("t", sS, em)
                    mm(avA, vaug(kt)[:, hA * 65:(hA + 1) * 65], ptA,
                       start=(kt == 0), stop=(kt == NTC - 1))
                    mm(avB, vaug(kt)[:, hB * 65:(hB + 1) * 65], ptB,
                       start=(kt == 0), stop=(kt == NTC - 1),
                       sem=sT if kt == NTC - 1 else None)
                avmk = k.mark(sT)
                k.wait("v", sT, avmk)
                ZA = a_rowb[64:65, RB_Z:RB_Z + 512]
                ZB = a_rowb[64:65, RB_Z + 512:RB_Z + 1024]
                ZAf = a_f32[64:65, 0:512]
                ZBf = a_f32[64:65, 512:1024]
                vo("v", lambda e, avA=avA, ZAf=ZAf: e.reciprocal(ZAf, avA[64:65, :]))
                vo("v", lambda e, avB=avB, ZBf=ZBf: e.reciprocal(ZBf, avB[64:65, :]))
                k.run("v", lambda e: e.drain())
                vo("v", lambda e, ZA=ZA, ZAf=ZAf: e.tensor_copy(ZA, ZAf))
                vo("v", lambda e, ZB=ZB, ZBf=ZBf: e.tensor_copy(ZB, ZBf), sem=sV)
                oa = scr(S_ST0, 512)
                ob = scr(S_ST0 + 512, 512)
                if nphase >= 1:
                    k.wait("v", sSC)
                vo("v", lambda e, avA=avA, oa=oa: e.tensor_copy(
                    oa[0:64, :], avA[0:64, :]))
                vo("v", lambda e, avB=avB, ob=ob: e.tensor_copy(
                    ob[0:64, :], avB[0:64, :]), sem=sV)
                freed = pB[:, (1 - par) * 1024:(1 - par) * 1024 + 512]
                k.wait("t", sV)
                mm(freed, a_crwb[64:65, 128:256], ZA, start=True, stop=True,
                   sem=sT)
                k.wait("v", sT)
                vo("v", lambda e, hp=hp, qq=qq, oa=oa, freed=freed: e.tensor_tensor(
                    bfm(hp)[0:64, qq * 512:(qq + 1) * 512], oa[0:64, :],
                    freed[0:64, :], ALU.mult), sem=sV)
                k.wait("t", sV)
                mm(freed, a_crwb[64:65, 128:256], ZB, start=True, stop=True,
                   sem=sT)
                k.wait("v", sT)
                vo("v", lambda e, ob=ob, freed=freed: e.tensor_tensor(
                    ob[0:64, :], ob[0:64, :], freed[0:64, :], ALU.mult), sem=sV)
                k.wait("s", sV)
                dma("s", bfm(hp)[64:128, qq * 512:(qq + 1) * 512], ob[0:64, :])
                nphase += 1
        k.barrier()
        for f in range(NCC):
            wt = scr(S_ST0 + (f % 2) * 1024, 1024)
            if f >= 2:
                k.wait("g", sT)
            dma("g", wt.rearrange("p (c j) -> p c j", c=8),
                d_wout.ap()[:, f * 128:(f + 1) * 128]
                .rearrange("(c p) j -> p c j", p=128))
            k.wait("t", sGP)
            po = pA if f % 2 == 0 else pB
            for c in range(NCC):
                for sl in range(4):
                    mm(po[:, sl * 512:(sl + 1) * 512], wt[:, c * 128:(c + 1) * 128],
                       bfm(c)[:, sl * 512:(sl + 1) * 512], start=(c == 0),
                       stop=(c == NCC - 1),
                       sem=sT if (c == NCC - 1 and sl == 3) else None)
            k.wait("v", sT)
            vo("v", lambda e, f=f, po=po: e.scalar_tensor_tensor(
                xfm(f), po[:, :], a_bo[:, f:f + 1], xfm(f), ALU.add, ALU.add),
               sem=sV)
            k.wait("t", sV)
        k.barrier()

    def emit_store_x(target):
        for tc in range(NTC):
            for half in range(2):
                pp = (pA if half == 0 else pB)[:, (tc % 2) * 512:
                                               (tc % 2) * 512 + 512]
                for j in range(4):
                    g = half * 4 + j
                    mm(pp[:, j * 128:(j + 1) * 128],
                       xfm(g)[:, tc * 128:(tc + 1) * 128], a_idb[:, :],
                       start=True, stop=True, sem=sT if j == 3 else None)
            k.wait("v", sT)
            stg = a_f32[:, FU:FU + 1024]
            vo("v", lambda e, tc=tc, stg=stg: e.tensor_copy(
                stg[:, 0:512], pA[:, (tc % 2) * 512:(tc % 2) * 512 + 512]))
            vo("v", lambda e, tc=tc, stg=stg: e.tensor_copy(
                stg[:, 512:1024], pB[:, (tc % 2) * 512:(tc % 2) * 512 + 512]),
               sem=sV)
            k.wait("y", sV)
            dma("y", target[tc * 128:(tc + 1) * 128, :], stg)
            k.wait("v", sSY)
            k.wait("t", sV)
        k.barrier()

    emit_ffn(save=True, dbg_tag=1)
    if debug:
        emit_store_x(dbg["x1"])
    emit_attn()
    if debug:
        emit_store_x(dbg["x2"])
    emit_ffn(save=False)
    emit_store_x(d_out)
    k.barrier()

    with nc.Block() as block:
        @block.tensor
        def _(e):
            for fn in k.q["t"]:
                fn(e)

        @block.vector
        def _(e):
            for fn in k.q["v"]:
                fn(e)

        @block.scalar
        def _(e):
            for fn in k.q["s"]:
                fn(e)

        @block.gpsimd
        def _(e):
            for fn in k.q["g"]:
                fn(e)

        @block.sync
        def _(e):
            for fn in k.q["y"]:
                fn(e)

    for g in reversed(guards):
        g.__exit__(None, None, None)
    for g in reversed(k.guards):
        g.__exit__(None, None, None)
    return nc


def _host_consts():
    ident = np.eye(128, dtype=np.float32)
    DH = 64
    inv = (1.0 / 10000.0 ** (np.arange(0, DH, 2, dtype=np.float64) / DH))
    tpos = np.arange(T, dtype=np.float64)[:, None]
    fr = tpos * inv[None, :]
    emb = np.concatenate([fr, fr], axis=1)
    cos = np.cos(emb).astype(np.float32).T
    sin = np.sin(emb).astype(np.float32).T
    cos_fm = np.concatenate([cos, cos], axis=0)
    sin_s = np.concatenate([-sin[0:32], sin[32:64]], axis=0)
    sin_fm = np.concatenate([sin_s, sin_s], axis=0)
    crows = np.zeros((128, 384), dtype=np.float32)
    crows[0, 0:128] = 1.0
    crows[64, 0:128] = 1.0
    crows[64, 128:192] = 1.0
    crows[64, 320:384] = 1.0
    crows[:, 300] = 1.0
    return ident, cos_fm, sin_fm, crows


_NC_CACHE = {}


def kernel(L0_attn_x, L0_attn_W_qkv, L0_attn_b_qkv, L0_attn_W_out, L0_attn_b_out,
           L0_ffn_clip_x, L0_ffn_clip_down, L0_ffn_w_gate, L0_ffn_w_up,
           L0_ffn_w_down, _debug=False):
    key = bool(_debug)
    if key not in _NC_CACHE:
        _NC_CACHE[key] = build_nc(debug=_debug)
    nc = _NC_CACHE[key]
    ident, cos_fm, sin_fm, crows = _host_consts()
    x = np.ascontiguousarray(np.asarray(L0_attn_x, dtype=np.float32))
    base = {
        "wqkv": np.ascontiguousarray(np.asarray(L0_attn_W_qkv, np.float32)),
        "bqkv": np.ascontiguousarray(np.asarray(L0_attn_b_qkv, np.float32)),
        "wout": np.ascontiguousarray(np.asarray(L0_attn_W_out, np.float32)),
        "bout": np.ascontiguousarray(np.asarray(L0_attn_b_out, np.float32)),
        "clipx": np.ascontiguousarray(np.asarray(L0_ffn_clip_x, np.float32)),
        "clipd": np.ascontiguousarray(np.asarray(L0_ffn_clip_down, np.float32)),
        "wg": np.ascontiguousarray(np.asarray(L0_ffn_w_gate, np.float32)),
        "wu": np.ascontiguousarray(np.asarray(L0_ffn_w_up, np.float32)),
        "wd": np.ascontiguousarray(np.asarray(L0_ffn_w_down, np.float32)),
        "cident": ident, "ccos": cos_fm, "csin": sin_fm, "crows": crows,
    }
    in_maps = []
    for b in range(8):
        m = dict(base)
        m["x"] = np.ascontiguousarray(x[b])
        in_maps.append(m)
    res = run_bass_kernel_spmd(nc, in_maps, core_ids=list(range(8)))
    out = np.stack([res.results[b]["out"] for b in range(8)], axis=0)
    if _debug:
        extra = {kk: np.stack([res.results[b]["dbg_" + kk] for b in range(8)])
                 for kk in ("x1", "x2", "sc")}
        return out, extra
    return out


# version 04
# speedup vs baseline: 1.2824x; 1.2824x over previous
import numpy as np
import ml_dtypes
import concourse.bass as bass
import concourse.mybir as mybir
from concourse.bass_utils import run_bass_kernel_spmd

F32 = mybir.dt.float32
BF16 = mybir.dt.bfloat16
AF = mybir.ActivationFunctionType
ALU = mybir.AluOpType
AXX = mybir.AxisListType.X

MAGIC = float(np.float32(1.5 * 2 ** 23))
T = 2048
C = 1024
NCC = 8
NTC = 16
II = 2816
NIC = 22
NHEAD = 16
EPS = 1e-5

ENGS = ("t", "v", "s", "g", "y")


class B:

    def __init__(self, nc):
        self.nc = nc
        self.q = {e: [] for e in ENGS}
        self.sems = {}
        self.cnt = {}
        self.guards = []

    def sem(self, name):
        g = self.nc.semaphore(name)
        h = g.__enter__()
        self.guards.append(g)
        self.sems[name] = h
        self.cnt[name] = 0
        return name

    def emit(self, e, fn):
        self.q[e].append(fn)

    def inc(self, e, mk, sem, k=1):
        h = self.sems[sem]
        self.emit(e, lambda eng, mk=mk, h=h, k=k: mk(eng).then_inc(h, k))
        self.cnt[sem] += k
        return self.cnt[sem]

    def run(self, e, mk):
        self.emit(e, mk)

    def wait(self, e, sem, val=None):
        h = self.sems[sem]
        v = self.cnt[sem] if val is None else val
        if v:
            self.emit(e, lambda eng, h=h, v=v: eng.wait_ge(h, v))

    def mark(self, sem):
        return self.cnt[sem]

    def barrier(self):
        snap = {s: self.cnt[s] for s in self.sems}
        for e in ENGS:
            for s, v in snap.items():
                if v:
                    self.wait(e, s, v)


def build_nc(debug=False):
    nc = bass.Bass("TRN2", target_bir_lowering=False)

    d_x = nc.dram_tensor("x", [T, C], F32, kind="ExternalInput")
    d_wqkv = nc.dram_tensor("wqkv", [C, 3 * C], F32, kind="ExternalInput")
    d_bqkv = nc.dram_tensor("bqkv", [3 * C], F32, kind="ExternalInput")
    d_wout = nc.dram_tensor("wout", [C, C], F32, kind="ExternalInput")
    d_bout = nc.dram_tensor("bout", [C], F32, kind="ExternalInput")
    d_clipx = nc.dram_tensor("clipx", [1], F32, kind="ExternalInput")
    d_clipd = nc.dram_tensor("clipd", [1], F32, kind="ExternalInput")
    d_wg = nc.dram_tensor("wg", [II, C], F32, kind="ExternalInput")
    d_wu = nc.dram_tensor("wu", [II, C], F32, kind="ExternalInput")
    d_wd = nc.dram_tensor("wd", [C, II], F32, kind="ExternalInput")
    d_ident = nc.dram_tensor("cident", [128, 128], F32, kind="ExternalInput")
    d_cos = nc.dram_tensor("ccos", [128, T], F32, kind="ExternalInput")
    d_sin = nc.dram_tensor("csin", [128, T], F32, kind="ExternalInput")
    d_crows = nc.dram_tensor("crows", [128, 384], F32, kind="ExternalInput")
    d_crot = nc.dram_tensor("crot", [128, 128], F32, kind="ExternalInput")
    d_out = nc.dram_tensor("out", [T, C], F32, kind="ExternalOutput")

    d_wgt = nc.dram_tensor("wgt_s", [NIC, 128, C], BF16)
    d_wut = nc.dram_tensor("wut_s", [NIC, 128, C], BF16)
    d_wdt = nc.dram_tensor("wdt_s", [NCC, 128, II], BF16)
    d_ccin = nc.dram_tensor("ccin", [1, 8], F32)
    d_ccout = nc.dram_tensor("ccout", [1, 8], F32)

    dbg = {}
    if debug:
        dbg["x1"] = nc.dram_tensor("dbg_x1", [T, C], F32, kind="ExternalOutput")
        dbg["x2"] = nc.dram_tensor("dbg_x2", [T, C], F32, kind="ExternalOutput")
        dbg["sc"] = nc.dram_tensor("dbg_sc", [1, 64], F32, kind="ExternalOutput")

    guards = []

    def sb(name, shape, dt):
        g = nc.sbuf_tensor(name, shape, dt)
        t = g.__enter__()
        guards.append(g)
        return t

    def psu(name, shape):
        g = nc.psum_tensor(name, shape, F32)
        t = g.__enter__()
        guards.append(g)
        return t

    a_x = sb("a_x", [128, NCC * T], BF16)
    a_b = sb("a_b", [128, NCC * T], BF16)
    a_big = sb("a_big", [128, 49408], BF16)
    a_scr = sb("a_scr", [128, 11264], BF16)
    a_f32 = sb("a_f32", [128, 3840], F32)
    a_col = sb("a_col", [128, 256], F32)
    a_row = sb("a_row", [128, 256], F32)
    a_rowb = sb("a_rowb", [128, 1024], BF16)
    a_id = sb("a_id", [128, 128], F32)
    a_idb = sb("a_idb", [128, 128], BF16)
    a_crw = sb("a_crw", [128, 384], F32)
    a_crwb = sb("a_crwb", [128, 384], BF16)
    a_rot = sb("a_rot", [128, 128], BF16)
    a_bq = sb("a_bq", [128, 24], F32)
    a_bo = sb("a_bo", [128, 8], F32)
    pA = psu("pA", [128, 2048])
    pB = psu("pB", [128, 2048])

    def xfm(g):
        return a_x[:, g * T:(g + 1) * T]

    def bfm(g):
        return a_b[:, g * T:(g + 1) * T]

    def tcode(i):
        return a_big[:, i * T:(i + 1) * T]

    def qfm(c):
        return a_big[:, c * T:(c + 1) * T]

    def kfm(c):
        return a_big[:, (8 + c) * T:(9 + c) * T]

    def vaug(tc):
        o = 16 * T + tc * (NHEAD * 65)
        return a_big[:, o:o + NHEAD * 65]

    S_BC0 = 0
    S_BC1 = 2048
    S_ST0 = 4096
    S_ST1 = 5120
    S_ST2 = 6144
    S_PT = 7168

    def scr(o, n):
        return a_scr[:, o:o + n]

    FU = 0
    FW = 1792

    C_SG, C_SU, C_SD = 0, 32, 64
    C_MX = 80
    C_KG = 112
    C_RC = 114
    C_FIN = 116
    C_T1, C_T2 = 120, 121
    C_C = 124
    C_INV = 128
    C_F3 = 136
    C_SX = 144
    C_BIAS = 152
    C_MQ = 176
    C_KGA = 192
    C_RCA = 216

    R_SGX = 0
    R_SGD = 4
    R_LM = 8
    R_GM = 12
    R_SX = 16
    R_INV = 20
    R_SD = 24
    R_F2 = 28
    R_F3 = 32
    R_MQ = 40
    R_MK = 64
    R_BE = 88
    R_CC = 128
    R_TMP = 160
    R_EPS = 168

    RB_Z = 0

    k = B(nc)
    sSY = k.sem("sSY")
    sGP = k.sem("sGP")
    sSC = k.sem("sSC")
    sT = k.sem("sT")
    sV = k.sem("sV")
    sS = k.sem("sS")
    sG = k.sem("sG")
    sCC = k.sem("sCC")

    def dma(queue, out, in_, slow=False):
        sem = {"y": sSY, "g": sGP, "s": sSC}[queue]
        return k.inc(queue, lambda e, o=out, i=in_, sl=slow: e.dma_start(
            out=o, in_=i, allow_slow_non_contiguous=sl), sem, 16)

    def mm(out, lhsT, rhs, start, stop, sem=None):
        def go(e, o=out, l=lhsT, r=rhs, st=start, sp=stop):
            return e.matmul(o, l, r, start=st, stop=sp)
        if sem:
            return k.inc("t", go, sem)
        k.run("t", go)

    def vo(e, mk, sem=None):
        if sem:
            return k.inc(e, mk, sem)
        k.run(e, mk)

    dma("y", a_id[:, :], d_ident[:, :])
    dma("y", a_crw[:, :], d_crows[:, :])
    dma("g", a_b.ap().rearrange("p (t c) -> p t c", t=NTC),
        d_x.ap().rearrange("(t p) c -> p t c", p=128))
    dma("g", a_rot[:, :], d_crot[:, :])
    dma("y", a_bq[:, :], d_bqkv.ap().rearrange("(f p) -> p f", p=128), slow=True)
    dma("y", a_bo[:, :], d_bout.ap().rearrange("(f p) -> p f", p=128), slow=True)
    dma("y", a_f32[0:1, 0:1024], d_bqkv.ap()[2048:3072].rearrange("(a f) -> a f", a=1))
    dma("y", a_row[0:1, R_SGX:R_SGX + 1], d_clipx.ap().rearrange("(b a) -> b a", b=1))
    dma("y", a_row[0:1, R_SGD:R_SGD + 1], d_clipd.ap().rearrange("(b a) -> b a", b=1))

    k.wait("v", sSY)
    k.wait("v", sGP)
    vo("v", lambda e: e.tensor_copy(a_idb[:, :], a_id[:, :]))
    vo("v", lambda e: e.tensor_copy(a_crwb[:, :], a_crw[:, :]))
    vo("v", lambda e: e.memset(a_col[:, C_C:C_C + 1], MAGIC))
    vo("v", lambda e: e.memset(a_row[0:1, R_EPS:R_EPS + 1], EPS))
    vo("v", lambda e: e.tensor_copy(a_rowb[0:1, 0:1024], a_f32[0:1, 0:1024]),
       sem=sV)
    k.wait("s", sSY)
    vo("s", lambda e: e.activation(a_row[0:1, R_SGX:R_SGX + 1],
                                   a_row[0:1, R_SGX:R_SGX + 1], AF.Sigmoid))
    vo("s", lambda e: e.activation(a_row[0:1, R_SGD:R_SGD + 1],
                                   a_row[0:1, R_SGD:R_SGD + 1], AF.Sigmoid), sem=sS)

    k.wait("t", sV)
    k.wait("t", sGP)
    for sl in range(2):
        mm(pA[:, sl * 512:(sl + 1) * 512], a_crwb[0:1, 0:128],
           a_rowb[0:1, sl * 512:(sl + 1) * 512], start=True, stop=True,
           sem=sT if sl == 1 else None)
    k.wait("v", sT)
    BVB = a_rowb[:, 0:1024]
    vo("v", lambda e: e.tensor_copy(BVB, pA[:, 0:1024]), sem=sV)

    for tc in range(NTC):
        for half in range(2):
            pp = (pA if half == 0 else pB)[:, 1024 + (tc % 2) * 512:
                                           1024 + (tc % 2) * 512 + 512]
            for j in range(4):
                g = half * 4 + j
                mm(pp[:, j * 128:(j + 1) * 128],
                   a_b[:, tc * C + g * 128:tc * C + (g + 1) * 128],
                   a_idb[:, :], start=True, stop=True,
                   sem=sT if j == 3 else None)
        k.wait("s", sT)
        for half in range(2):
            pp = (pA if half == 0 else pB)[:, 1024 + (tc % 2) * 512:
                                           1024 + (tc % 2) * 512 + 512]
            def ev(e, tc=tc, half=half, pp=pp):
                outs = a_x.ap().rearrange("p (g t) -> p g t", g=NCC)
                return e.copy(outs[:, half * 4:(half + 1) * 4,
                                   tc * 128:(tc + 1) * 128], pp)
            vo("s", ev, sem=sS)
        if tc >= 1:
            k.wait("t", sS)
    k.barrier()

    def ln_norm(codes_max):
        onescol = a_crwb[:, 300:301]
        for g in range(NCC):
            sq = scr(S_ST0, T) if g % 2 == 0 else scr(S_BC1, T)
            if g >= 2:
                k.wait("v", sT)
            vo("v", lambda e, g=g, sq=sq: e.tensor_tensor(sq, xfm(g), xfm(g),
                                                          ALU.mult), sem=sV)
            k.wait("t", sV)
            for sl in range(4):
                mm(pA[0:1, sl * 512:(sl + 1) * 512], onescol,
                   xfm(g)[:, sl * 512:(sl + 1) * 512], start=(g == 0),
                   stop=(g == NCC - 1))
                mm(pB[0:1, sl * 512:(sl + 1) * 512], onescol,
                   sq[:, sl * 512:(sl + 1) * 512], start=(g == 0),
                   stop=(g == NCC - 1),
                   sem=sT if sl == 3 else None)
        MSQ = a_f32[0:1, 0:T]
        PSUMR = pA[0:1, :]
        PSQR = pB[0:1, :]
        RBM = a_scr[0:1, S_ST0:S_ST0 + T]
        RBS = a_scr[0:1, S_BC1:S_BC1 + T]
        k.wait("s", sT)
        vo("s", lambda e: e.activation(RBM, PSUMR, AF.Identity, scale=1.0 / C))
        vo("s", lambda e: e.activation(MSQ, PSUMR, AF.Square, scale=1.0 / C))
        vo("s", lambda e: e.activation(PSQR, PSQR, AF.Identity, scale=1.0 / C),
           sem=sS)
        k.wait("v", sS)
        vo("v", lambda e: e.tensor_tensor(PSQR, PSQR, MSQ, ALU.subtract), sem=sV)
        k.wait("s", sV)
        vo("s", lambda e: e.activation(MSQ, PSQR, AF.Ln, bias=a_row[0:1, R_EPS:R_EPS + 1]))
        vo("s", lambda e: e.activation(MSQ, MSQ, AF.Exp, scale=-0.5))
        vo("s", lambda e: e.activation(RBS, MSQ, AF.Identity), sem=sS)
        k.wait("v", sS)
        vo("v", lambda e: e.nop(), sem=sV)
        k.wait("t", sV)
        orow = a_crwb[0:1, 0:128]
        for sl in range(4):
            mm(pA[:, sl * 512:(sl + 1) * 512], orow, RBM[:, sl * 512:(sl + 1) * 512],
               start=True, stop=True)
            mm(pB[:, sl * 512:(sl + 1) * 512], orow, RBS[:, sl * 512:(sl + 1) * 512],
               start=True, stop=True, sem=sT if sl == 3 else None)
        k.wait("v", sT)
        MB = scr(S_BC0, T)
        RSB = scr(S_BC1, T)
        vo("v", lambda e: e.tensor_copy(MB, pA[:, :]))
        vo("v", lambda e: e.tensor_copy(RSB, pB[:, :]), sem=sV)
        for g in range(NCC):
            vo("v", lambda e, g=g: e.tensor_tensor(bfm(g), xfm(g), MB, ALU.subtract))
            vo("v", lambda e, g=g: e.tensor_tensor(bfm(g), bfm(g), RSB, ALU.mult),
               sem=sV)
            if codes_max:
                vo("v", lambda e, g=g: e.tensor_reduce(
                    a_col[:, C_MX + g:C_MX + g + 1], bfm(g), AXX, ALU.max,
                    apply_absolute_value=True), sem=sV)
        if codes_max:
            k.run("v", lambda e: e.drain())

    def global_max(ncols, scale_cols=None, sfact_row=None):
        if scale_cols is not None:
            for i in range(ncols):
                vo("v", lambda e, i=i: e.tensor_tensor(
                    a_col[:, C_MX + i:C_MX + i + 1],
                    a_col[:, C_MX + i:C_MX + i + 1],
                    a_col[:, scale_cols + i:scale_cols + i + 1], ALU.mult))
        step = 1
        while step < ncols:
            for i in range(0, ncols, 2 * step):
                if i + step < ncols:
                    vo("v", lambda e, i=i, s=step: e.tensor_tensor(
                        a_col[:, C_MX + i:C_MX + i + 1],
                        a_col[:, C_MX + i:C_MX + i + 1],
                        a_col[:, C_MX + i + s:C_MX + i + s + 1], ALU.max))
            step *= 2
        vo("v", lambda e: e.drain(), sem=sV)
        k.wait("t", sV)
        k.inc("t", lambda e: e.transpose(pA[0:1, 0:128], a_col[:, C_MX:C_MX + 1],
                                         a_id[:, :]), sT)
        k.wait("v", sT)
        vo("v", lambda e: e.tensor_reduce(a_row[0:1, R_LM:R_LM + 1],
                                          pA[0:1, 0:128], AXX, ALU.max), sem=sV)
        k.run("v", lambda e: e.drain())
        if sfact_row is not None:
            vo("v", lambda e: e.tensor_tensor(
                a_row[0:1, R_LM:R_LM + 1], a_row[0:1, R_LM:R_LM + 1],
                a_row[0:1, sfact_row:sfact_row + 1], ALU.mult), sem=sV)
            k.run("v", lambda e: e.drain())
        vo("v", lambda e: e.memset(a_row[0:1, R_CC:R_CC + 8], 0.0))
        vo("v", lambda e: e.tensor_copy(a_row[0:1, R_CC:R_CC + 1],
                                        a_row[0:1, R_LM:R_LM + 1]), sem=sV)
        k.wait("y", sV)
        k.wait("y", sCC)
        dma("y", d_ccin[:, :], a_row[0:1, R_CC:R_CC + 8])
        k.wait("g", sSY)
        k.inc("g", lambda e: e.collective_compute(
            "AllReduce", ALU.max, replica_groups=[list(range(8))],
            ins=[d_ccin[:, :]], outs=[d_ccout[:, :]]), sCC)
        k.wait("g", sCC)
        dma("g", a_row[0:1, R_GM:R_GM + 1], d_ccout[0:1, 0:1])
        k.wait("v", sGP)

    def bcast_scalar(rowoff, coloff):
        k.wait("t", sV)
        mm(pB[:, 0:1], a_crw[0:1, 0:128], a_row[0:1, rowoff:rowoff + 1],
           start=True, stop=True, sem=sT)
        k.wait("v", sT)
        vo("v", lambda e, c=coloff: e.tensor_copy(a_col[:, c:c + 1], pB[:, 0:1]),
           sem=sV)

    def scalar_chain(dst_sd, dst_inv, sig_row):
        vo("v", lambda e, d=dst_sd: e.tensor_tensor(
            a_row[0:1, d:d + 1], a_row[0:1, R_GM:R_GM + 1],
            a_row[0:1, sig_row:sig_row + 1], ALU.mult), sem=sV)
        k.run("v", lambda e: e.drain())
        vo("v", lambda e, d=dst_sd, i=dst_inv: e.reciprocal(
            a_row[0:1, i:i + 1], a_row[0:1, d:d + 1]))
        k.run("v", lambda e: e.drain())
        vo("v", lambda e, i=dst_inv: e.tensor_scalar(
            a_row[0:1, i:i + 1], a_row[0:1, i:i + 1], 127.0, None, ALU.mult))
        vo("v", lambda e, d=dst_sd: e.tensor_scalar(
            a_row[0:1, d:d + 1], a_row[0:1, d:d + 1], 1.0 / 127.0, None,
            ALU.mult), sem=sV)
        k.run("v", lambda e: e.drain())

    def quant_chunks(nch, src_fn, scale_col, out_fn):
        for i in range(nch):
            sc = scale_col(i) if callable(scale_col) else scale_col
            u = a_f32[:, FU:FU + T]
            k.wait("s", sV)
            vo("s", lambda e, i=i, sc=sc, u=u: e.activation(
                u, src_fn(i), AF.Identity, bias=a_col[:, C_C:C_C + 1], scale=sc),
               sem=sS)
            k.wait("v", sS)
            vo("v", lambda e, u=u: e.tensor_scalar(u, u, -MAGIC, 127.0,
                                                   ALU.add, ALU.min))
            vo("v", lambda e, i=i, u=u: e.tensor_scalar(out_fn(i), u, -128.0,
                                                        None, ALU.max), sem=sV)
        k.wait("v", sV)

    def wquant_rows(wf, scol_ap, wint, width):
        rmax = a_col[:, C_T1:C_T1 + 1]
        rinv = a_col[:, C_T2:C_T2 + 1]
        vo("v", lambda e: e.tensor_reduce(rmax, wf, AXX, ALU.max,
                                          apply_absolute_value=True), sem=sV)
        k.run("v", lambda e: e.drain())
        vo("v", lambda e: e.tensor_scalar(scol_ap, rmax, 1.0 / 127.0, None,
                                          ALU.mult))
        vo("v", lambda e: e.reciprocal(rinv, rmax))
        k.run("v", lambda e: e.drain())
        vo("v", lambda e: e.tensor_scalar(rinv, rinv, 127.0, None, ALU.mult))
        k.run("v", lambda e: e.drain())
        for part in range((width + 1023) // 1024):
            lo = part * 1024
            hi = min(width, lo + 1024)
            u = a_f32[:, FU:FU + (hi - lo)]
            vo("v", lambda e, lo=lo, hi=hi, u=u, wf=wf: e.tensor_scalar(
                u, wf[:, lo:hi], rinv, MAGIC, ALU.mult, ALU.add))
            vo("v", lambda e, lo=lo, hi=hi, u=u, wint=wint: e.tensor_scalar(
                wint[:, lo:hi], u, -MAGIC, None, ALU.add), sem=sV)

    def transpose_tiles(wint, ntile, dststage, pbank):
        k.wait("t", sV)
        k.wait("t", sS)
        for i in range(ntile):
            pp = pbank[:, (i % 4) * 128:(i % 4) * 128 + 128]
            mm(pp, wint[:, i * 128:(i + 1) * 128], a_idb[:, :], start=True,
               stop=True, sem=sT if (i % 4 == 3 or i == ntile - 1) else None)
            if i % 4 == 3 or i == ntile - 1:
                k.wait("s", sT)
                base = (i // 4) * 4
                ngrp = i - base + 1
                vo("s", lambda e, base=base, ngrp=ngrp, d=dststage, p=pbank: e.copy(
                    d[:, base * 128:(base + ngrp) * 128], p[:, 0:ngrp * 128]),
                   sem=sS)
                if i != ntile - 1:
                    k.wait("t", sS)

    def emit_ffn(save, dbg_tag=None):
        ln_norm(codes_max=True)
        global_max(NCC)
        scalar_chain(R_SX, R_INV, R_SGX)
        bcast_scalar(R_INV, C_INV)
        bcast_scalar(R_SX, C_SX)
        quant_chunks(NCC, bfm, a_col[:, C_INV:C_INV + 1], bfm)
        if not save:
            vo("v", lambda e: e.tensor_scalar(
                a_col[:, C_KGA:C_KGA + NIC], a_col[:, C_SG:C_SG + NIC],
                a_col[:, C_SX:C_SX + 1], None, ALU.mult), sem=sV)
        k.barrier()
        if debug and dbg_tag == 1:
            k.wait("y", sV)
            dma("y", dbg["sc"][0:1, 0:16], a_row[0:1, 0:16])
            dma("y", dbg["sc"][0:1, 16:32], a_row[0:1, 16:32])
        st_ready = [None, None]
        kgm = [0, 0]
        stm = [0, 0]
        wf_free = [0]
        wint_free = [0]
        mm_done = [0, 0]
        ev_done = [(0, 0)]

        def gu_stage(i):
            par = i % 2
            gst = scr(S_ST0, 1024) if par == 0 else scr(S_BC0, 1024)
            ust = scr(S_ST1, 1024) if par == 0 else scr(S_BC0 + 1024, 1024)
            return gst, ust

        def gu_prep_a(i):
            par = i % 2
            gst, ust = gu_stage(i)
            if save:
                k.wait("y", sV, wf_free[0])
                wfg = a_f32[:, FW:FW + 1024]
                wfu = a_f32[:, FW + 1024:FW + 2048]
                dma("y", wfg, d_wg[i * 128:(i + 1) * 128, :])
                dma("y", wfu, d_wu[i * 128:(i + 1) * 128, :])
                ldm = k.mark(sSY)
                k.wait("v", sSY, ldm)
                k.wait("v", sT, wint_free[0])
                wquant_rows(wfg, a_col[:, C_SG + i:C_SG + i + 1],
                            scr(S_ST2, 1024), 1024)
                wquant_rows(wfu, a_col[:, C_SU + i:C_SU + i + 1],
                            scr(S_BC1 + 1024, 1024), 1024)
                wf_free[0] = k.mark(sV)
                vo("v", lambda e, i=i, par=par: e.tensor_tensor(
                    a_col[:, C_KG + par:C_KG + par + 1],
                    a_col[:, C_SG + i:C_SG + i + 1],
                    a_col[:, C_SX:C_SX + 1], ALU.mult), sem=sV)
                kgm[par] = k.mark(sV)
            else:
                k.wait("y", sT, mm_done[par])
                dma("y", gst, d_wgt[i, :, :])
                dma("y", ust, d_wut[i, :, :])
                st_ready[par] = ("y", k.mark(sSY))

        def gu_prep_b(i):
            if not save:
                return
            par = i % 2
            gst, ust = gu_stage(i)
            ss, vv = ev_done[0]
            k.wait("t", sS, ss)
            k.wait("t", sV, vv)
            k.wait("t", sT, mm_done[par])
            k.wait("s", sSY, stm[par])
            transpose_tiles(scr(S_ST2, 1024), 8, gst, pA[:, 0:512])
            transpose_tiles(scr(S_BC1 + 1024, 1024), 8, ust, pA[:, 512:1024])
            wint_free[0] = k.mark(sT)
            evm = k.mark(sS)
            st_ready[par] = ("s", evm)
            k.wait("y", sS, evm)
            dma("y", d_wgt[i, :, :], gst)
            dma("y", d_wut[i, :, :], ust)
            stm[par] = k.mark(sSY)

        gu_prep_a(0)
        gu_prep_b(0)
        for i in range(NIC):
            par = i % 2
            gst, ust = gu_stage(i)
            if i + 1 < NIC:
                gu_prep_a(i + 1)
            kind, mk = st_ready[par]
            k.wait("t", sSY if kind == "y" else sS, mk)
            for th in range(2):
                pg = (pA if th == 0 else pB)[:, 0:1024]
                pu = (pA if th == 0 else pB)[:, 1024:2048]
                ssm, vvm = ev_done[0]
                k.wait("t", sS, ssm)
                k.wait("t", sV, vvm)
                for h in range(NCC):
                    for sl in range(2):
                        nsl = th * 1024 + sl * 512
                        mm(pg[:, sl * 512:(sl + 1) * 512],
                           gst[:, h * 128:(h + 1) * 128],
                           bfm(h)[:, nsl:nsl + 512], start=(h == 0),
                           stop=(h == NCC - 1))
                        mm(pu[:, sl * 512:(sl + 1) * 512],
                           ust[:, h * 128:(h + 1) * 128],
                           bfm(h)[:, nsl:nsl + 512], start=(h == 0),
                           stop=(h == NCC - 1),
                           sem=sT if (h == NCC - 1 and sl == 1) else None)
                mmk = k.mark(sT)
                k.wait("s", sT, mmk)
                sil = scr(S_PT, 1024)
                kgap = (a_col[:, C_KG + par:C_KG + par + 1] if save
                        else a_col[:, C_KGA + i:C_KGA + i + 1])
                k.wait("s", sV, max(kgm[par], ev_done[0][1]))
                vo("s", lambda e, pg=pg, sil=sil, kgap=kgap: e.activation(
                    sil, pg, AF.Silu, scale=kgap), sem=sS)
                silm = k.mark(sS)
                k.wait("v", sS, silm)
                vo("v", lambda e, i=i, th=th, pu=pu, sil=sil: e.scalar_tensor_tensor(
                    tcode(i)[:, th * 1024:(th + 1) * 1024], pu, 1.0, sil,
                    ALU.mult, ALU.mult), sem=sV)
                ev_done[0] = (k.mark(sS), k.mark(sV))
            mm_done[par] = k.mark(sT)
            if i + 1 < NIC:
                gu_prep_b(i + 1)
            vo("v", lambda e, i=i: e.tensor_reduce(
                a_col[:, C_MX + i:C_MX + i + 1], tcode(i), AXX, ALU.max,
                apply_absolute_value=True), sem=sV)
        k.wait("v", sV)
        k.barrier()
        global_max(NIC, scale_cols=C_SU, sfact_row=R_SX)
        scalar_chain(R_SD, R_TMP, R_SGD)
        vo("v", lambda e: e.tensor_tensor(a_row[0:1, R_F2:R_F2 + 1],
                                          a_row[0:1, R_SX:R_SX + 1],
                                          a_row[0:1, R_TMP:R_TMP + 1], ALU.mult))
        vo("v", lambda e: e.tensor_scalar(a_row[0:1, R_F3:R_F3 + 1],
                                          a_row[0:1, R_SD:R_SD + 1], 0.5, None,
                                          ALU.mult), sem=sV)
        k.wait("v", sV)
        bcast_scalar(R_F2, C_INV)
        bcast_scalar(R_F3, C_F3)
        k.barrier()
        if debug and dbg_tag == 1:
            k.wait("y", sV)
            dma("y", dbg["sc"][0:1, 32:64], a_row[0:1, 0:32])
        vo("v", lambda e: e.tensor_scalar(
            a_col[:, C_RCA:C_RCA + NIC], a_col[:, C_SU:C_SU + NIC],
            a_col[:, C_INV:C_INV + 1], None, ALU.mult), sem=sV)
        k.wait("v", sV)
        quant_chunks(NIC, tcode, lambda i: a_col[:, C_RCA + i:C_RCA + i + 1], tcode)
        k.barrier()
        dn_ready = [None, None]
        dn_mm_done = [0, 0]
        dn_wf_free = [0]
        dn_wint_free = [0]
        dn_ev_done = [0]

        def dn_stage(h):
            return scr(S_ST0, II) if h % 2 == 0 else scr(S_BC0, II)

        def dn_prep_a(h):
            par = h % 2
            dstg = dn_stage(h)
            if save:
                k.wait("y", sV, dn_wf_free[0])
                wfh = a_f32[:, 0:II]
                dma("y", wfh, d_wd[h * 128:(h + 1) * 128, :])
                ldm = k.mark(sSY)
                k.wait("v", sSY, ldm)
                k.wait("v", sT, dn_wint_free[0])
                rmax = a_col[:, C_T1:C_T1 + 1]
                rinv = a_col[:, C_T2:C_T2 + 1]
                vo("v", lambda e, wfh=wfh: e.tensor_reduce(
                    rmax, wfh, AXX, ALU.max, apply_absolute_value=True), sem=sV)
                k.run("v", lambda e: e.drain())
                vo("v", lambda e, h=h: e.tensor_scalar(
                    a_col[:, C_SD + h:C_SD + h + 1], rmax, 1.0 / 127.0, None,
                    ALU.mult))
                vo("v", lambda e: e.reciprocal(rinv, rmax))
                k.run("v", lambda e: e.drain())
                vo("v", lambda e: e.tensor_scalar(rinv, rinv, 127.0, None,
                                                  ALU.mult))
                k.run("v", lambda e: e.drain())
                wint = scr(S_PT, II)
                u2 = a_f32[:, II:II + 1024]
                for part in range(3):
                    lo = part * 1024
                    hi = min(II, lo + 1024)
                    vo("v", lambda e, lo=lo, hi=hi, u2=u2, wfh=wfh: e.tensor_scalar(
                        u2[:, 0:hi - lo], wfh[:, lo:hi], rinv, MAGIC, ALU.mult,
                        ALU.add))
                    vo("v", lambda e, lo=lo, hi=hi, u2=u2, wint=wint: e.tensor_scalar(
                        wint[:, lo:hi], u2[:, 0:hi - lo], -MAGIC, None, ALU.add),
                        sem=sV)
                dn_wf_free[0] = k.mark(sV)
                vo("v", lambda e, h=h, par=par: e.tensor_tensor(
                    a_col[:, C_FIN + par:C_FIN + par + 1],
                    a_col[:, C_SD + h:C_SD + h + 1],
                    a_col[:, C_F3:C_F3 + 1], ALU.mult), sem=sV)
            else:
                k.wait("y", sT, dn_mm_done[par])
                dma("y", dstg, d_wdt[h, :, :])
                dn_ready[par] = ("y", k.mark(sSY))
                vo("v", lambda e, h=h, par=par: e.tensor_tensor(
                    a_col[:, C_FIN + par:C_FIN + par + 1],
                    a_col[:, C_SD + h:C_SD + h + 1],
                    a_col[:, C_F3:C_F3 + 1], ALU.mult), sem=sV)

        def dn_prep_b(h):
            if not save:
                return
            par = h % 2
            dstg = dn_stage(h)
            pbank = (pA if par == 0 else pB)[:, 1024:1536]
            k.wait("t", sT, dn_mm_done[par])
            k.wait("t", sV, dn_ev_done[0])
            k.wait("s", sSY)
            transpose_tiles(scr(S_PT, II), NIC, dstg, pbank)
            dn_wint_free[0] = k.mark(sT)
            evm = k.mark(sS)
            dn_ready[par] = ("s", evm)
            k.wait("y", sS, evm)
            dma("y", d_wdt[h, :, :], dstg)

        dn_prep_a(0)
        dn_prep_b(0)
        for h in range(NCC):
            par = h % 2
            dstg = dn_stage(h)
            pdn = pA if par == 0 else pB
            if h + 1 < NCC:
                dn_prep_a(h + 1)
            kind, mk = dn_ready[par]
            k.wait("t", sSY if kind == "y" else sS, mk)
            k.wait("t", sV, dn_ev_done[0])
            for i in range(NIC):
                for sl in range(4):
                    mm(pdn[:, sl * 512:(sl + 1) * 512],
                       dstg[:, i * 128:(i + 1) * 128],
                       tcode(i)[:, sl * 512:(sl + 1) * 512], start=(i == 0),
                       stop=(i == NIC - 1),
                       sem=sT if (i == NIC - 1 and sl == 3) else None)
            dn_mm_done[par] = k.mark(sT)
            k.wait("v", sT, dn_mm_done[par])
            k.wait("v", sV)
            vo("v", lambda e, h=h, par=par, pdn=pdn: e.scalar_tensor_tensor(
                xfm(h), pdn[:, :], a_col[:, C_FIN + par:C_FIN + par + 1],
                xfm(h), ALU.mult, ALU.add), sem=sV)
            dn_ev_done[0] = k.mark(sV)
            if h + 1 < NCC:
                dn_prep_b(h + 1)
        k.barrier()

    def emit_attn():
        ln_norm(codes_max=False)
        k.barrier()
        for f in range(16):
            wt = scr(S_ST0 + (f % 2) * 1024, 1024)
            if f >= 2:
                k.wait("g", sT)
            dma("g", wt.rearrange("p (c j) -> p c j", c=8),
                d_wqkv.ap()[:, f * 128:(f + 1) * 128]
                .rearrange("(c p) j -> p c j", p=128))
            k.wait("t", sGP)
            pq = pA if f % 2 == 0 else pB
            for c in range(NCC):
                for sl in range(4):
                    mm(pq[:, sl * 512:(sl + 1) * 512], wt[:, c * 128:(c + 1) * 128],
                       bfm(c)[:, sl * 512:(sl + 1) * 512], start=(c == 0),
                       stop=(c == NCC - 1),
                       sem=sT if (c == NCC - 1 and sl == 3) else None)
            k.wait("s", sT)
            dst = qfm(f) if f < 8 else kfm(f - 8)
            vo("s", lambda e, dst=dst, pq=pq, f=f: e.activation(
                dst, pq[:, :], AF.Identity, bias=a_bq[:, f:f + 1], scale=1.0),
               sem=sS)
            k.wait("t", sS)
        k.barrier()
        vo("v", lambda e: e.memset(a_big[:, 16 * T:16 * T + NTC * NHEAD * 65],
                                   1.0), sem=sV)
        WV = a_scr[:, S_BC1:S_BC1 + 8192]
        dma("g", WV.rearrange("p (c j) -> p c j", c=8),
            d_wqkv.ap()[:, 2048:3072].rearrange("(c p) j -> p c j", p=128))
        k.wait("t", sGP)
        k.wait("t", sV)
        for tc in range(NTC):
            pv = (pA if tc % 2 == 0 else pB)[:, 0:1024]
            for c in range(NCC):
                for sl in range(2):
                    mm(pv[:, sl * 512:(sl + 1) * 512],
                       bfm(c)[:, tc * 128:(tc + 1) * 128],
                       WV[:, c * 1024 + sl * 512:c * 1024 + (sl + 1) * 512],
                       start=(c == 0), stop=(c == NCC - 1),
                       sem=sT if (c == NCC - 1 and sl == 1) else None)
            k.wait("v", sT)
            def vev(e, tc=tc, pv=pv):
                outp = vaug(tc).rearrange("p (h j) -> p h j", j=65)[:, :, 0:64]
                return e.scalar_tensor_tensor(
                    outp, pv, 1.0, BVB.rearrange("p (h j) -> p h j", j=64),
                    ALU.mult, ALU.add)
            vo("v", vev, sem=sV)
            k.wait("t", sV)
        k.barrier()
        A_COS = scr(S_PT, T)
        A_SIN = scr(S_PT + T, T)
        dma("g", A_COS, d_cos[:, :])
        dma("g", A_SIN, d_sin[:, :])
        k.wait("v", sGP)
        for c in range(16):
            src = qfm(c) if c < 8 else kfm(c - 8)
            prm = scr(S_ST0 + (c % 2) * 1024, 1024) if False else None
            prm = scr(S_ST0, T) if c % 2 == 0 else scr(S_ST2, T)
            t1 = scr(S_ST2 + 1024, 1024) if False else None
            t1 = a_scr[:, S_BC0:S_BC0 + T] if c % 2 == 0 else a_scr[:, S_BC1:S_BC1 + T]
            pr = (pA if c % 2 == 0 else pB)
            if c >= 2:
                k.wait("t", sS)
            for sl in range(4):
                mm(pr[:, sl * 512:(sl + 1) * 512], a_rot[:, :],
                   src[:, sl * 512:(sl + 1) * 512], start=True, stop=True,
                   sem=sT if sl == 3 else None)
            k.wait("s", sT)
            if c >= 2:
                k.wait("s", sV)
            vo("s", lambda e, prm=prm, pr=pr: e.copy(prm, pr[:, :]), sem=sS)
            k.wait("v", sS)
            vo("v", lambda e, src=src, t1=t1: e.tensor_tensor(
                t1, src, A_COS, ALU.mult))
            vo("v", lambda e, prm=prm: e.tensor_tensor(prm, prm, A_SIN,
                                                       ALU.mult))
            vo("v", lambda e, src=src, t1=t1, prm=prm: e.tensor_tensor(
                src, t1, prm, ALU.add))
            vo("v", lambda e, c=c: e.tensor_reduce(
                a_col[:, C_MQ + c:C_MQ + c + 1], src, AXX, ALU.max,
                apply_absolute_value=True), sem=sV)
        k.wait("v", sV)
        k.barrier()
        k.inc("t", lambda e: e.transpose(pA[0:16, 0:128],
                                         a_col[:, C_MQ:C_MQ + 16], a_id[:, :]), sT)
        k.wait("v", sT)
        r1c = a_f32[0:16, 3072:3073]
        r2c = a_f32[0:16, 3073:3074]
        vo("v", lambda e: e.tensor_reduce(r1c, pA[0:16, 0:64], AXX, ALU.max))
        vo("v", lambda e: e.tensor_reduce(r2c, pA[0:16, 64:128], AXX, ALU.max),
           sem=sV)
        k.run("v", lambda e: e.drain())
        k.wait("t", sV)
        k.inc("t", lambda e: e.transpose(pB[0:1, 0:16], r1c, a_id[0:16, 0:16]), sT)
        k.inc("t", lambda e: e.transpose(pB[0:1, 16:32], r2c, a_id[0:16, 0:16]), sT)
        k.wait("v", sT)
        r1row = a_f32[0:1, 3074:3090]
        r2row = a_f32[0:1, 3090:3106]
        vo("v", lambda e: e.tensor_copy(r1row, pB[0:1, 0:16]))
        vo("v", lambda e: e.tensor_copy(r2row, pB[0:1, 16:32]), sem=sV)
        BE = a_row[0:1, R_BE:R_BE + 16]
        bev = BE.rearrange("p (c t) -> p c t", t=2)

        def r3(ap):
            return ap.rearrange("p (c o) -> p c o", o=1)
        vo("v", lambda e: e.tensor_tensor(bev[:, :, 0:1], r3(r1row[:, 0:8]),
                                          r3(r1row[:, 8:16]), ALU.mult))
        vo("v", lambda e: e.tensor_tensor(bev[:, :, 1:2], r3(r2row[:, 0:8]),
                                          r3(r2row[:, 8:16]), ALU.mult), sem=sV)
        k.run("v", lambda e: e.drain())
        vo("v", lambda e: e.tensor_scalar(BE, BE, -8.0, 40.0, ALU.mult, ALU.add),
           sem=sV)
        k.run("v", lambda e: e.drain())
        k.wait("t", sV)
        for h in range(NHEAD):
            mm(pB[:, h:h + 1], a_crw[0:1, 0:128],
               a_row[0:1, R_BE + h:R_BE + h + 1], start=True, stop=True,
               sem=sT if h == NHEAD - 1 else None)
        k.wait("v", sT)
        vo("v", lambda e: e.tensor_copy(a_col[:, C_BIAS:C_BIAS + 16],
                                        pB[:, 0:16]), sem=sV)
        k.barrier()
        phases = [(hp, qq) for hp in range(8) for qq in range(4)]
        avmk_hist = {-1: 0}
        norm_done = {-1: 0, -2: 0}
        pending = [None]

        def emit_evac(ctx):
            idx, hp, qq, par, avA, avB, hA, hB = ctx
            ZA = a_rowb[64:65, RB_Z:RB_Z + 512]
            ZB = a_rowb[64:65, RB_Z + 512:RB_Z + 1024]
            ZAf = a_f32[64:65, 0:512]
            ZBf = a_f32[64:65, 512:1024]
            k.wait("s", sT, avmk_hist[idx])
            vo("s", lambda e: e.activation(ZAf, avA[64:65, :], AF.Ln))
            vo("s", lambda e: e.activation(ZA, ZAf, AF.Exp, scale=-1.0))
            vo("s", lambda e: e.activation(ZBf, avB[64:65, :], AF.Ln))
            vo("s", lambda e: e.activation(ZB, ZBf, AF.Exp, scale=-1.0), sem=sS)
            zm = k.mark(sS)
            oa = scr(S_ST0, 512)
            ob = scr(S_ST0 + 512, 512)
            k.wait("v", sSC)
            k.wait("v", sT, avmk_hist[idx])
            vo("v", lambda e: e.tensor_copy(oa[0:64, :], avA[0:64, :]))
            vo("v", lambda e: e.tensor_copy(ob[0:64, :], avB[0:64, :]), sem=sV)
            om = k.mark(sV)
            freed = pB[:, par * 1024:par * 1024 + 512]
            k.wait("t", sV, om)
            k.wait("t", sS, zm)
            mm(freed, a_crwb[64:65, 128:256], ZA, start=True, stop=True, sem=sT)
            m1 = k.mark(sT)
            k.wait("v", sT, m1)
            vo("v", lambda e: e.tensor_tensor(
                bfm(hp)[0:64, qq * 512:(qq + 1) * 512], oa[0:64, :],
                freed[0:64, :], ALU.mult), sem=sV)
            m2 = k.mark(sV)
            k.wait("t", sV, m2)
            mm(freed, a_crwb[64:65, 128:256], ZB, start=True, stop=True, sem=sT)
            m3 = k.mark(sT)
            k.wait("v", sT, m3)
            vo("v", lambda e: e.tensor_tensor(
                ob[0:64, :], ob[0:64, :], freed[0:64, :], ALU.mult), sem=sV)
            norm_done[idx] = k.mark(sV)
            k.wait("s", sV, norm_done[idx])
            dma("s", bfm(hp)[64:128, qq * 512:(qq + 1) * 512], ob[0:64, :])

        expmk = 0
        for idx, (hp, qq) in enumerate(phases):
            par = idx % 2
            avA = pB[0:65, par * 1024:par * 1024 + 512]
            avB = pB[0:65, par * 1024 + 512:par * 1024 + 1024]
            hA, hB = 2 * hp, 2 * hp + 1
            pav = {}
            for p in range(8):
                kt0 = 2 * p
                if expmk:
                    k.wait("t", sS, expmk)
                if p == 0:
                    k.wait("t", sV, norm_done[idx - 2])
                for j, (ktx, base) in enumerate(
                        ((kt0, 0), (kt0 + 1, 512), (kt0, 1024), (kt0 + 1, 1536))):
                    lo = 0 if base < 1024 else 64
                    mm(pA[:, base:base + 512],
                       kfm(hp)[lo:lo + 64, ktx * 128:(ktx + 1) * 128],
                       qfm(hp)[lo:lo + 64, qq * 512:(qq + 1) * 512],
                       start=True, stop=True, sem=sT if j == 3 else None)
                scm = k.mark(sT)
                slot = (p % 2) * 2048
                k.wait("s", sT, scm)
                if p >= 2:
                    k.wait("s", sT, pav[p - 2])
                elif idx > 0:
                    k.wait("s", sT, avmk_hist[idx - 1])
                ptA = scr(S_PT + slot, 1024)
                ptB = scr(S_PT + slot + 1024, 1024)
                vo("s", lambda e, ptA=ptA, hA=hA: e.activation(
                    ptA, pA[:, 0:1024], AF.Exp,
                    bias=a_col[:, C_BIAS + hA:C_BIAS + hA + 1], scale=0.125))
                expmk = vo("s", lambda e, ptB=ptB, hB=hB: e.activation(
                    ptB, pA[:, 1024:2048], AF.Exp,
                    bias=a_col[:, C_BIAS + hB:C_BIAS + hB + 1], scale=0.125),
                    sem=sS)
                k.wait("t", sS, expmk)
                for jj, (av, pt, kti) in enumerate(
                        ((avA, ptA, 0), (avA, ptA, 1), (avB, ptB, 0),
                         (avB, ptB, 1))):
                    mm(av, vaug(kt0 + kti)[:, (hA if jj < 2 else hB) * 65:
                                           (hA if jj < 2 else hB) * 65 + 65],
                       pt[:, kti * 512:(kti + 1) * 512],
                       start=(p == 0 and kti == 0), stop=(p == 7 and kti == 1),
                       sem=sT if jj == 3 else None)
                pav[p] = k.mark(sT)
                if p == 1 and pending[0] is not None:
                    emit_evac(pending[0])
                    pending[0] = None
            avmk_hist[idx] = pav[7]
            pending[0] = (idx, hp, qq, par, avA, avB, hA, hB)
        emit_evac(pending[0])
        k.barrier()
        for f in range(NCC):
            wt = scr(S_ST0 + (f % 2) * 1024, 1024)
            if f >= 2:
                k.wait("g", sT)
            dma("g", wt.rearrange("p (c j) -> p c j", c=8),
                d_wout.ap()[:, f * 128:(f + 1) * 128]
                .rearrange("(c p) j -> p c j", p=128))
            k.wait("t", sGP)
            po = pA if f % 2 == 0 else pB
            for c in range(NCC):
                for sl in range(4):
                    mm(po[:, sl * 512:(sl + 1) * 512], wt[:, c * 128:(c + 1) * 128],
                       bfm(c)[:, sl * 512:(sl + 1) * 512], start=(c == 0),
                       stop=(c == NCC - 1),
                       sem=sT if (c == NCC - 1 and sl == 3) else None)
            k.wait("v", sT)
            vo("v", lambda e, f=f, po=po: e.scalar_tensor_tensor(
                xfm(f), po[:, :], a_bo[:, f:f + 1], xfm(f), ALU.add, ALU.add),
               sem=sV)
            k.wait("t", sV)
        k.barrier()

    def emit_store_x(target):
        for tc in range(NTC):
            for half in range(2):
                pp = (pA if half == 0 else pB)[:, (tc % 2) * 512:
                                               (tc % 2) * 512 + 512]
                for j in range(4):
                    g = half * 4 + j
                    mm(pp[:, j * 128:(j + 1) * 128],
                       xfm(g)[:, tc * 128:(tc + 1) * 128], a_idb[:, :],
                       start=True, stop=True, sem=sT if j == 3 else None)
            k.wait("v", sT)
            stg = a_f32[:, FU:FU + 1024]
            vo("v", lambda e, tc=tc, stg=stg: e.tensor_copy(
                stg[:, 0:512], pA[:, (tc % 2) * 512:(tc % 2) * 512 + 512]))
            vo("v", lambda e, tc=tc, stg=stg: e.tensor_copy(
                stg[:, 512:1024], pB[:, (tc % 2) * 512:(tc % 2) * 512 + 512]),
               sem=sV)
            k.wait("y", sV)
            dma("y", target[tc * 128:(tc + 1) * 128, :], stg)
            k.wait("v", sSY)
            k.wait("t", sV)
        k.barrier()

    emit_ffn(save=True, dbg_tag=1)
    if debug:
        emit_store_x(dbg["x1"])
    emit_attn()
    if debug:
        emit_store_x(dbg["x2"])
    emit_ffn(save=False)
    emit_store_x(d_out)
    k.barrier()

    with nc.Block() as block:
        @block.tensor
        def _(e):
            for fn in k.q["t"]:
                fn(e)

        @block.vector
        def _(e):
            for fn in k.q["v"]:
                fn(e)

        @block.scalar
        def _(e):
            for fn in k.q["s"]:
                fn(e)

        @block.gpsimd
        def _(e):
            for fn in k.q["g"]:
                fn(e)

        @block.sync
        def _(e):
            for fn in k.q["y"]:
                fn(e)

    for g in reversed(guards):
        g.__exit__(None, None, None)
    for g in reversed(k.guards):
        g.__exit__(None, None, None)
    return nc


def _host_consts():
    ident = np.eye(128, dtype=np.float32)
    DH = 64
    inv = (1.0 / 10000.0 ** (np.arange(0, DH, 2, dtype=np.float64) / DH))
    tpos = np.arange(T, dtype=np.float64)[:, None]
    fr = tpos * inv[None, :]
    emb = np.concatenate([fr, fr], axis=1)
    cos = np.cos(emb).astype(np.float32).T
    sin = np.sin(emb).astype(np.float32).T
    cos_fm = np.concatenate([cos, cos], axis=0)
    sin_fm = np.concatenate([sin, sin], axis=0)
    R = np.zeros((128, 128), dtype=np.float32)
    for blk in (0, 64):
        for d in range(32):
            R[blk + d, blk + d + 32] = -1.0
            R[blk + d + 32, blk + d] = 1.0
    rotT = R.T.copy()
    crows = np.zeros((128, 384), dtype=np.float32)
    crows[0, 0:128] = 1.0
    crows[64, 0:128] = 1.0
    crows[64, 128:192] = 1.0
    crows[64, 320:384] = 1.0
    crows[:, 300] = 1.0
    return ident, cos_fm, sin_fm, crows, rotT


_NC_CACHE = {}


def kernel(L0_attn_x, L0_attn_W_qkv, L0_attn_b_qkv, L0_attn_W_out, L0_attn_b_out,
           L0_ffn_clip_x, L0_ffn_clip_down, L0_ffn_w_gate, L0_ffn_w_up,
           L0_ffn_w_down, _debug=False):
    key = bool(_debug)
    if key not in _NC_CACHE:
        _NC_CACHE[key] = build_nc(debug=_debug)
    nc = _NC_CACHE[key]
    ident, cos_fm, sin_fm, crows, rotT = _host_consts()
    x = np.ascontiguousarray(np.asarray(L0_attn_x, dtype=np.float32))
    base = {
        "wqkv": np.ascontiguousarray(np.asarray(L0_attn_W_qkv, np.float32)),
        "bqkv": np.ascontiguousarray(np.asarray(L0_attn_b_qkv, np.float32)),
        "wout": np.ascontiguousarray(np.asarray(L0_attn_W_out, np.float32)),
        "bout": np.ascontiguousarray(np.asarray(L0_attn_b_out, np.float32)),
        "clipx": np.ascontiguousarray(np.asarray(L0_ffn_clip_x, np.float32)),
        "clipd": np.ascontiguousarray(np.asarray(L0_ffn_clip_down, np.float32)),
        "wg": np.ascontiguousarray(np.asarray(L0_ffn_w_gate, np.float32)),
        "wu": np.ascontiguousarray(np.asarray(L0_ffn_w_up, np.float32)),
        "wd": np.ascontiguousarray(np.asarray(L0_ffn_w_down, np.float32)),
        "cident": ident, "ccos": cos_fm, "csin": sin_fm, "crows": crows,
        "crot": rotT,
    }
    in_maps = []
    for b in range(8):
        m = dict(base)
        m["x"] = np.ascontiguousarray(x[b])
        in_maps.append(m)
    res = run_bass_kernel_spmd(nc, in_maps, core_ids=list(range(8)))
    out = np.stack([res.results[b]["out"] for b in range(8)], axis=0)
    if _debug:
        extra = {kk: np.stack([res.results[b]["dbg_" + kk] for b in range(8)])
                 for kk in ("x1", "x2", "sc")}
        return out, extra
    return out


# version 06
# speedup vs baseline: 1.5728x; 1.2264x over previous
import numpy as np
import ml_dtypes
import concourse.bass as bass
import concourse.mybir as mybir
from concourse.bass_utils import run_bass_kernel_spmd

F32 = mybir.dt.float32
BF16 = mybir.dt.bfloat16
AF = mybir.ActivationFunctionType
ALU = mybir.AluOpType
AXX = mybir.AxisListType.X

MAGIC = float(np.float32(1.5 * 2 ** 23))
T = 2048
C = 1024
NCC = 8
NTC = 16
II = 2816
NIC = 22
NHEAD = 16
EPS = 1e-5

ENGS = ("t", "v", "s", "g", "y")


class B:

    def __init__(self, nc):
        self.nc = nc
        self.q = {e: [] for e in ENGS}
        self.sems = {}
        self.cnt = {}
        self.guards = []

    def sem(self, name):
        g = self.nc.semaphore(name)
        h = g.__enter__()
        self.guards.append(g)
        self.sems[name] = h
        self.cnt[name] = 0
        return name

    def emit(self, e, fn):
        self.q[e].append(fn)

    def inc(self, e, mk, sem, k=1):
        h = self.sems[sem]
        self.emit(e, lambda eng, mk=mk, h=h, k=k: mk(eng).then_inc(h, k))
        self.cnt[sem] += k
        return self.cnt[sem]

    def run(self, e, mk):
        self.emit(e, mk)

    def wait(self, e, sem, val=None):
        h = self.sems[sem]
        v = self.cnt[sem] if val is None else val
        if v:
            self.emit(e, lambda eng, h=h, v=v: eng.wait_ge(h, v))

    def mark(self, sem):
        return self.cnt[sem]

    def barrier(self):
        snap = {s: self.cnt[s] for s in self.sems}
        for e in ENGS:
            for s, v in snap.items():
                if v:
                    self.wait(e, s, v)


def build_nc(debug=False):
    nc = bass.Bass("TRN2", target_bir_lowering=False)

    d_x = nc.dram_tensor("x", [T, C], F32, kind="ExternalInput")
    d_wqkv = nc.dram_tensor("wqkv", [C, 3 * C], F32, kind="ExternalInput")
    d_bqkv = nc.dram_tensor("bqkv", [3 * C], F32, kind="ExternalInput")
    d_wout = nc.dram_tensor("wout", [C, C], F32, kind="ExternalInput")
    d_bout = nc.dram_tensor("bout", [C], F32, kind="ExternalInput")
    d_clipx = nc.dram_tensor("clipx", [1], F32, kind="ExternalInput")
    d_clipd = nc.dram_tensor("clipd", [1], F32, kind="ExternalInput")
    d_wg = nc.dram_tensor("wg", [II, C], F32, kind="ExternalInput")
    d_wu = nc.dram_tensor("wu", [II, C], F32, kind="ExternalInput")
    d_wd = nc.dram_tensor("wd", [C, II], F32, kind="ExternalInput")
    d_ident = nc.dram_tensor("cident", [128, 128], F32, kind="ExternalInput")
    d_cos = nc.dram_tensor("ccos", [128, T], F32, kind="ExternalInput")
    d_sin = nc.dram_tensor("csin", [128, T], F32, kind="ExternalInput")
    d_crows = nc.dram_tensor("crows", [128, 384], F32, kind="ExternalInput")
    d_crot = nc.dram_tensor("crot", [128, 128], F32, kind="ExternalInput")
    d_out = nc.dram_tensor("out", [T, C], F32, kind="ExternalOutput")

    d_wgt = nc.dram_tensor("wgt_s", [NIC, 128, C], BF16)
    d_wut = nc.dram_tensor("wut_s", [NIC, 128, C], BF16)
    d_wdt = nc.dram_tensor("wdt_s", [NCC, 128, II], BF16)
    d_ccin = nc.dram_tensor("ccin", [1, 8], F32)
    d_ccout = nc.dram_tensor("ccout", [1, 8], F32)

    dbg = {}
    if debug:
        dbg["x1"] = nc.dram_tensor("dbg_x1", [T, C], F32, kind="ExternalOutput")
        dbg["x2"] = nc.dram_tensor("dbg_x2", [T, C], F32, kind="ExternalOutput")
        dbg["sc"] = nc.dram_tensor("dbg_sc", [1, 64], F32, kind="ExternalOutput")

    guards = []

    def sb(name, shape, dt):
        g = nc.sbuf_tensor(name, shape, dt)
        t = g.__enter__()
        guards.append(g)
        return t

    def psu(name, shape):
        g = nc.psum_tensor(name, shape, F32)
        t = g.__enter__()
        guards.append(g)
        return t

    a_x = sb("a_x", [128, NCC * T], BF16)
    a_b = sb("a_b", [128, NCC * T], BF16)
    a_big = sb("a_big", [128, 49408], BF16)
    a_scr = sb("a_scr", [128, 11264], BF16)
    a_f32 = sb("a_f32", [128, 3840], F32)
    a_col = sb("a_col", [128, 256], F32)
    a_row = sb("a_row", [128, 256], F32)
    a_rowb = sb("a_rowb", [128, 2048], BF16)
    a_id = sb("a_id", [128, 128], F32)
    a_idb = sb("a_idb", [128, 128], BF16)
    a_crw = sb("a_crw", [128, 384], F32)
    a_crwb = sb("a_crwb", [128, 384], BF16)
    a_rot = sb("a_rot", [128, 128], BF16)
    a_bq = sb("a_bq", [128, 24], F32)
    a_bo = sb("a_bo", [128, 8], F32)
    pA = psu("pA", [128, 2048])
    pB = psu("pB", [128, 2048])

    def xfm(g):
        return a_x[:, g * T:(g + 1) * T]

    def bfm(g):
        return a_b[:, g * T:(g + 1) * T]

    def tcode(i):
        return a_big[:, i * T:(i + 1) * T]

    def qfm(c):
        return a_big[:, c * T:(c + 1) * T]

    def kfm(c):
        return a_big[:, (8 + c) * T:(9 + c) * T]

    def vaug(tc):
        o = 16 * T + tc * (NHEAD * 65)
        return a_big[:, o:o + NHEAD * 65]

    S_BC0 = 0
    S_BC1 = 2048
    S_ST0 = 4096
    S_ST1 = 5120
    S_ST2 = 6144
    S_PT = 7168

    def scr(o, n):
        return a_scr[:, o:o + n]

    FU = 0
    FW = 1792

    C_SG, C_SU, C_SD = 0, 32, 64
    C_MX = 80
    C_KG = 112
    C_RC = 114
    C_FIN = 116
    C_T1, C_T2 = 120, 121
    C_C = 124
    C_INV = 128
    C_F3 = 136
    C_SX = 144
    C_BIAS = 152
    C_MQ = 176
    C_KGA = 192
    C_RCA = 216

    R_SGX = 0
    R_SGD = 4
    R_LM = 8
    R_GM = 12
    R_SX = 16
    R_INV = 20
    R_SD = 24
    R_F2 = 28
    R_F3 = 32
    R_MQ = 40
    R_MK = 64
    R_BE = 88
    R_CC = 128
    R_TMP = 160
    R_EPS = 168

    RB_Z = 0

    k = B(nc)
    sSY = k.sem("sSY")
    sGP = k.sem("sGP")
    sSC = k.sem("sSC")
    sT = k.sem("sT")
    sV = k.sem("sV")
    sS = k.sem("sS")
    sG = k.sem("sG")
    sCC = k.sem("sCC")

    def dma(queue, out, in_, slow=False):
        sem = {"y": sSY, "g": sGP, "s": sSC}[queue]
        return k.inc(queue, lambda e, o=out, i=in_, sl=slow: e.dma_start(
            out=o, in_=i, allow_slow_non_contiguous=sl), sem, 16)

    def mm(out, lhsT, rhs, start, stop, sem=None):
        def go(e, o=out, l=lhsT, r=rhs, st=start, sp=stop):
            return e.matmul(o, l, r, start=st, stop=sp)
        if sem:
            return k.inc("t", go, sem)
        k.run("t", go)

    def vo(e, mk, sem=None):
        if sem:
            return k.inc(e, mk, sem)
        k.run(e, mk)

    dma("y", a_id[:, :], d_ident[:, :])
    dma("y", a_crw[:, :], d_crows[:, :])
    dma("g", a_b.ap().rearrange("p (t c) -> p t c", t=NTC),
        d_x.ap().rearrange("(t p) c -> p t c", p=128))
    dma("g", a_rot[:, :], d_crot[:, :])
    dma("y", a_bq[:, :], d_bqkv.ap().rearrange("(f p) -> p f", p=128), slow=True)
    dma("y", a_bo[:, :], d_bout.ap().rearrange("(f p) -> p f", p=128), slow=True)
    dma("y", a_f32[0:1, 0:1024], d_bqkv.ap()[2048:3072].rearrange("(a f) -> a f", a=1))
    dma("y", a_row[0:1, R_SGX:R_SGX + 1], d_clipx.ap().rearrange("(b a) -> b a", b=1))
    dma("y", a_row[0:1, R_SGD:R_SGD + 1], d_clipd.ap().rearrange("(b a) -> b a", b=1))

    k.wait("v", sSY)
    k.wait("v", sGP)
    vo("v", lambda e: e.tensor_copy(a_idb[:, :], a_id[:, :]))
    vo("v", lambda e: e.tensor_copy(a_crwb[:, :], a_crw[:, :]))
    vo("v", lambda e: e.memset(a_col[:, C_C:C_C + 1], MAGIC))
    vo("v", lambda e: e.memset(a_row[0:1, R_EPS:R_EPS + 1], EPS))
    vo("v", lambda e: e.tensor_copy(a_rowb[0:1, 0:1024], a_f32[0:1, 0:1024]),
       sem=sV)
    k.wait("s", sSY)
    vo("s", lambda e: e.activation(a_row[0:1, R_SGX:R_SGX + 1],
                                   a_row[0:1, R_SGX:R_SGX + 1], AF.Sigmoid))
    vo("s", lambda e: e.activation(a_row[0:1, R_SGD:R_SGD + 1],
                                   a_row[0:1, R_SGD:R_SGD + 1], AF.Sigmoid), sem=sS)

    k.wait("t", sV)
    k.wait("t", sGP)
    for sl in range(2):
        mm(pA[:, sl * 512:(sl + 1) * 512], a_crwb[0:1, 0:128],
           a_rowb[0:1, sl * 512:(sl + 1) * 512], start=True, stop=True,
           sem=sT if sl == 1 else None)
    k.wait("v", sT)
    BVB = a_rowb[:, 0:1024]
    vo("v", lambda e: e.tensor_copy(BVB, pA[:, 0:1024]), sem=sV)

    for tc in range(NTC):
        for half in range(2):
            pp = (pA if half == 0 else pB)[:, 1024 + (tc % 2) * 512:
                                           1024 + (tc % 2) * 512 + 512]
            for j in range(4):
                g = half * 4 + j
                mm(pp[:, j * 128:(j + 1) * 128],
                   a_b[:, tc * C + g * 128:tc * C + (g + 1) * 128],
                   a_idb[:, :], start=True, stop=True,
                   sem=sT if j == 3 else None)
        k.wait("s", sT)
        for half in range(2):
            pp = (pA if half == 0 else pB)[:, 1024 + (tc % 2) * 512:
                                           1024 + (tc % 2) * 512 + 512]
            def ev(e, tc=tc, half=half, pp=pp):
                outs = a_x.ap().rearrange("p (g t) -> p g t", g=NCC)
                return e.copy(outs[:, half * 4:(half + 1) * 4,
                                   tc * 128:(tc + 1) * 128], pp)
            vo("s", ev, sem=sS)
        if tc >= 1:
            k.wait("t", sS)
    k.barrier()

    def ln_norm(codes_max):
        onescol = a_crwb[:, 300:301]
        for g in range(NCC):
            sq = scr(S_ST0, T) if g % 2 == 0 else scr(S_BC1, T)
            if g >= 2:
                k.wait("v", sT)
            vo("v", lambda e, g=g, sq=sq: e.tensor_tensor(sq, xfm(g), xfm(g),
                                                          ALU.mult), sem=sV)
            k.wait("t", sV)
            for sl in range(4):
                mm(pA[0:1, sl * 512:(sl + 1) * 512], onescol,
                   xfm(g)[:, sl * 512:(sl + 1) * 512], start=(g == 0),
                   stop=(g == NCC - 1))
                mm(pB[0:1, sl * 512:(sl + 1) * 512], onescol,
                   sq[:, sl * 512:(sl + 1) * 512], start=(g == 0),
                   stop=(g == NCC - 1),
                   sem=sT if sl == 3 else None)
        MSQ = a_f32[0:1, 0:T]
        PSUMR = pA[0:1, :]
        PSQR = pB[0:1, :]
        RBM = a_scr[0:1, S_ST0:S_ST0 + T]
        RBS = a_scr[0:1, S_BC1:S_BC1 + T]
        k.wait("s", sT)
        vo("s", lambda e: e.activation(RBM, PSUMR, AF.Identity, scale=1.0 / C))
        vo("s", lambda e: e.activation(MSQ, PSUMR, AF.Square, scale=1.0 / C))
        vo("s", lambda e: e.activation(PSQR, PSQR, AF.Identity, scale=1.0 / C),
           sem=sS)
        k.wait("v", sS)
        vo("v", lambda e: e.tensor_tensor(PSQR, PSQR, MSQ, ALU.subtract), sem=sV)
        k.wait("s", sV)
        vo("s", lambda e: e.activation(MSQ, PSQR, AF.Ln, bias=a_row[0:1, R_EPS:R_EPS + 1]))
        vo("s", lambda e: e.activation(MSQ, MSQ, AF.Exp, scale=-0.5))
        vo("s", lambda e: e.activation(RBS, MSQ, AF.Identity), sem=sS)
        k.wait("v", sS)
        vo("v", lambda e: e.nop(), sem=sV)
        k.wait("t", sV)
        orow = a_crwb[0:1, 0:128]
        for sl in range(4):
            mm(pA[:, sl * 512:(sl + 1) * 512], orow, RBM[:, sl * 512:(sl + 1) * 512],
               start=True, stop=True)
            mm(pB[:, sl * 512:(sl + 1) * 512], orow, RBS[:, sl * 512:(sl + 1) * 512],
               start=True, stop=True, sem=sT if sl == 3 else None)
        k.wait("v", sT)
        MB = scr(S_BC0, T)
        RSB = scr(S_BC1, T)
        vo("v", lambda e: e.tensor_copy(MB, pA[:, :]))
        vo("v", lambda e: e.tensor_copy(RSB, pB[:, :]), sem=sV)
        for g in range(NCC):
            vo("v", lambda e, g=g: e.tensor_tensor(bfm(g), xfm(g), MB, ALU.subtract))
            vo("v", lambda e, g=g: e.tensor_tensor(bfm(g), bfm(g), RSB, ALU.mult),
               sem=sV)
            if codes_max:
                vo("v", lambda e, g=g: e.tensor_reduce(
                    a_col[:, C_MX + g:C_MX + g + 1], bfm(g), AXX, ALU.max,
                    apply_absolute_value=True), sem=sV)
        if codes_max:
            k.run("v", lambda e: e.drain())

    def global_max(ncols, scale_cols=None, sfact_row=None):
        if scale_cols is not None:
            for i in range(ncols):
                vo("v", lambda e, i=i: e.tensor_tensor(
                    a_col[:, C_MX + i:C_MX + i + 1],
                    a_col[:, C_MX + i:C_MX + i + 1],
                    a_col[:, scale_cols + i:scale_cols + i + 1], ALU.mult))
        step = 1
        while step < ncols:
            for i in range(0, ncols, 2 * step):
                if i + step < ncols:
                    vo("v", lambda e, i=i, s=step: e.tensor_tensor(
                        a_col[:, C_MX + i:C_MX + i + 1],
                        a_col[:, C_MX + i:C_MX + i + 1],
                        a_col[:, C_MX + i + s:C_MX + i + s + 1], ALU.max))
            step *= 2
        vo("v", lambda e: e.drain(), sem=sV)
        k.wait("t", sV)
        k.inc("t", lambda e: e.transpose(pA[0:1, 0:128], a_col[:, C_MX:C_MX + 1],
                                         a_id[:, :]), sT)
        k.wait("v", sT)
        vo("v", lambda e: e.tensor_reduce(a_row[0:1, R_LM:R_LM + 1],
                                          pA[0:1, 0:128], AXX, ALU.max), sem=sV)
        k.run("v", lambda e: e.drain())
        if sfact_row is not None:
            vo("v", lambda e: e.tensor_tensor(
                a_row[0:1, R_LM:R_LM + 1], a_row[0:1, R_LM:R_LM + 1],
                a_row[0:1, sfact_row:sfact_row + 1], ALU.mult), sem=sV)
            k.run("v", lambda e: e.drain())
        vo("v", lambda e: e.memset(a_row[0:1, R_CC:R_CC + 8], 0.0))
        vo("v", lambda e: e.tensor_copy(a_row[0:1, R_CC:R_CC + 1],
                                        a_row[0:1, R_LM:R_LM + 1]), sem=sV)
        k.wait("y", sV)
        k.wait("y", sCC)
        dma("y", d_ccin[:, :], a_row[0:1, R_CC:R_CC + 8])
        k.wait("g", sSY)
        k.inc("g", lambda e: e.collective_compute(
            "AllReduce", ALU.max, replica_groups=[list(range(8))],
            ins=[d_ccin[:, :]], outs=[d_ccout[:, :]]), sCC)
        k.wait("g", sCC)
        dma("g", a_row[0:1, R_GM:R_GM + 1], d_ccout[0:1, 0:1])
        k.wait("v", sGP)

    def bcast_scalar(rowoff, coloff):
        k.wait("t", sV)
        mm(pB[:, 0:1], a_crw[0:1, 0:128], a_row[0:1, rowoff:rowoff + 1],
           start=True, stop=True, sem=sT)
        k.wait("v", sT)
        vo("v", lambda e, c=coloff: e.tensor_copy(a_col[:, c:c + 1], pB[:, 0:1]),
           sem=sV)

    def scalar_chain(dst_sd, dst_inv, sig_row):
        vo("v", lambda e, d=dst_sd: e.tensor_tensor(
            a_row[0:1, d:d + 1], a_row[0:1, R_GM:R_GM + 1],
            a_row[0:1, sig_row:sig_row + 1], ALU.mult), sem=sV)
        k.run("v", lambda e: e.drain())
        vo("v", lambda e, d=dst_sd, i=dst_inv: e.reciprocal(
            a_row[0:1, i:i + 1], a_row[0:1, d:d + 1]))
        k.run("v", lambda e: e.drain())
        vo("v", lambda e, i=dst_inv: e.tensor_scalar(
            a_row[0:1, i:i + 1], a_row[0:1, i:i + 1], 127.0, None, ALU.mult))
        vo("v", lambda e, d=dst_sd: e.tensor_scalar(
            a_row[0:1, d:d + 1], a_row[0:1, d:d + 1], 1.0 / 127.0, None,
            ALU.mult), sem=sV)
        k.run("v", lambda e: e.drain())

    def quant_chunks(nch, src_fn, scale_col, out_fn):
        vmk = {}
        k.wait("s", sV)
        for n in range(2 * nch):
            i, hh = n // 2, n % 2
            sc = scale_col(i) if callable(scale_col) else scale_col
            u = a_f32[:, FU + (n % 2) * 1024:FU + (n % 2) * 1024 + 1024]
            if n >= 2:
                k.wait("s", sV, vmk[n - 2])
            vo("s", lambda e, i=i, hh=hh, sc=sc, u=u: e.activation(
                u, src_fn(i)[:, hh * 1024:(hh + 1) * 1024], AF.Identity,
                bias=a_col[:, C_C:C_C + 1], scale=sc), sem=sS)
            sm = k.mark(sS)
            k.wait("v", sS, sm)
            vo("v", lambda e, u=u: e.tensor_scalar(u, u, -MAGIC, 127.0,
                                                   ALU.add, ALU.min))
            vo("v", lambda e, i=i, hh=hh, u=u: e.tensor_scalar(
                out_fn(i)[:, hh * 1024:(hh + 1) * 1024], u, -128.0,
                None, ALU.max), sem=sV)
            vmk[n] = k.mark(sV)
        k.wait("v", sV)

    def wquant_rows(wf, scol_ap, wint, width):
        rmax = a_col[:, C_T1:C_T1 + 1]
        rinv = a_col[:, C_T2:C_T2 + 1]
        vo("v", lambda e: e.tensor_reduce(rmax, wf, AXX, ALU.max,
                                          apply_absolute_value=True), sem=sV)
        k.run("v", lambda e: e.drain())
        vo("v", lambda e: e.tensor_scalar(scol_ap, rmax, 1.0 / 127.0, None,
                                          ALU.mult))
        vo("v", lambda e: e.reciprocal(rinv, rmax))
        k.run("v", lambda e: e.drain())
        vo("v", lambda e: e.tensor_scalar(rinv, rinv, 127.0, None, ALU.mult))
        k.run("v", lambda e: e.drain())
        for part in range((width + 1023) // 1024):
            lo = part * 1024
            hi = min(width, lo + 1024)
            u = a_f32[:, FU:FU + (hi - lo)]
            vo("v", lambda e, lo=lo, hi=hi, u=u, wf=wf: e.tensor_scalar(
                u, wf[:, lo:hi], rinv, MAGIC, ALU.mult, ALU.add))
            vo("v", lambda e, lo=lo, hi=hi, u=u, wint=wint: e.tensor_scalar(
                wint[:, lo:hi], u, -MAGIC, None, ALU.add), sem=sV)

    def transpose_tiles(wint, ntile, dststage, pbank):
        k.wait("t", sV)
        k.wait("t", sS)
        for i in range(ntile):
            pp = pbank[:, (i % 4) * 128:(i % 4) * 128 + 128]
            mm(pp, wint[:, i * 128:(i + 1) * 128], a_idb[:, :], start=True,
               stop=True, sem=sT if (i % 4 == 3 or i == ntile - 1) else None)
            if i % 4 == 3 or i == ntile - 1:
                k.wait("s", sT)
                base = (i // 4) * 4
                ngrp = i - base + 1
                vo("s", lambda e, base=base, ngrp=ngrp, d=dststage, p=pbank: e.copy(
                    d[:, base * 128:(base + ngrp) * 128], p[:, 0:ngrp * 128]),
                   sem=sS)
                if i != ntile - 1:
                    k.wait("t", sS)

    def emit_ffn(save, dbg_tag=None):
        ln_norm(codes_max=True)
        global_max(NCC)
        scalar_chain(R_SX, R_INV, R_SGX)
        bcast_scalar(R_INV, C_INV)
        bcast_scalar(R_SX, C_SX)
        quant_chunks(NCC, bfm, a_col[:, C_INV:C_INV + 1], bfm)
        if not save:
            vo("v", lambda e: e.tensor_scalar(
                a_col[:, C_KGA:C_KGA + NIC], a_col[:, C_SG:C_SG + NIC],
                a_col[:, C_SX:C_SX + 1], None, ALU.mult), sem=sV)
        k.barrier()
        if debug and dbg_tag == 1:
            k.wait("y", sV)
            dma("y", dbg["sc"][0:1, 0:16], a_row[0:1, 0:16])
            dma("y", dbg["sc"][0:1, 16:32], a_row[0:1, 16:32])
        st_ready = [None, None]
        kgm = [0, 0]
        stm = [0, 0]
        wf_free = [0]
        wint_free = [0]
        mm_done = [0, 0]
        ev_done = [(0, 0)]

        def gu_stage(i):
            par = i % 2
            gst = scr(S_ST0, 1024) if par == 0 else scr(S_BC0, 1024)
            ust = scr(S_ST1, 1024) if par == 0 else scr(S_BC0 + 1024, 1024)
            return gst, ust

        def gu_prep_a(i):
            par = i % 2
            gst, ust = gu_stage(i)
            if save:
                k.wait("y", sV, wf_free[0])
                wfg = a_f32[:, FW:FW + 1024]
                wfu = a_f32[:, FW + 1024:FW + 2048]
                dma("y", wfg, d_wg[i * 128:(i + 1) * 128, :])
                dma("y", wfu, d_wu[i * 128:(i + 1) * 128, :])
                ldm = k.mark(sSY)
                k.wait("v", sSY, ldm)
                k.wait("v", sT, wint_free[0])
                wquant_rows(wfg, a_col[:, C_SG + i:C_SG + i + 1],
                            scr(S_ST2, 1024), 1024)
                wquant_rows(wfu, a_col[:, C_SU + i:C_SU + i + 1],
                            scr(S_BC1 + 1024, 1024), 1024)
                wf_free[0] = k.mark(sV)
                vo("v", lambda e, i=i, par=par: e.tensor_tensor(
                    a_col[:, C_KG + par:C_KG + par + 1],
                    a_col[:, C_SG + i:C_SG + i + 1],
                    a_col[:, C_SX:C_SX + 1], ALU.mult), sem=sV)
                kgm[par] = k.mark(sV)
            else:
                k.wait("y", sT, mm_done[par])
                dma("y", gst, d_wgt[i, :, :])
                dma("y", ust, d_wut[i, :, :])
                st_ready[par] = ("y", k.mark(sSY))

        def gu_prep_b(i):
            if not save:
                return
            par = i % 2
            gst, ust = gu_stage(i)
            ss, vv = ev_done[0]
            k.wait("t", sS, ss)
            k.wait("t", sV, vv)
            k.wait("t", sT, mm_done[par])
            k.wait("s", sSY, stm[par])
            transpose_tiles(scr(S_ST2, 1024), 8, gst, pA[:, 0:512])
            transpose_tiles(scr(S_BC1 + 1024, 1024), 8, ust, pA[:, 512:1024])
            wint_free[0] = k.mark(sT)
            evm = k.mark(sS)
            st_ready[par] = ("s", evm)
            k.wait("y", sS, evm)
            dma("y", d_wgt[i, :, :], gst)
            dma("y", d_wut[i, :, :], ust)
            stm[par] = k.mark(sSY)

        gu_prep_a(0)
        gu_prep_b(0)
        for i in range(NIC):
            par = i % 2
            gst, ust = gu_stage(i)
            if i + 1 < NIC:
                gu_prep_a(i + 1)
            kind, mk = st_ready[par]
            k.wait("t", sSY if kind == "y" else sS, mk)
            for th in range(2):
                pg = (pA if th == 0 else pB)[:, 0:1024]
                pu = (pA if th == 0 else pB)[:, 1024:2048]
                ssm, vvm = ev_done[0]
                k.wait("t", sS, ssm)
                k.wait("t", sV, vvm)
                for h in range(NCC):
                    for sl in range(2):
                        nsl = th * 1024 + sl * 512
                        mm(pg[:, sl * 512:(sl + 1) * 512],
                           gst[:, h * 128:(h + 1) * 128],
                           bfm(h)[:, nsl:nsl + 512], start=(h == 0),
                           stop=(h == NCC - 1))
                        mm(pu[:, sl * 512:(sl + 1) * 512],
                           ust[:, h * 128:(h + 1) * 128],
                           bfm(h)[:, nsl:nsl + 512], start=(h == 0),
                           stop=(h == NCC - 1),
                           sem=sT if (h == NCC - 1 and sl == 1) else None)
                mmk = k.mark(sT)
                k.wait("s", sT, mmk)
                sil = scr(S_PT, 1024)
                kgap = (a_col[:, C_KG + par:C_KG + par + 1] if save
                        else a_col[:, C_KGA + i:C_KGA + i + 1])
                k.wait("s", sV, max(kgm[par], ev_done[0][1]))
                vo("s", lambda e, pg=pg, sil=sil, kgap=kgap: e.activation(
                    sil, pg, AF.Silu, scale=kgap), sem=sS)
                silm = k.mark(sS)
                k.wait("v", sS, silm)
                vo("v", lambda e, i=i, th=th, pu=pu, sil=sil: e.scalar_tensor_tensor(
                    tcode(i)[:, th * 1024:(th + 1) * 1024], pu, 1.0, sil,
                    ALU.mult, ALU.mult), sem=sV)
                ev_done[0] = (k.mark(sS), k.mark(sV))
            mm_done[par] = k.mark(sT)
            if i + 1 < NIC:
                gu_prep_b(i + 1)
            vo("v", lambda e, i=i: e.tensor_reduce(
                a_col[:, C_MX + i:C_MX + i + 1], tcode(i), AXX, ALU.max,
                apply_absolute_value=True), sem=sV)
        k.wait("v", sV)
        k.barrier()
        global_max(NIC, scale_cols=C_SU, sfact_row=R_SX)
        scalar_chain(R_SD, R_TMP, R_SGD)
        vo("v", lambda e: e.tensor_tensor(a_row[0:1, R_F2:R_F2 + 1],
                                          a_row[0:1, R_SX:R_SX + 1],
                                          a_row[0:1, R_TMP:R_TMP + 1], ALU.mult))
        vo("v", lambda e: e.tensor_scalar(a_row[0:1, R_F3:R_F3 + 1],
                                          a_row[0:1, R_SD:R_SD + 1], 0.5, None,
                                          ALU.mult), sem=sV)
        k.wait("v", sV)
        bcast_scalar(R_F2, C_INV)
        bcast_scalar(R_F3, C_F3)
        k.barrier()
        if debug and dbg_tag == 1:
            k.wait("y", sV)
            dma("y", dbg["sc"][0:1, 32:64], a_row[0:1, 0:32])
        vo("v", lambda e: e.tensor_scalar(
            a_col[:, C_RCA:C_RCA + NIC], a_col[:, C_SU:C_SU + NIC],
            a_col[:, C_INV:C_INV + 1], None, ALU.mult), sem=sV)
        k.wait("v", sV)
        quant_chunks(NIC, tcode, lambda i: a_col[:, C_RCA + i:C_RCA + i + 1], tcode)
        k.barrier()
        dn_ready = [None, None]
        dn_mm_done = [0, 0]
        dn_wf_free = [0]
        dn_wint_free = [0]
        dn_ev_done = [0]

        def dn_stage(h):
            return scr(S_ST0, II) if h % 2 == 0 else scr(S_BC0, II)

        def dn_prep_a(h):
            par = h % 2
            dstg = dn_stage(h)
            if save:
                k.wait("y", sV, dn_wf_free[0])
                wfh = a_f32[:, 0:II]
                dma("y", wfh, d_wd[h * 128:(h + 1) * 128, :])
                ldm = k.mark(sSY)
                k.wait("v", sSY, ldm)
                k.wait("v", sT, dn_wint_free[0])
                rmax = a_col[:, C_T1:C_T1 + 1]
                rinv = a_col[:, C_T2:C_T2 + 1]
                vo("v", lambda e, wfh=wfh: e.tensor_reduce(
                    rmax, wfh, AXX, ALU.max, apply_absolute_value=True), sem=sV)
                k.run("v", lambda e: e.drain())
                vo("v", lambda e, h=h: e.tensor_scalar(
                    a_col[:, C_SD + h:C_SD + h + 1], rmax, 1.0 / 127.0, None,
                    ALU.mult))
                vo("v", lambda e: e.reciprocal(rinv, rmax))
                k.run("v", lambda e: e.drain())
                vo("v", lambda e: e.tensor_scalar(rinv, rinv, 127.0, None,
                                                  ALU.mult))
                k.run("v", lambda e: e.drain())
                wint = scr(S_PT, II)
                u2 = a_f32[:, II:II + 1024]
                for part in range(3):
                    lo = part * 1024
                    hi = min(II, lo + 1024)
                    vo("v", lambda e, lo=lo, hi=hi, u2=u2, wfh=wfh: e.tensor_scalar(
                        u2[:, 0:hi - lo], wfh[:, lo:hi], rinv, MAGIC, ALU.mult,
                        ALU.add))
                    vo("v", lambda e, lo=lo, hi=hi, u2=u2, wint=wint: e.tensor_scalar(
                        wint[:, lo:hi], u2[:, 0:hi - lo], -MAGIC, None, ALU.add),
                        sem=sV)
                dn_wf_free[0] = k.mark(sV)
                vo("v", lambda e, h=h, par=par: e.tensor_tensor(
                    a_col[:, C_FIN + par:C_FIN + par + 1],
                    a_col[:, C_SD + h:C_SD + h + 1],
                    a_col[:, C_F3:C_F3 + 1], ALU.mult), sem=sV)
            else:
                k.wait("y", sT, dn_mm_done[par])
                dma("y", dstg, d_wdt[h, :, :])
                dn_ready[par] = ("y", k.mark(sSY))
                vo("v", lambda e, h=h, par=par: e.tensor_tensor(
                    a_col[:, C_FIN + par:C_FIN + par + 1],
                    a_col[:, C_SD + h:C_SD + h + 1],
                    a_col[:, C_F3:C_F3 + 1], ALU.mult), sem=sV)

        def dn_prep_b(h):
            if not save:
                return
            par = h % 2
            dstg = dn_stage(h)
            pbank = (pA if par == 0 else pB)[:, 1024:1536]
            k.wait("t", sT, dn_mm_done[par])
            k.wait("t", sV, dn_ev_done[0])
            k.wait("s", sSY)
            transpose_tiles(scr(S_PT, II), NIC, dstg, pbank)
            dn_wint_free[0] = k.mark(sT)
            evm = k.mark(sS)
            dn_ready[par] = ("s", evm)
            k.wait("y", sS, evm)
            dma("y", d_wdt[h, :, :], dstg)

        dn_prep_a(0)
        dn_prep_b(0)
        for h in range(NCC):
            par = h % 2
            dstg = dn_stage(h)
            pdn = pA if par == 0 else pB
            if h + 1 < NCC:
                dn_prep_a(h + 1)
            kind, mk = dn_ready[par]
            k.wait("t", sSY if kind == "y" else sS, mk)
            k.wait("t", sV, dn_ev_done[0])
            for i in range(NIC):
                for sl in range(4):
                    mm(pdn[:, sl * 512:(sl + 1) * 512],
                       dstg[:, i * 128:(i + 1) * 128],
                       tcode(i)[:, sl * 512:(sl + 1) * 512], start=(i == 0),
                       stop=(i == NIC - 1),
                       sem=sT if (i == NIC - 1 and sl == 3) else None)
            dn_mm_done[par] = k.mark(sT)
            k.wait("v", sT, dn_mm_done[par])
            k.wait("v", sV)
            vo("v", lambda e, h=h, par=par, pdn=pdn: e.scalar_tensor_tensor(
                xfm(h), pdn[:, :], a_col[:, C_FIN + par:C_FIN + par + 1],
                xfm(h), ALU.mult, ALU.add), sem=sV)
            dn_ev_done[0] = k.mark(sV)
            if h + 1 < NCC:
                dn_prep_b(h + 1)
        k.barrier()

    def emit_attn():
        ln_norm(codes_max=False)
        k.barrier()
        qk_mm = {-1: 0, 0: 0, 1: 0}
        qk_ld = {}
        qk_ev = {-1: 0, 0: 0, 1: 0}

        def qk_load(f):
            wt = scr(S_ST0 + (f % 2) * 1024, 1024)
            k.wait("g", sT, qk_mm[(f % 2)])
            dma("g", wt.rearrange("p (c j) -> p c j", c=8),
                d_wqkv.ap()[:, f * 128:(f + 1) * 128]
                .rearrange("(c p) j -> p c j", p=128))
            qk_ld[f] = k.mark(sGP)
            return wt

        wts = {0: qk_load(0)}
        for f in range(16):
            wt = wts.pop(f)
            if f + 1 < 16:
                wts[f + 1] = qk_load(f + 1)
            k.wait("t", sGP, qk_ld[f])
            k.wait("t", sS, qk_ev[f % 2])
            pq = pA if f % 2 == 0 else pB
            for c in range(NCC):
                for sl in range(4):
                    mm(pq[:, sl * 512:(sl + 1) * 512], wt[:, c * 128:(c + 1) * 128],
                       bfm(c)[:, sl * 512:(sl + 1) * 512], start=(c == 0),
                       stop=(c == NCC - 1),
                       sem=sT if (c == NCC - 1 and sl == 3) else None)
            qk_mm[f % 2] = k.mark(sT)
            k.wait("s", sT, qk_mm[f % 2])
            dst = qfm(f) if f < 8 else kfm(f - 8)
            vo("s", lambda e, dst=dst, pq=pq, f=f: e.activation(
                dst, pq[:, :], AF.Identity, bias=a_bq[:, f:f + 1], scale=1.0),
               sem=sS)
            qk_ev[f % 2] = k.mark(sS)
        k.barrier()
        vo("v", lambda e: e.memset(a_big[:, 16 * T:16 * T + NTC * NHEAD * 65],
                                   1.0), sem=sV)
        WV = a_scr[:, S_BC1:S_BC1 + 8192]
        dma("g", WV.rearrange("p (c j) -> p c j", c=8),
            d_wqkv.ap()[:, 2048:3072].rearrange("(c p) j -> p c j", p=128))
        k.wait("t", sGP)
        k.wait("t", sV)
        for tc in range(NTC):
            pv = (pA if tc % 2 == 0 else pB)[:, 0:1024]
            for c in range(NCC):
                for sl in range(2):
                    mm(pv[:, sl * 512:(sl + 1) * 512],
                       bfm(c)[:, tc * 128:(tc + 1) * 128],
                       WV[:, c * 1024 + sl * 512:c * 1024 + (sl + 1) * 512],
                       start=(c == 0), stop=(c == NCC - 1),
                       sem=sT if (c == NCC - 1 and sl == 1) else None)
            k.wait("v", sT)
            def vev(e, tc=tc, pv=pv):
                outp = vaug(tc).rearrange("p (h j) -> p h j", j=65)[:, :, 0:64]
                return e.scalar_tensor_tensor(
                    outp, pv, 1.0, BVB.rearrange("p (h j) -> p h j", j=64),
                    ALU.mult, ALU.add)
            vo("v", vev, sem=sV)
            k.wait("t", sV)
        k.barrier()
        A_COS = scr(S_PT, T)
        A_SIN = scr(S_PT + T, T)
        dma("g", A_COS, d_cos[:, :])
        dma("g", A_SIN, d_sin[:, :])
        k.wait("v", sGP)
        rp_mm = {0: 0, 1: 0}
        rp_ev = {0: 0, 1: 0}
        rp_v = {0: 0, 1: 0}
        for c in range(16):
            src = qfm(c) if c < 8 else kfm(c - 8)
            prm = scr(S_ST0, T) if c % 2 == 0 else scr(S_ST2, T)
            t1 = a_scr[:, S_BC0:S_BC0 + T] if c % 2 == 0 else a_scr[:, S_BC1:S_BC1 + T]
            pr = (pA if c % 2 == 0 else pB)
            k.wait("t", sS, rp_ev[c % 2])
            k.wait("t", sV, rp_v[c % 2])
            for sl in range(4):
                mm(pr[:, sl * 512:(sl + 1) * 512], a_rot[:, :],
                   src[:, sl * 512:(sl + 1) * 512], start=True, stop=True,
                   sem=sT if sl == 3 else None)
            rp_mm[c % 2] = k.mark(sT)
            k.wait("s", sT, rp_mm[c % 2])
            k.wait("s", sV, rp_v[c % 2])
            vo("s", lambda e, prm=prm, pr=pr: e.copy(prm, pr[:, :]), sem=sS)
            evm = k.mark(sS)
            rp_ev[c % 2] = evm
            k.wait("v", sS, evm)
            vo("v", lambda e, src=src, t1=t1: e.tensor_tensor(
                t1, src, A_COS, ALU.mult))
            vo("v", lambda e, prm=prm: e.tensor_tensor(prm, prm, A_SIN,
                                                       ALU.mult))
            vo("v", lambda e, src=src, t1=t1, prm=prm: e.tensor_tensor(
                src, t1, prm, ALU.add))
            vo("v", lambda e, c=c: e.tensor_reduce(
                a_col[:, C_MQ + c:C_MQ + c + 1], src, AXX, ALU.max,
                apply_absolute_value=True), sem=sV)
            rp_v[c % 2] = k.mark(sV)
        k.wait("v", sV)
        k.barrier()
        k.inc("t", lambda e: e.transpose(pA[0:16, 0:128],
                                         a_col[:, C_MQ:C_MQ + 16], a_id[:, :]), sT)
        k.wait("v", sT)
        r1c = a_f32[0:16, 3072:3073]
        r2c = a_f32[0:16, 3073:3074]
        vo("v", lambda e: e.tensor_reduce(r1c, pA[0:16, 0:64], AXX, ALU.max))
        vo("v", lambda e: e.tensor_reduce(r2c, pA[0:16, 64:128], AXX, ALU.max),
           sem=sV)
        k.run("v", lambda e: e.drain())
        k.wait("t", sV)
        k.inc("t", lambda e: e.transpose(pB[0:1, 0:16], r1c, a_id[0:16, 0:16]), sT)
        k.inc("t", lambda e: e.transpose(pB[0:1, 16:32], r2c, a_id[0:16, 0:16]), sT)
        k.wait("v", sT)
        r1row = a_f32[0:1, 3074:3090]
        r2row = a_f32[0:1, 3090:3106]
        vo("v", lambda e: e.tensor_copy(r1row, pB[0:1, 0:16]))
        vo("v", lambda e: e.tensor_copy(r2row, pB[0:1, 16:32]), sem=sV)
        BE = a_row[0:1, R_BE:R_BE + 16]
        bev = BE.rearrange("p (c t) -> p c t", t=2)

        def r3(ap):
            return ap.rearrange("p (c o) -> p c o", o=1)
        vo("v", lambda e: e.tensor_tensor(bev[:, :, 0:1], r3(r1row[:, 0:8]),
                                          r3(r1row[:, 8:16]), ALU.mult))
        vo("v", lambda e: e.tensor_tensor(bev[:, :, 1:2], r3(r2row[:, 0:8]),
                                          r3(r2row[:, 8:16]), ALU.mult), sem=sV)
        k.run("v", lambda e: e.drain())
        vo("v", lambda e: e.tensor_scalar(BE, BE, -8.0, 40.0, ALU.mult, ALU.add),
           sem=sV)
        k.run("v", lambda e: e.drain())
        k.wait("t", sV)
        for h in range(NHEAD):
            mm(pB[:, h:h + 1], a_crw[0:1, 0:128],
               a_row[0:1, R_BE + h:R_BE + h + 1], start=True, stop=True,
               sem=sT if h == NHEAD - 1 else None)
        k.wait("v", sT)
        vo("v", lambda e: e.tensor_copy(a_col[:, C_BIAS:C_BIAS + 16],
                                        pB[:, 0:16]), sem=sV)
        k.barrier()
        phases = [(hp, qq) for hp in range(8) for qq in range(4)]
        norm2 = {-1: 0}

        def emit_evac2(ctx):
            hp, qq, par, avA, avB, hA, hB, avmk = ctx
            ZA = a_rowb[64:65, RB_Z + par * 1024:RB_Z + par * 1024 + 512]
            ZB = a_rowb[64:65, RB_Z + par * 1024 + 512:RB_Z + par * 1024 + 1024]
            ZAf = a_f32[64:65, par * 1024:par * 1024 + 512]
            ZBf = a_f32[64:65, par * 1024 + 512:par * 1024 + 1024]
            k.wait("s", sT, avmk)
            vo("s", lambda e: e.activation(ZAf, avA[64:65, :], AF.Ln))
            vo("s", lambda e: e.activation(ZA, ZAf, AF.Exp, scale=-1.0))
            vo("s", lambda e: e.activation(ZBf, avB[64:65, :], AF.Ln))
            vo("s", lambda e: e.activation(ZB, ZBf, AF.Exp, scale=-1.0), sem=sS)
            zm = k.mark(sS)
            oa = scr(S_ST0 + par * 1024, 512)
            ob = scr(S_ST0 + par * 1024 + 512, 512)
            k.wait("v", sSC)
            k.wait("v", sT, avmk)
            vo("v", lambda e, oa=oa, avA=avA: e.tensor_copy(oa[0:64, :],
                                                            avA[0:64, :]))
            vo("v", lambda e, ob=ob, avB=avB: e.tensor_copy(ob[0:64, :],
                                                            avB[0:64, :]), sem=sV)
            om = k.mark(sV)
            freed = pB[:, par * 1024:par * 1024 + 512]
            k.wait("t", sV, om)
            k.wait("t", sS, zm)
            mm(freed, a_crwb[64:65, 128:256], ZA, start=True, stop=True, sem=sT)
            m1 = k.mark(sT)
            k.wait("v", sT, m1)
            vo("v", lambda e, hp=hp, qq=qq, oa=oa, freed=freed: e.tensor_tensor(
                bfm(hp)[0:64, qq * 512:(qq + 1) * 512], oa[0:64, :],
                freed[0:64, :], ALU.mult), sem=sV)
            m2 = k.mark(sV)
            k.wait("t", sV, m2)
            mm(freed, a_crwb[64:65, 128:256], ZB, start=True, stop=True, sem=sT)
            m3 = k.mark(sT)
            k.wait("v", sT, m3)
            vo("v", lambda e, ob=ob, freed=freed: e.tensor_tensor(
                ob[0:64, :], ob[0:64, :], freed[0:64, :], ALU.mult), sem=sV)
            nm = k.mark(sV)
            k.wait("s", sV, nm)
            dma("s", bfm(hp)[64:128, qq * 512:(qq + 1) * 512], ob[0:64, :])
            return nm

        pend = [None]
        for p2 in range(16):
            ph = []
            for par in range(2):
                hp, qq = phases[2 * p2 + par]
                ph.append({
                    "hp": hp, "qq": qq, "par": par,
                    "avA": pB[0:65, par * 1024:par * 1024 + 512],
                    "avB": pB[0:65, par * 1024 + 512:par * 1024 + 1024],
                    "hA": 2 * hp, "hB": 2 * hp + 1,
                    "expmk": {-1: 0}, "scm": {}, "avmk": {-2: 0, -1: 0},
                })
            if pend[0] is not None:
                nm = 0
                for ctx in pend[0]:
                    nm = emit_evac2(ctx)
                norm2[p2 - 1] = nm
                pend[0] = None
            for kt in range(NTC + 1):
                for par in range(2):
                    f = ph[par]
                    base = par * 1024
                    if kt < NTC:
                        k.wait("t", sS, f["expmk"][kt - 1])
                        mm(pA[:, base:base + 512],
                           kfm(f["hp"])[0:64, kt * 128:(kt + 1) * 128],
                           qfm(f["hp"])[0:64, f["qq"] * 512:(f["qq"] + 1) * 512],
                           start=True, stop=True)
                        mm(pA[:, base + 512:base + 1024],
                           kfm(f["hp"])[64:128, kt * 128:(kt + 1) * 128],
                           qfm(f["hp"])[64:128, f["qq"] * 512:(f["qq"] + 1) * 512],
                           start=True, stop=True, sem=sT)
                        f["scm"][kt] = k.mark(sT)
                for par in range(2):
                    f = ph[par]
                    base = par * 1024
                    if kt < NTC:
                        k.wait("s", sT, f["scm"][kt])
                        k.wait("s", sT, f["avmk"][kt - 2])
                        slot = (kt % 2) * 2048 + par * 1024
                        ptA = scr(S_PT + slot, 512)
                        ptB = scr(S_PT + slot + 512, 512)
                        vo("s", lambda e, ptA=ptA, f=f, base=base: e.activation(
                            ptA, pA[:, base:base + 512], AF.Exp,
                            bias=a_col[:, C_BIAS + f["hA"]:C_BIAS + f["hA"] + 1],
                            scale=0.125))
                        f["expmk"][kt] = vo(
                            "s", lambda e, ptB=ptB, f=f, base=base: e.activation(
                                ptB, pA[:, base + 512:base + 1024], AF.Exp,
                                bias=a_col[:, C_BIAS + f["hB"]:C_BIAS + f["hB"] + 1],
                                scale=0.125), sem=sS)
                for par in range(2):
                    f = ph[par]
                    if kt >= 1:
                        ka = kt - 1
                        slot = (ka % 2) * 2048 + par * 1024
                        ptA = scr(S_PT + slot, 512)
                        ptB = scr(S_PT + slot + 512, 512)
                        k.wait("t", sS, f["expmk"][ka])
                        if ka == 0:
                            k.wait("t", sV, norm2[p2 - 1])
                        mm(f["avA"], vaug(ka)[:, f["hA"] * 65:f["hA"] * 65 + 65],
                           ptA, start=(ka == 0), stop=(ka == NTC - 1))
                        mm(f["avB"], vaug(ka)[:, f["hB"] * 65:f["hB"] * 65 + 65],
                           ptB, start=(ka == 0), stop=(ka == NTC - 1), sem=sT)
                        f["avmk"][ka] = k.mark(sT)
            pend[0] = [(ph[par]["hp"], ph[par]["qq"], par, ph[par]["avA"],
                        ph[par]["avB"], ph[par]["hA"], ph[par]["hB"],
                        ph[par]["avmk"][NTC - 1]) for par in range(2)]
        for ctx in pend[0]:
            emit_evac2(ctx)
        k.barrier()
        op_mm = {0: 0, 1: 0}
        op_ld = {}
        op_ev = {0: 0, 1: 0}

        def op_load(f):
            wt = scr(S_ST0 + (f % 2) * 1024, 1024)
            k.wait("g", sT, op_mm[f % 2])
            dma("g", wt.rearrange("p (c j) -> p c j", c=8),
                d_wout.ap()[:, f * 128:(f + 1) * 128]
                .rearrange("(c p) j -> p c j", p=128))
            op_ld[f] = k.mark(sGP)
            return wt

        wts2 = {0: op_load(0)}
        for f in range(NCC):
            wt = wts2.pop(f)
            if f + 1 < NCC:
                wts2[f + 1] = op_load(f + 1)
            k.wait("t", sGP, op_ld[f])
            k.wait("t", sV, op_ev[f % 2])
            po = pA if f % 2 == 0 else pB
            for c in range(NCC):
                for sl in range(4):
                    mm(po[:, sl * 512:(sl + 1) * 512], wt[:, c * 128:(c + 1) * 128],
                       bfm(c)[:, sl * 512:(sl + 1) * 512], start=(c == 0),
                       stop=(c == NCC - 1),
                       sem=sT if (c == NCC - 1 and sl == 3) else None)
            op_mm[f % 2] = k.mark(sT)
            k.wait("v", sT, op_mm[f % 2])
            vo("v", lambda e, f=f, po=po: e.scalar_tensor_tensor(
                xfm(f), po[:, :], a_bo[:, f:f + 1], xfm(f), ALU.add, ALU.add),
               sem=sV)
            op_ev[f % 2] = k.mark(sV)
        k.barrier()

    def emit_store_x(target):
        for tc in range(NTC):
            for half in range(2):
                pp = (pA if half == 0 else pB)[:, (tc % 2) * 512:
                                               (tc % 2) * 512 + 512]
                for j in range(4):
                    g = half * 4 + j
                    mm(pp[:, j * 128:(j + 1) * 128],
                       xfm(g)[:, tc * 128:(tc + 1) * 128], a_idb[:, :],
                       start=True, stop=True, sem=sT if j == 3 else None)
            k.wait("v", sT)
            stg = a_f32[:, FU:FU + 1024]
            vo("v", lambda e, tc=tc, stg=stg: e.tensor_copy(
                stg[:, 0:512], pA[:, (tc % 2) * 512:(tc % 2) * 512 + 512]))
            vo("v", lambda e, tc=tc, stg=stg: e.tensor_copy(
                stg[:, 512:1024], pB[:, (tc % 2) * 512:(tc % 2) * 512 + 512]),
               sem=sV)
            k.wait("y", sV)
            dma("y", target[tc * 128:(tc + 1) * 128, :], stg)
            k.wait("v", sSY)
            k.wait("t", sV)
        k.barrier()

    emit_ffn(save=True, dbg_tag=1)
    if debug:
        emit_store_x(dbg["x1"])
    emit_attn()
    if debug:
        emit_store_x(dbg["x2"])
    emit_ffn(save=False)
    emit_store_x(d_out)
    k.barrier()

    with nc.Block() as block:
        @block.tensor
        def _(e):
            for fn in k.q["t"]:
                fn(e)

        @block.vector
        def _(e):
            for fn in k.q["v"]:
                fn(e)

        @block.scalar
        def _(e):
            for fn in k.q["s"]:
                fn(e)

        @block.gpsimd
        def _(e):
            for fn in k.q["g"]:
                fn(e)

        @block.sync
        def _(e):
            for fn in k.q["y"]:
                fn(e)

    for g in reversed(guards):
        g.__exit__(None, None, None)
    for g in reversed(k.guards):
        g.__exit__(None, None, None)
    return nc


def _host_consts():
    ident = np.eye(128, dtype=np.float32)
    DH = 64
    inv = (1.0 / 10000.0 ** (np.arange(0, DH, 2, dtype=np.float64) / DH))
    tpos = np.arange(T, dtype=np.float64)[:, None]
    fr = tpos * inv[None, :]
    emb = np.concatenate([fr, fr], axis=1)
    cos = np.cos(emb).astype(np.float32).T
    sin = np.sin(emb).astype(np.float32).T
    cos_fm = np.concatenate([cos, cos], axis=0)
    sin_fm = np.concatenate([sin, sin], axis=0)
    R = np.zeros((128, 128), dtype=np.float32)
    for blk in (0, 64):
        for d in range(32):
            R[blk + d, blk + d + 32] = -1.0
            R[blk + d + 32, blk + d] = 1.0
    rotT = R.T.copy()
    crows = np.zeros((128, 384), dtype=np.float32)
    crows[0, 0:128] = 1.0
    crows[64, 0:128] = 1.0
    crows[64, 128:192] = 1.0
    crows[64, 320:384] = 1.0
    crows[:, 300] = 1.0
    return ident, cos_fm, sin_fm, crows, rotT


_NC_CACHE = {}


def kernel(L0_attn_x, L0_attn_W_qkv, L0_attn_b_qkv, L0_attn_W_out, L0_attn_b_out,
           L0_ffn_clip_x, L0_ffn_clip_down, L0_ffn_w_gate, L0_ffn_w_up,
           L0_ffn_w_down, _debug=False):
    key = bool(_debug)
    if key not in _NC_CACHE:
        _NC_CACHE[key] = build_nc(debug=_debug)
    nc = _NC_CACHE[key]
    ident, cos_fm, sin_fm, crows, rotT = _host_consts()
    x = np.ascontiguousarray(np.asarray(L0_attn_x, dtype=np.float32))
    base = {
        "wqkv": np.ascontiguousarray(np.asarray(L0_attn_W_qkv, np.float32)),
        "bqkv": np.ascontiguousarray(np.asarray(L0_attn_b_qkv, np.float32)),
        "wout": np.ascontiguousarray(np.asarray(L0_attn_W_out, np.float32)),
        "bout": np.ascontiguousarray(np.asarray(L0_attn_b_out, np.float32)),
        "clipx": np.ascontiguousarray(np.asarray(L0_ffn_clip_x, np.float32)),
        "clipd": np.ascontiguousarray(np.asarray(L0_ffn_clip_down, np.float32)),
        "wg": np.ascontiguousarray(np.asarray(L0_ffn_w_gate, np.float32)),
        "wu": np.ascontiguousarray(np.asarray(L0_ffn_w_up, np.float32)),
        "wd": np.ascontiguousarray(np.asarray(L0_ffn_w_down, np.float32)),
        "cident": ident, "ccos": cos_fm, "csin": sin_fm, "crows": crows,
        "crot": rotT,
    }
    in_maps = []
    for b in range(8):
        m = dict(base)
        m["x"] = np.ascontiguousarray(x[b])
        in_maps.append(m)
    res = run_bass_kernel_spmd(nc, in_maps, core_ids=list(range(8)))
    out = np.stack([res.results[b]["out"] for b in range(8)], axis=0)
    if _debug:
        extra = {kk: np.stack([res.results[b]["dbg_" + kk] for b in range(8)])
                 for kk in ("x1", "x2", "sc")}
        return out, extra
    return out


# version 07 (final)
# speedup vs baseline: 1.6562x; 1.0531x over previous
import numpy as np
import ml_dtypes
import concourse.bass as bass
import concourse.mybir as mybir
from concourse.bass_utils import run_bass_kernel_spmd

F32 = mybir.dt.float32
BF16 = mybir.dt.bfloat16
AF = mybir.ActivationFunctionType
ALU = mybir.AluOpType
AXX = mybir.AxisListType.X

MAGIC = float(np.float32(1.5 * 2 ** 23))
T = 2048
C = 1024
NCC = 8
NTC = 16
II = 2816
NIC = 22
NHEAD = 16
EPS = 1e-5

ENGS = ("t", "v", "s", "g", "y")


class B:

    def __init__(self, nc):
        self.nc = nc
        self.q = {e: [] for e in ENGS}
        self.sems = {}
        self.cnt = {}
        self.guards = []

    def sem(self, name):
        g = self.nc.semaphore(name)
        h = g.__enter__()
        self.guards.append(g)
        self.sems[name] = h
        self.cnt[name] = 0
        return name

    def emit(self, e, fn):
        self.q[e].append(fn)

    def inc(self, e, mk, sem, k=1):
        h = self.sems[sem]
        self.emit(e, lambda eng, mk=mk, h=h, k=k: mk(eng).then_inc(h, k))
        self.cnt[sem] += k
        return self.cnt[sem]

    def run(self, e, mk):
        self.emit(e, mk)

    def wait(self, e, sem, val=None):
        h = self.sems[sem]
        v = self.cnt[sem] if val is None else val
        if v:
            self.emit(e, lambda eng, h=h, v=v: eng.wait_ge(h, v))

    def mark(self, sem):
        return self.cnt[sem]

    def barrier(self):
        snap = {s: self.cnt[s] for s in self.sems}
        for e in ENGS:
            for s, v in snap.items():
                if v:
                    self.wait(e, s, v)


def build_nc(debug=False):
    nc = bass.Bass("TRN2", target_bir_lowering=False)

    d_x = nc.dram_tensor("x", [T, C], F32, kind="ExternalInput")
    d_wqkv = nc.dram_tensor("wqkv", [C, 3 * C], F32, kind="ExternalInput")
    d_bqkv = nc.dram_tensor("bqkv", [3 * C], F32, kind="ExternalInput")
    d_wout = nc.dram_tensor("wout", [C, C], F32, kind="ExternalInput")
    d_bout = nc.dram_tensor("bout", [C], F32, kind="ExternalInput")
    d_clipx = nc.dram_tensor("clipx", [1], F32, kind="ExternalInput")
    d_clipd = nc.dram_tensor("clipd", [1], F32, kind="ExternalInput")
    d_wg = nc.dram_tensor("wg", [II, C], F32, kind="ExternalInput")
    d_wu = nc.dram_tensor("wu", [II, C], F32, kind="ExternalInput")
    d_wd = nc.dram_tensor("wd", [C, II], F32, kind="ExternalInput")
    d_ident = nc.dram_tensor("cident", [128, 128], F32, kind="ExternalInput")
    d_cos = nc.dram_tensor("ccos", [128, T], F32, kind="ExternalInput")
    d_sin = nc.dram_tensor("csin", [128, T], F32, kind="ExternalInput")
    d_crows = nc.dram_tensor("crows", [128, 384], F32, kind="ExternalInput")
    d_crot = nc.dram_tensor("crot", [128, 128], F32, kind="ExternalInput")
    d_out = nc.dram_tensor("out", [T, C], F32, kind="ExternalOutput")

    d_wgt = nc.dram_tensor("wgt_s", [NIC, 128, C], BF16)
    d_wut = nc.dram_tensor("wut_s", [NIC, 128, C], BF16)
    d_wdt = nc.dram_tensor("wdt_s", [NCC, 128, II], BF16)
    d_ccin = nc.dram_tensor("ccin", [1, 8], F32)
    d_ccout = nc.dram_tensor("ccout", [1, 8], F32)

    dbg = {}
    if debug:
        dbg["x1"] = nc.dram_tensor("dbg_x1", [T, C], F32, kind="ExternalOutput")
        dbg["x2"] = nc.dram_tensor("dbg_x2", [T, C], F32, kind="ExternalOutput")
        dbg["sc"] = nc.dram_tensor("dbg_sc", [1, 64], F32, kind="ExternalOutput")

    guards = []

    def sb(name, shape, dt):
        g = nc.sbuf_tensor(name, shape, dt)
        t = g.__enter__()
        guards.append(g)
        return t

    def psu(name, shape):
        g = nc.psum_tensor(name, shape, F32)
        t = g.__enter__()
        guards.append(g)
        return t

    a_x = sb("a_x", [128, NCC * T], BF16)
    a_b = sb("a_b", [128, NCC * T], BF16)
    a_big = sb("a_big", [128, 49408], BF16)
    a_scr = sb("a_scr", [128, 11264], BF16)
    a_f32 = sb("a_f32", [128, 3840], F32)
    a_col = sb("a_col", [128, 256], F32)
    a_row = sb("a_row", [128, 256], F32)
    a_rowb = sb("a_rowb", [128, 2048], BF16)
    a_id = sb("a_id", [128, 128], F32)
    a_idb = sb("a_idb", [128, 128], BF16)
    a_crw = sb("a_crw", [128, 384], F32)
    a_crwb = sb("a_crwb", [128, 384], BF16)
    a_rot = sb("a_rot", [128, 128], BF16)
    a_bq = sb("a_bq", [128, 24], F32)
    a_bo = sb("a_bo", [128, 8], F32)
    pA = psu("pA", [128, 2048])
    pB = psu("pB", [128, 2048])

    def xfm(g):
        return a_x[:, g * T:(g + 1) * T]

    def bfm(g):
        return a_b[:, g * T:(g + 1) * T]

    def tcode(i):
        return a_big[:, i * T:(i + 1) * T]

    def qfm(c):
        return a_big[:, c * T:(c + 1) * T]

    def kfm(c):
        return a_big[:, (8 + c) * T:(9 + c) * T]

    def vaug(tc):
        o = 16 * T + tc * (NHEAD * 65)
        return a_big[:, o:o + NHEAD * 65]

    S_BC0 = 0
    S_BC1 = 2048
    S_ST0 = 4096
    S_ST1 = 5120
    S_ST2 = 6144
    S_PT = 7168

    def scr(o, n):
        return a_scr[:, o:o + n]

    FU = 0
    FW = 1792

    C_SG, C_SU, C_SD = 0, 32, 64
    C_MX = 80
    C_KG = 112
    C_RC = 114
    C_FIN = 116
    C_T1, C_T2 = 120, 121
    C_C = 124
    C_INV = 128
    C_F3 = 136
    C_SX = 144
    C_BIAS = 152
    C_MQ = 176
    C_KGA = 192
    C_RCA = 216

    R_SGX = 0
    R_SGD = 4
    R_LM = 8
    R_GM = 12
    R_SX = 16
    R_INV = 20
    R_SD = 24
    R_F2 = 28
    R_F3 = 32
    R_MQ = 40
    R_MK = 64
    R_BE = 88
    R_CC = 128
    R_TMP = 160
    R_EPS = 168

    RB_Z = 0

    k = B(nc)
    sSY = k.sem("sSY")
    sGP = k.sem("sGP")
    sSC = k.sem("sSC")
    sT = k.sem("sT")
    sV = k.sem("sV")
    sS = k.sem("sS")
    sG = k.sem("sG")
    sCC = k.sem("sCC")

    def dma(queue, out, in_, slow=False):
        sem = {"y": sSY, "g": sGP, "s": sSC}[queue]
        return k.inc(queue, lambda e, o=out, i=in_, sl=slow: e.dma_start(
            out=o, in_=i, allow_slow_non_contiguous=sl), sem, 16)

    def mm(out, lhsT, rhs, start, stop, sem=None):
        def go(e, o=out, l=lhsT, r=rhs, st=start, sp=stop):
            return e.matmul(o, l, r, start=st, stop=sp)
        if sem:
            return k.inc("t", go, sem)
        k.run("t", go)

    def vo(e, mk, sem=None):
        if sem:
            return k.inc(e, mk, sem)
        k.run(e, mk)

    dma("y", a_id[:, :], d_ident[:, :])
    dma("y", a_crw[:, :], d_crows[:, :])
    dma("g", a_b.ap().rearrange("p (t c) -> p t c", t=NTC),
        d_x.ap().rearrange("(t p) c -> p t c", p=128))
    dma("g", a_rot[:, :], d_crot[:, :])
    dma("y", a_bq[:, :], d_bqkv.ap().rearrange("(f p) -> p f", p=128), slow=True)
    dma("y", a_bo[:, :], d_bout.ap().rearrange("(f p) -> p f", p=128), slow=True)
    dma("y", a_f32[0:1, 0:1024], d_bqkv.ap()[2048:3072].rearrange("(a f) -> a f", a=1))
    dma("y", a_row[0:1, R_SGX:R_SGX + 1], d_clipx.ap().rearrange("(b a) -> b a", b=1))
    dma("y", a_row[0:1, R_SGD:R_SGD + 1], d_clipd.ap().rearrange("(b a) -> b a", b=1))

    k.wait("v", sSY)
    k.wait("v", sGP)
    vo("v", lambda e: e.tensor_copy(a_idb[:, :], a_id[:, :]))
    vo("v", lambda e: e.tensor_copy(a_crwb[:, :], a_crw[:, :]))
    vo("v", lambda e: e.memset(a_col[:, C_C:C_C + 1], MAGIC))
    vo("v", lambda e: e.memset(a_row[0:1, R_EPS:R_EPS + 1], EPS))
    vo("v", lambda e: e.tensor_copy(a_rowb[0:1, 0:1024], a_f32[0:1, 0:1024]),
       sem=sV)
    k.wait("s", sSY)
    vo("s", lambda e: e.activation(a_row[0:1, R_SGX:R_SGX + 1],
                                   a_row[0:1, R_SGX:R_SGX + 1], AF.Sigmoid))
    vo("s", lambda e: e.activation(a_row[0:1, R_SGD:R_SGD + 1],
                                   a_row[0:1, R_SGD:R_SGD + 1], AF.Sigmoid), sem=sS)

    k.wait("t", sV)
    k.wait("t", sGP)
    for sl in range(2):
        mm(pA[:, sl * 512:(sl + 1) * 512], a_crwb[0:1, 0:128],
           a_rowb[0:1, sl * 512:(sl + 1) * 512], start=True, stop=True,
           sem=sT if sl == 1 else None)
    k.wait("v", sT)
    BVB = a_rowb[:, 0:1024]
    vo("v", lambda e: e.tensor_copy(BVB, pA[:, 0:1024]), sem=sV)

    for tc in range(NTC):
        for half in range(2):
            pp = (pA if half == 0 else pB)[:, 1024 + (tc % 2) * 512:
                                           1024 + (tc % 2) * 512 + 512]
            for j in range(4):
                g = half * 4 + j
                mm(pp[:, j * 128:(j + 1) * 128],
                   a_b[:, tc * C + g * 128:tc * C + (g + 1) * 128],
                   a_idb[:, :], start=True, stop=True,
                   sem=sT if j == 3 else None)
        k.wait("s", sT)
        for half in range(2):
            pp = (pA if half == 0 else pB)[:, 1024 + (tc % 2) * 512:
                                           1024 + (tc % 2) * 512 + 512]
            def ev(e, tc=tc, half=half, pp=pp):
                outs = a_x.ap().rearrange("p (g t) -> p g t", g=NCC)
                return e.copy(outs[:, half * 4:(half + 1) * 4,
                                   tc * 128:(tc + 1) * 128], pp)
            vo("s", ev, sem=sS)
        if tc >= 1:
            k.wait("t", sS)
    k.barrier()

    def ln_norm(codes_max):
        onescol = a_crwb[:, 300:301]
        for g in range(NCC):
            sq = scr(S_ST0, T) if g % 2 == 0 else scr(S_BC1, T)
            if g >= 2:
                k.wait("v", sT)
            vo("v", lambda e, g=g, sq=sq: e.tensor_tensor(sq, xfm(g), xfm(g),
                                                          ALU.mult), sem=sV)
            k.wait("t", sV)
            for sl in range(4):
                mm(pA[0:1, sl * 512:(sl + 1) * 512], onescol,
                   xfm(g)[:, sl * 512:(sl + 1) * 512], start=(g == 0),
                   stop=(g == NCC - 1))
                mm(pB[0:1, sl * 512:(sl + 1) * 512], onescol,
                   sq[:, sl * 512:(sl + 1) * 512], start=(g == 0),
                   stop=(g == NCC - 1),
                   sem=sT if sl == 3 else None)
        MSQ = a_f32[0:1, 0:T]
        PSUMR = pA[0:1, :]
        PSQR = pB[0:1, :]
        RBM = a_scr[0:1, S_ST0:S_ST0 + T]
        RBS = a_scr[0:1, S_BC1:S_BC1 + T]
        k.wait("s", sT)
        vo("s", lambda e: e.activation(RBM, PSUMR, AF.Identity, scale=1.0 / C))
        vo("s", lambda e: e.activation(MSQ, PSUMR, AF.Square, scale=1.0 / C))
        vo("s", lambda e: e.activation(PSQR, PSQR, AF.Identity, scale=1.0 / C),
           sem=sS)
        k.wait("v", sS)
        vo("v", lambda e: e.tensor_tensor(PSQR, PSQR, MSQ, ALU.subtract), sem=sV)
        k.wait("s", sV)
        vo("s", lambda e: e.activation(MSQ, PSQR, AF.Ln, bias=a_row[0:1, R_EPS:R_EPS + 1]))
        vo("s", lambda e: e.activation(MSQ, MSQ, AF.Exp, scale=-0.5))
        vo("s", lambda e: e.activation(RBS, MSQ, AF.Identity), sem=sS)
        k.wait("v", sS)
        vo("v", lambda e: e.nop(), sem=sV)
        k.wait("t", sV)
        orow = a_crwb[0:1, 0:128]
        for sl in range(4):
            mm(pA[:, sl * 512:(sl + 1) * 512], orow, RBM[:, sl * 512:(sl + 1) * 512],
               start=True, stop=True)
            mm(pB[:, sl * 512:(sl + 1) * 512], orow, RBS[:, sl * 512:(sl + 1) * 512],
               start=True, stop=True, sem=sT if sl == 3 else None)
        k.wait("v", sT)
        MB = scr(S_BC0, T)
        RSB = scr(S_BC1, T)
        vo("v", lambda e: e.tensor_copy(MB, pA[:, :]))
        vo("v", lambda e: e.tensor_copy(RSB, pB[:, :]), sem=sV)
        for g in range(NCC):
            vo("v", lambda e, g=g: e.tensor_tensor(bfm(g), xfm(g), MB, ALU.subtract))
            vo("v", lambda e, g=g: e.tensor_tensor(bfm(g), bfm(g), RSB, ALU.mult),
               sem=sV)
            if codes_max:
                vo("v", lambda e, g=g: e.tensor_reduce(
                    a_col[:, C_MX + g:C_MX + g + 1], bfm(g), AXX, ALU.max,
                    apply_absolute_value=True), sem=sV)
        if codes_max:
            k.run("v", lambda e: e.drain())

    def global_max(ncols, scale_cols=None, sfact_row=None):
        if scale_cols is not None:
            for i in range(ncols):
                vo("v", lambda e, i=i: e.tensor_tensor(
                    a_col[:, C_MX + i:C_MX + i + 1],
                    a_col[:, C_MX + i:C_MX + i + 1],
                    a_col[:, scale_cols + i:scale_cols + i + 1], ALU.mult))
        step = 1
        while step < ncols:
            for i in range(0, ncols, 2 * step):
                if i + step < ncols:
                    vo("v", lambda e, i=i, s=step: e.tensor_tensor(
                        a_col[:, C_MX + i:C_MX + i + 1],
                        a_col[:, C_MX + i:C_MX + i + 1],
                        a_col[:, C_MX + i + s:C_MX + i + s + 1], ALU.max))
            step *= 2
        vo("v", lambda e: e.drain(), sem=sV)
        k.wait("t", sV)
        k.inc("t", lambda e: e.transpose(pA[0:1, 0:128], a_col[:, C_MX:C_MX + 1],
                                         a_id[:, :]), sT)
        k.wait("v", sT)
        vo("v", lambda e: e.tensor_reduce(a_row[0:1, R_LM:R_LM + 1],
                                          pA[0:1, 0:128], AXX, ALU.max), sem=sV)
        k.run("v", lambda e: e.drain())
        if sfact_row is not None:
            vo("v", lambda e: e.tensor_tensor(
                a_row[0:1, R_LM:R_LM + 1], a_row[0:1, R_LM:R_LM + 1],
                a_row[0:1, sfact_row:sfact_row + 1], ALU.mult), sem=sV)
            k.run("v", lambda e: e.drain())
        vo("v", lambda e: e.memset(a_row[0:1, R_CC:R_CC + 8], 0.0))
        vo("v", lambda e: e.tensor_copy(a_row[0:1, R_CC:R_CC + 1],
                                        a_row[0:1, R_LM:R_LM + 1]), sem=sV)
        k.wait("y", sV)
        k.wait("y", sCC)
        dma("y", d_ccin[:, :], a_row[0:1, R_CC:R_CC + 8])
        k.wait("g", sSY)
        k.inc("g", lambda e: e.collective_compute(
            "AllReduce", ALU.max, replica_groups=[list(range(8))],
            ins=[d_ccin[:, :]], outs=[d_ccout[:, :]]), sCC)
        k.wait("g", sCC)
        dma("g", a_row[0:1, R_GM:R_GM + 1], d_ccout[0:1, 0:1])
        k.wait("v", sGP)

    def bcast_scalar(rowoff, coloff):
        k.wait("t", sV)
        mm(pB[:, 0:1], a_crw[0:1, 0:128], a_row[0:1, rowoff:rowoff + 1],
           start=True, stop=True, sem=sT)
        k.wait("v", sT)
        vo("v", lambda e, c=coloff: e.tensor_copy(a_col[:, c:c + 1], pB[:, 0:1]),
           sem=sV)

    def scalar_chain(dst_sd, dst_inv, sig_row):
        vo("v", lambda e, d=dst_sd: e.tensor_tensor(
            a_row[0:1, d:d + 1], a_row[0:1, R_GM:R_GM + 1],
            a_row[0:1, sig_row:sig_row + 1], ALU.mult), sem=sV)
        k.run("v", lambda e: e.drain())
        vo("v", lambda e, d=dst_sd, i=dst_inv: e.reciprocal(
            a_row[0:1, i:i + 1], a_row[0:1, d:d + 1]))
        k.run("v", lambda e: e.drain())
        vo("v", lambda e, i=dst_inv: e.tensor_scalar(
            a_row[0:1, i:i + 1], a_row[0:1, i:i + 1], 127.0, None, ALU.mult))
        vo("v", lambda e, d=dst_sd: e.tensor_scalar(
            a_row[0:1, d:d + 1], a_row[0:1, d:d + 1], 1.0 / 127.0, None,
            ALU.mult), sem=sV)
        k.run("v", lambda e: e.drain())

    def quant_chunks(nch, src_fn, scale_col, out_fn):
        vmk = {}
        k.wait("s", sV)
        for n in range(2 * nch):
            i, hh = n // 2, n % 2
            sc = scale_col(i) if callable(scale_col) else scale_col
            u = a_f32[:, FU + (n % 2) * 1024:FU + (n % 2) * 1024 + 1024]
            if n >= 2:
                k.wait("s", sV, vmk[n - 2])
            vo("s", lambda e, i=i, hh=hh, sc=sc, u=u: e.activation(
                u, src_fn(i)[:, hh * 1024:(hh + 1) * 1024], AF.Identity,
                bias=a_col[:, C_C:C_C + 1], scale=sc), sem=sS)
            sm = k.mark(sS)
            k.wait("v", sS, sm)
            vo("v", lambda e, u=u: e.tensor_scalar(u, u, -MAGIC, 127.0,
                                                   ALU.add, ALU.min))
            vo("v", lambda e, i=i, hh=hh, u=u: e.tensor_scalar(
                out_fn(i)[:, hh * 1024:(hh + 1) * 1024], u, -128.0,
                None, ALU.max), sem=sV)
            vmk[n] = k.mark(sV)
        k.wait("v", sV)

    def wquant_rows(wf, scol_ap, wint, width):
        rmax = a_col[:, C_T1:C_T1 + 1]
        rinv = a_col[:, C_T2:C_T2 + 1]
        vo("v", lambda e: e.tensor_reduce(rmax, wf, AXX, ALU.max,
                                          apply_absolute_value=True), sem=sV)
        k.run("v", lambda e: e.drain())
        vo("v", lambda e: e.tensor_scalar(scol_ap, rmax, 1.0 / 127.0, None,
                                          ALU.mult))
        vo("v", lambda e: e.reciprocal(rinv, rmax))
        k.run("v", lambda e: e.drain())
        vo("v", lambda e: e.tensor_scalar(rinv, rinv, 127.0, None, ALU.mult))
        k.run("v", lambda e: e.drain())
        for part in range((width + 1023) // 1024):
            lo = part * 1024
            hi = min(width, lo + 1024)
            u = a_f32[:, FU:FU + (hi - lo)]
            vo("v", lambda e, lo=lo, hi=hi, u=u, wf=wf: e.tensor_scalar(
                u, wf[:, lo:hi], rinv, MAGIC, ALU.mult, ALU.add))
            vo("v", lambda e, lo=lo, hi=hi, u=u, wint=wint: e.tensor_scalar(
                wint[:, lo:hi], u, -MAGIC, None, ALU.add), sem=sV)

    def transpose_tiles(wint, ntile, dststage, pbank):
        k.wait("t", sV)
        k.wait("t", sS)
        for i in range(ntile):
            pp = pbank[:, (i % 4) * 128:(i % 4) * 128 + 128]
            mm(pp, wint[:, i * 128:(i + 1) * 128], a_idb[:, :], start=True,
               stop=True, sem=sT if (i % 4 == 3 or i == ntile - 1) else None)
            if i % 4 == 3 or i == ntile - 1:
                k.wait("s", sT)
                base = (i // 4) * 4
                ngrp = i - base + 1
                vo("s", lambda e, base=base, ngrp=ngrp, d=dststage, p=pbank: e.copy(
                    d[:, base * 128:(base + ngrp) * 128], p[:, 0:ngrp * 128]),
                   sem=sS)
                if i != ntile - 1:
                    k.wait("t", sS)

    def emit_ffn(save, dbg_tag=None):
        ln_norm(codes_max=True)
        global_max(NCC)
        scalar_chain(R_SX, R_INV, R_SGX)
        bcast_scalar(R_INV, C_INV)
        bcast_scalar(R_SX, C_SX)
        quant_chunks(NCC, bfm, a_col[:, C_INV:C_INV + 1], bfm)
        if not save:
            vo("v", lambda e: e.tensor_scalar(
                a_col[:, C_KGA:C_KGA + NIC], a_col[:, C_SG:C_SG + NIC],
                a_col[:, C_SX:C_SX + 1], None, ALU.mult), sem=sV)
        k.barrier()
        if debug and dbg_tag == 1:
            k.wait("y", sV)
            dma("y", dbg["sc"][0:1, 0:16], a_row[0:1, 0:16])
            dma("y", dbg["sc"][0:1, 16:32], a_row[0:1, 16:32])
        st_ready = [None, None]
        kgm = [0, 0]
        stm = [0, 0]
        wf_free = [0]
        wint_free = [0]
        mm_done = [0, 0]
        ev_done = [(0, 0)]

        def gu_stage(i):
            par = i % 2
            gst = scr(S_ST0, 1024) if par == 0 else scr(S_BC0, 1024)
            ust = scr(S_ST1, 1024) if par == 0 else scr(S_BC0 + 1024, 1024)
            return gst, ust

        def gu_prep_a(i):
            par = i % 2
            gst, ust = gu_stage(i)
            if save:
                k.wait("y", sV, wf_free[0])
                wfg = a_f32[:, FW:FW + 1024]
                wfu = a_f32[:, FW + 1024:FW + 2048]
                dma("y", wfg, d_wg[i * 128:(i + 1) * 128, :])
                dma("y", wfu, d_wu[i * 128:(i + 1) * 128, :])
                ldm = k.mark(sSY)
                k.wait("v", sSY, ldm)
                k.wait("v", sT, wint_free[0])
                wquant_rows(wfg, a_col[:, C_SG + i:C_SG + i + 1],
                            scr(S_ST2, 1024), 1024)
                wquant_rows(wfu, a_col[:, C_SU + i:C_SU + i + 1],
                            scr(S_BC1 + 1024, 1024), 1024)
                wf_free[0] = k.mark(sV)
                vo("v", lambda e, i=i, par=par: e.tensor_tensor(
                    a_col[:, C_KG + par:C_KG + par + 1],
                    a_col[:, C_SG + i:C_SG + i + 1],
                    a_col[:, C_SX:C_SX + 1], ALU.mult), sem=sV)
                kgm[par] = k.mark(sV)
            else:
                k.wait("y", sT, mm_done[par])
                dma("y", gst, d_wgt[i, :, :])
                dma("y", ust, d_wut[i, :, :])
                st_ready[par] = ("y", k.mark(sSY))

        def gu_prep_b(i):
            if not save:
                return
            par = i % 2
            gst, ust = gu_stage(i)
            ss, vv = ev_done[0]
            k.wait("t", sS, ss)
            k.wait("t", sV, vv)
            k.wait("t", sT, mm_done[par])
            k.wait("s", sSY, stm[par])
            transpose_tiles(scr(S_ST2, 1024), 8, gst, pA[:, 0:512])
            transpose_tiles(scr(S_BC1 + 1024, 1024), 8, ust, pA[:, 512:1024])
            wint_free[0] = k.mark(sT)
            evm = k.mark(sS)
            st_ready[par] = ("s", evm)
            k.wait("y", sS, evm)
            dma("y", d_wgt[i, :, :], gst)
            dma("y", d_wut[i, :, :], ust)
            stm[par] = k.mark(sSY)

        gu_prep_a(0)
        gu_prep_b(0)
        for i in range(NIC):
            par = i % 2
            gst, ust = gu_stage(i)
            if i + 1 < NIC:
                gu_prep_a(i + 1)
            kind, mk = st_ready[par]
            k.wait("t", sSY if kind == "y" else sS, mk)
            for th in range(2):
                if th == 1 and i + 1 < NIC:
                    gu_prep_b(i + 1)
                pg = (pA if th == 0 else pB)[:, 0:1024]
                pu = (pA if th == 0 else pB)[:, 1024:2048]
                ssm, vvm = ev_done[0]
                k.wait("t", sS, ssm)
                k.wait("t", sV, vvm)
                for h in range(NCC):
                    for sl in range(2):
                        nsl = th * 1024 + sl * 512
                        mm(pg[:, sl * 512:(sl + 1) * 512],
                           gst[:, h * 128:(h + 1) * 128],
                           bfm(h)[:, nsl:nsl + 512], start=(h == 0),
                           stop=(h == NCC - 1))
                        mm(pu[:, sl * 512:(sl + 1) * 512],
                           ust[:, h * 128:(h + 1) * 128],
                           bfm(h)[:, nsl:nsl + 512], start=(h == 0),
                           stop=(h == NCC - 1),
                           sem=sT if (h == NCC - 1 and sl == 1) else None)
                mmk = k.mark(sT)
                k.wait("s", sT, mmk)
                sil = scr(S_PT, 1024)
                kgap = (a_col[:, C_KG + par:C_KG + par + 1] if save
                        else a_col[:, C_KGA + i:C_KGA + i + 1])
                k.wait("s", sV, max(kgm[par], ev_done[0][1]))
                vo("s", lambda e, pg=pg, sil=sil, kgap=kgap: e.activation(
                    sil, pg, AF.Silu, scale=kgap), sem=sS)
                silm = k.mark(sS)
                k.wait("v", sS, silm)
                vo("v", lambda e, i=i, th=th, pu=pu, sil=sil: e.scalar_tensor_tensor(
                    tcode(i)[:, th * 1024:(th + 1) * 1024], pu, 1.0, sil,
                    ALU.mult, ALU.mult), sem=sV)
                ev_done[0] = (k.mark(sS), k.mark(sV))
            mm_done[par] = k.mark(sT)
            vo("v", lambda e, i=i: e.tensor_reduce(
                a_col[:, C_MX + i:C_MX + i + 1], tcode(i), AXX, ALU.max,
                apply_absolute_value=True), sem=sV)
        k.wait("v", sV)
        k.barrier()
        global_max(NIC, scale_cols=C_SU, sfact_row=R_SX)
        scalar_chain(R_SD, R_TMP, R_SGD)
        vo("v", lambda e: e.tensor_tensor(a_row[0:1, R_F2:R_F2 + 1],
                                          a_row[0:1, R_SX:R_SX + 1],
                                          a_row[0:1, R_TMP:R_TMP + 1], ALU.mult))
        vo("v", lambda e: e.tensor_scalar(a_row[0:1, R_F3:R_F3 + 1],
                                          a_row[0:1, R_SD:R_SD + 1], 0.5, None,
                                          ALU.mult), sem=sV)
        k.wait("v", sV)
        bcast_scalar(R_F2, C_INV)
        bcast_scalar(R_F3, C_F3)
        k.barrier()
        if debug and dbg_tag == 1:
            k.wait("y", sV)
            dma("y", dbg["sc"][0:1, 32:64], a_row[0:1, 0:32])
        vo("v", lambda e: e.tensor_scalar(
            a_col[:, C_RCA:C_RCA + NIC], a_col[:, C_SU:C_SU + NIC],
            a_col[:, C_INV:C_INV + 1], None, ALU.mult), sem=sV)
        k.wait("v", sV)
        quant_chunks(NIC, tcode, lambda i: a_col[:, C_RCA + i:C_RCA + i + 1], tcode)
        k.barrier()
        dn_ready = [None, None]
        dn_mm_done = [0, 0]
        dn_wf_free = [0]
        dn_wint_free = [0]
        dn_ev_done = [0]

        def dn_stage(h):
            return scr(S_ST0, II) if h % 2 == 0 else scr(S_BC0, II)

        def dn_prep_a(h):
            par = h % 2
            dstg = dn_stage(h)
            if save:
                k.wait("y", sV, dn_wf_free[0])
                wfh = a_f32[:, 0:II]
                dma("y", wfh, d_wd[h * 128:(h + 1) * 128, :])
                ldm = k.mark(sSY)
                k.wait("v", sSY, ldm)
                k.wait("v", sT, dn_wint_free[0])
                rmax = a_col[:, C_T1:C_T1 + 1]
                rinv = a_col[:, C_T2:C_T2 + 1]
                vo("v", lambda e, wfh=wfh: e.tensor_reduce(
                    rmax, wfh, AXX, ALU.max, apply_absolute_value=True), sem=sV)
                k.run("v", lambda e: e.drain())
                vo("v", lambda e, h=h: e.tensor_scalar(
                    a_col[:, C_SD + h:C_SD + h + 1], rmax, 1.0 / 127.0, None,
                    ALU.mult))
                vo("v", lambda e: e.reciprocal(rinv, rmax))
                k.run("v", lambda e: e.drain())
                vo("v", lambda e: e.tensor_scalar(rinv, rinv, 127.0, None,
                                                  ALU.mult))
                k.run("v", lambda e: e.drain())
                wint = scr(S_PT, II)
                u2 = a_f32[:, II:II + 1024]
                for part in range(3):
                    lo = part * 1024
                    hi = min(II, lo + 1024)
                    vo("v", lambda e, lo=lo, hi=hi, u2=u2, wfh=wfh: e.tensor_scalar(
                        u2[:, 0:hi - lo], wfh[:, lo:hi], rinv, MAGIC, ALU.mult,
                        ALU.add))
                    vo("v", lambda e, lo=lo, hi=hi, u2=u2, wint=wint: e.tensor_scalar(
                        wint[:, lo:hi], u2[:, 0:hi - lo], -MAGIC, None, ALU.add),
                        sem=sV)
                dn_wf_free[0] = k.mark(sV)
                vo("v", lambda e, h=h, par=par: e.tensor_tensor(
                    a_col[:, C_FIN + par:C_FIN + par + 1],
                    a_col[:, C_SD + h:C_SD + h + 1],
                    a_col[:, C_F3:C_F3 + 1], ALU.mult), sem=sV)
            else:
                k.wait("y", sT, dn_mm_done[par])
                dma("y", dstg, d_wdt[h, :, :])
                dn_ready[par] = ("y", k.mark(sSY))
                vo("v", lambda e, h=h, par=par: e.tensor_tensor(
                    a_col[:, C_FIN + par:C_FIN + par + 1],
                    a_col[:, C_SD + h:C_SD + h + 1],
                    a_col[:, C_F3:C_F3 + 1], ALU.mult), sem=sV)

        def dn_prep_b(h):
            if not save:
                return
            par = h % 2
            dstg = dn_stage(h)
            pbank = (pA if par == 0 else pB)[:, 1024:1536]
            k.wait("t", sT, dn_mm_done[par])
            k.wait("t", sV, dn_ev_done[0])
            k.wait("s", sSY)
            transpose_tiles(scr(S_PT, II), NIC, dstg, pbank)
            dn_wint_free[0] = k.mark(sT)
            evm = k.mark(sS)
            dn_ready[par] = ("s", evm)
            k.wait("y", sS, evm)
            dma("y", d_wdt[h, :, :], dstg)

        dn_prep_a(0)
        dn_prep_b(0)
        for h in range(NCC):
            par = h % 2
            dstg = dn_stage(h)
            pdn = pA if par == 0 else pB
            if h + 1 < NCC:
                dn_prep_a(h + 1)
            kind, mk = dn_ready[par]
            k.wait("t", sSY if kind == "y" else sS, mk)
            k.wait("t", sV, dn_ev_done[0])
            for i in range(NIC):
                for sl in range(4):
                    mm(pdn[:, sl * 512:(sl + 1) * 512],
                       dstg[:, i * 128:(i + 1) * 128],
                       tcode(i)[:, sl * 512:(sl + 1) * 512], start=(i == 0),
                       stop=(i == NIC - 1),
                       sem=sT if (i == NIC - 1 and sl == 3) else None)
            dn_mm_done[par] = k.mark(sT)
            k.wait("v", sT, dn_mm_done[par])
            k.wait("v", sV)
            vo("v", lambda e, h=h, par=par, pdn=pdn: e.scalar_tensor_tensor(
                xfm(h), pdn[:, :], a_col[:, C_FIN + par:C_FIN + par + 1],
                xfm(h), ALU.mult, ALU.add), sem=sV)
            dn_ev_done[0] = k.mark(sV)
            if h + 1 < NCC:
                dn_prep_b(h + 1)
        k.barrier()

    def emit_attn():
        ln_norm(codes_max=False)
        k.barrier()
        qk_mm = {-1: 0, 0: 0, 1: 0}
        qk_ld = {}
        qk_ev = {-1: 0, 0: 0, 1: 0}

        def qk_load(f):
            wt = scr(S_ST0 + (f % 2) * 1024, 1024)
            k.wait("g", sT, qk_mm[(f % 2)])
            dma("g", wt.rearrange("p (c j) -> p c j", c=8),
                d_wqkv.ap()[:, f * 128:(f + 1) * 128]
                .rearrange("(c p) j -> p c j", p=128))
            qk_ld[f] = k.mark(sGP)
            return wt

        wts = {0: qk_load(0)}
        for f in range(16):
            wt = wts.pop(f)
            if f + 1 < 16:
                wts[f + 1] = qk_load(f + 1)
            k.wait("t", sGP, qk_ld[f])
            k.wait("t", sS, qk_ev[f % 2])
            pq = pA if f % 2 == 0 else pB
            for c in range(NCC):
                for sl in range(4):
                    mm(pq[:, sl * 512:(sl + 1) * 512], wt[:, c * 128:(c + 1) * 128],
                       bfm(c)[:, sl * 512:(sl + 1) * 512], start=(c == 0),
                       stop=(c == NCC - 1),
                       sem=sT if (c == NCC - 1 and sl == 3) else None)
            qk_mm[f % 2] = k.mark(sT)
            k.wait("s", sT, qk_mm[f % 2])
            dst = qfm(f) if f < 8 else kfm(f - 8)
            vo("s", lambda e, dst=dst, pq=pq, f=f: e.activation(
                dst, pq[:, :], AF.Identity, bias=a_bq[:, f:f + 1], scale=1.0),
               sem=sS)
            qk_ev[f % 2] = k.mark(sS)
        k.barrier()
        vo("v", lambda e: e.memset(a_big[:, 16 * T:16 * T + NTC * NHEAD * 65],
                                   1.0), sem=sV)
        WV = a_scr[:, S_BC1:S_BC1 + 8192]
        dma("g", WV.rearrange("p (c j) -> p c j", c=8),
            d_wqkv.ap()[:, 2048:3072].rearrange("(c p) j -> p c j", p=128))
        k.wait("t", sGP)
        k.wait("t", sV)
        vev_mk = {0: 0, 1: 0}
        for tc in range(NTC):
            pv = (pA if tc % 2 == 0 else pB)[:, 0:1024]
            k.wait("t", sV, vev_mk[tc % 2])
            for c in range(NCC):
                for sl in range(2):
                    mm(pv[:, sl * 512:(sl + 1) * 512],
                       bfm(c)[:, tc * 128:(tc + 1) * 128],
                       WV[:, c * 1024 + sl * 512:c * 1024 + (sl + 1) * 512],
                       start=(c == 0), stop=(c == NCC - 1),
                       sem=sT if (c == NCC - 1 and sl == 1) else None)
            vmmk = k.mark(sT)
            k.wait("v", sT, vmmk)
            def vev(e, tc=tc, pv=pv):
                outp = vaug(tc).rearrange("p (h j) -> p h j", j=65)[:, :, 0:64]
                return e.scalar_tensor_tensor(
                    outp, pv, 1.0, BVB.rearrange("p (h j) -> p h j", j=64),
                    ALU.mult, ALU.add)
            vo("v", vev, sem=sV)
            vev_mk[tc % 2] = k.mark(sV)
        k.barrier()
        A_COS = scr(S_PT, T)
        A_SIN = scr(S_PT + T, T)
        dma("g", A_COS, d_cos[:, :])
        dma("g", A_SIN, d_sin[:, :])
        k.wait("v", sGP)
        rp_mm = {0: 0, 1: 0}
        rp_ev = {0: 0, 1: 0}
        rp_v = {0: 0, 1: 0}
        for c in range(16):
            src = qfm(c) if c < 8 else kfm(c - 8)
            prm = scr(S_ST0, T) if c % 2 == 0 else scr(S_ST2, T)
            t1 = a_scr[:, S_BC0:S_BC0 + T] if c % 2 == 0 else a_scr[:, S_BC1:S_BC1 + T]
            pr = (pA if c % 2 == 0 else pB)
            k.wait("t", sS, rp_ev[c % 2])
            k.wait("t", sV, rp_v[c % 2])
            for sl in range(4):
                mm(pr[:, sl * 512:(sl + 1) * 512], a_rot[:, :],
                   src[:, sl * 512:(sl + 1) * 512], start=True, stop=True,
                   sem=sT if sl == 3 else None)
            rp_mm[c % 2] = k.mark(sT)
            k.wait("s", sT, rp_mm[c % 2])
            k.wait("s", sV, rp_v[c % 2])
            vo("s", lambda e, prm=prm, pr=pr: e.copy(prm, pr[:, :]), sem=sS)
            evm = k.mark(sS)
            rp_ev[c % 2] = evm
            k.wait("v", sS, evm)
            vo("v", lambda e, src=src, t1=t1: e.tensor_tensor(
                t1, src, A_COS, ALU.mult))
            vo("v", lambda e, prm=prm: e.tensor_tensor(prm, prm, A_SIN,
                                                       ALU.mult))
            vo("v", lambda e, src=src, t1=t1, prm=prm: e.tensor_tensor(
                src, t1, prm, ALU.add))
            vo("v", lambda e, c=c: e.tensor_reduce(
                a_col[:, C_MQ + c:C_MQ + c + 1], src, AXX, ALU.max,
                apply_absolute_value=True), sem=sV)
            rp_v[c % 2] = k.mark(sV)
        k.wait("v", sV)
        k.barrier()
        k.inc("t", lambda e: e.transpose(pA[0:16, 0:128],
                                         a_col[:, C_MQ:C_MQ + 16], a_id[:, :]), sT)
        k.wait("v", sT)
        r1c = a_f32[0:16, 3072:3073]
        r2c = a_f32[0:16, 3073:3074]
        vo("v", lambda e: e.tensor_reduce(r1c, pA[0:16, 0:64], AXX, ALU.max))
        vo("v", lambda e: e.tensor_reduce(r2c, pA[0:16, 64:128], AXX, ALU.max),
           sem=sV)
        k.run("v", lambda e: e.drain())
        k.wait("t", sV)
        k.inc("t", lambda e: e.transpose(pB[0:1, 0:16], r1c, a_id[0:16, 0:16]), sT)
        k.inc("t", lambda e: e.transpose(pB[0:1, 16:32], r2c, a_id[0:16, 0:16]), sT)
        k.wait("v", sT)
        r1row = a_f32[0:1, 3074:3090]
        r2row = a_f32[0:1, 3090:3106]
        vo("v", lambda e: e.tensor_copy(r1row, pB[0:1, 0:16]))
        vo("v", lambda e: e.tensor_copy(r2row, pB[0:1, 16:32]), sem=sV)
        BE = a_row[0:1, R_BE:R_BE + 16]
        bev = BE.rearrange("p (c t) -> p c t", t=2)

        def r3(ap):
            return ap.rearrange("p (c o) -> p c o", o=1)
        vo("v", lambda e: e.tensor_tensor(bev[:, :, 0:1], r3(r1row[:, 0:8]),
                                          r3(r1row[:, 8:16]), ALU.mult))
        vo("v", lambda e: e.tensor_tensor(bev[:, :, 1:2], r3(r2row[:, 0:8]),
                                          r3(r2row[:, 8:16]), ALU.mult), sem=sV)
        k.run("v", lambda e: e.drain())
        vo("v", lambda e: e.tensor_scalar(BE, BE, -8.0, 40.0, ALU.mult, ALU.add),
           sem=sV)
        k.wait("v", sV)
        BP = a_row[0:1, R_BE + 16:R_BE + 24]
        vo("v", lambda e: e.tensor_tensor(
            BP, bev[:, :, 0:1].rearrange("p c o -> p (c o)"),
            bev[:, :, 1:2].rearrange("p c o -> p (c o)"), ALU.min), sem=sV)
        k.run("v", lambda e: e.drain())
        k.wait("t", sV)
        for h in range(8):
            mm(pB[:, h:h + 1], a_crw[0:1, 0:128],
               a_row[0:1, R_BE + 16 + h:R_BE + 17 + h], start=True, stop=True,
               sem=sT if h == 7 else None)
        k.wait("v", sT)
        vo("v", lambda e: e.tensor_copy(a_col[:, C_BIAS:C_BIAS + 8],
                                        pB[:, 0:8]), sem=sV)
        k.barrier()
        phases = [(hp, qq) for hp in range(8) for qq in range(4)]
        norm2 = {-1: 0}

        def emit_evac2(ctx):
            hp, qq, par, avA, avB, hA, hB, avmk = ctx
            ZA = a_rowb[64:65, RB_Z + par * 1024:RB_Z + par * 1024 + 512]
            ZB = a_rowb[64:65, RB_Z + par * 1024 + 512:RB_Z + par * 1024 + 1024]
            ZAf = a_f32[64:65, par * 1024:par * 1024 + 512]
            ZBf = a_f32[64:65, par * 1024 + 512:par * 1024 + 1024]
            k.wait("s", sT, avmk)
            vo("s", lambda e: e.activation(ZAf, avA[64:65, :], AF.Ln))
            vo("s", lambda e: e.activation(ZA, ZAf, AF.Exp, scale=-1.0))
            vo("s", lambda e: e.activation(ZBf, avB[64:65, :], AF.Ln))
            vo("s", lambda e: e.activation(ZB, ZBf, AF.Exp, scale=-1.0), sem=sS)
            zm = k.mark(sS)
            oa = scr(S_ST0 + par * 1024, 512)
            ob = scr(S_ST0 + par * 1024 + 512, 512)
            k.wait("v", sSC)
            k.wait("v", sT, avmk)
            vo("v", lambda e, oa=oa, avA=avA: e.tensor_copy(oa[0:64, :],
                                                            avA[0:64, :]))
            vo("v", lambda e, ob=ob, avB=avB: e.tensor_copy(ob[0:64, :],
                                                            avB[0:64, :]), sem=sV)
            om = k.mark(sV)
            freed = pB[:, par * 1024:par * 1024 + 512]
            k.wait("t", sV, om)
            k.wait("t", sS, zm)
            mm(freed, a_crwb[64:65, 128:256], ZA, start=True, stop=True, sem=sT)
            m1 = k.mark(sT)
            k.wait("v", sT, m1)
            vo("v", lambda e, hp=hp, qq=qq, oa=oa, freed=freed: e.tensor_tensor(
                bfm(hp)[0:64, qq * 512:(qq + 1) * 512], oa[0:64, :],
                freed[0:64, :], ALU.mult), sem=sV)
            m2 = k.mark(sV)
            k.wait("t", sV, m2)
            mm(freed, a_crwb[64:65, 128:256], ZB, start=True, stop=True, sem=sT)
            m3 = k.mark(sT)
            k.wait("v", sT, m3)
            vo("v", lambda e, ob=ob, freed=freed: e.tensor_tensor(
                ob[0:64, :], ob[0:64, :], freed[0:64, :], ALU.mult), sem=sV)
            nm = k.mark(sV)
            k.wait("s", sV, nm)
            dma("s", bfm(hp)[64:128, qq * 512:(qq + 1) * 512], ob[0:64, :])
            return nm

        pend = [None]
        for p2 in range(16):
            ph = []
            for par in range(2):
                hp, qq = phases[2 * p2 + par]
                ph.append({
                    "hp": hp, "qq": qq, "par": par,
                    "avA": pB[0:65, par * 1024:par * 1024 + 512],
                    "avB": pB[0:65, par * 1024 + 512:par * 1024 + 1024],
                    "hA": 2 * hp, "hB": 2 * hp + 1,
                    "expmk": {-1: 0}, "scm": {}, "avmk": {-2: 0, -1: 0},
                })
            if pend[0] is not None:
                nm = 0
                for ctx in pend[0]:
                    nm = emit_evac2(ctx)
                norm2[p2 - 1] = nm
                pend[0] = None
            for kt in range(NTC + 1):
                for par in range(2):
                    f = ph[par]
                    base = par * 1024
                    if kt < NTC:
                        k.wait("t", sS, f["expmk"][kt - 1])
                        mm(pA[:, base:base + 512],
                           kfm(f["hp"])[0:64, kt * 128:(kt + 1) * 128],
                           qfm(f["hp"])[0:64, f["qq"] * 512:(f["qq"] + 1) * 512],
                           start=True, stop=True)
                        mm(pA[:, base + 512:base + 1024],
                           kfm(f["hp"])[64:128, kt * 128:(kt + 1) * 128],
                           qfm(f["hp"])[64:128, f["qq"] * 512:(f["qq"] + 1) * 512],
                           start=True, stop=True, sem=sT)
                        f["scm"][kt] = k.mark(sT)
                for par in range(2):
                    f = ph[par]
                    base = par * 1024
                    if kt < NTC:
                        k.wait("s", sT, f["scm"][kt])
                        k.wait("s", sT, f["avmk"][kt - 2])
                        slot = (kt % 2) * 2048 + par * 1024
                        ptAB = scr(S_PT + slot, 1024)
                        f["expmk"][kt] = vo(
                            "s", lambda e, ptAB=ptAB, f=f, base=base: e.activation(
                                ptAB, pA[:, base:base + 1024], AF.Exp,
                                bias=a_col[:, C_BIAS + f["hp"]:C_BIAS + f["hp"] + 1],
                                scale=0.125), sem=sS)
                for par in range(2):
                    f = ph[par]
                    if kt >= 1:
                        ka = kt - 1
                        slot = (ka % 2) * 2048 + par * 1024
                        ptA = scr(S_PT + slot, 512)
                        ptB = scr(S_PT + slot + 512, 512)
                        k.wait("t", sS, f["expmk"][ka])
                        if ka == 0:
                            k.wait("t", sV, norm2[p2 - 1])
                        mm(f["avA"], vaug(ka)[:, f["hA"] * 65:f["hA"] * 65 + 65],
                           ptA, start=(ka == 0), stop=(ka == NTC - 1))
                        mm(f["avB"], vaug(ka)[:, f["hB"] * 65:f["hB"] * 65 + 65],
                           ptB, start=(ka == 0), stop=(ka == NTC - 1), sem=sT)
                        f["avmk"][ka] = k.mark(sT)
            pend[0] = [(ph[par]["hp"], ph[par]["qq"], par, ph[par]["avA"],
                        ph[par]["avB"], ph[par]["hA"], ph[par]["hB"],
                        ph[par]["avmk"][NTC - 1]) for par in range(2)]
        for ctx in pend[0]:
            emit_evac2(ctx)
        k.barrier()
        op_mm = {0: 0, 1: 0}
        op_ld = {}
        op_ev = {0: 0, 1: 0}

        def op_load(f):
            wt = scr(S_ST0 + (f % 2) * 1024, 1024)
            k.wait("g", sT, op_mm[f % 2])
            dma("g", wt.rearrange("p (c j) -> p c j", c=8),
                d_wout.ap()[:, f * 128:(f + 1) * 128]
                .rearrange("(c p) j -> p c j", p=128))
            op_ld[f] = k.mark(sGP)
            return wt

        wts2 = {0: op_load(0)}
        for f in range(NCC):
            wt = wts2.pop(f)
            if f + 1 < NCC:
                wts2[f + 1] = op_load(f + 1)
            k.wait("t", sGP, op_ld[f])
            k.wait("t", sV, op_ev[f % 2])
            po = pA if f % 2 == 0 else pB
            for c in range(NCC):
                for sl in range(4):
                    mm(po[:, sl * 512:(sl + 1) * 512], wt[:, c * 128:(c + 1) * 128],
                       bfm(c)[:, sl * 512:(sl + 1) * 512], start=(c == 0),
                       stop=(c == NCC - 1),
                       sem=sT if (c == NCC - 1 and sl == 3) else None)
            op_mm[f % 2] = k.mark(sT)
            k.wait("v", sT, op_mm[f % 2])
            vo("v", lambda e, f=f, po=po: e.scalar_tensor_tensor(
                xfm(f), po[:, :], a_bo[:, f:f + 1], xfm(f), ALU.add, ALU.add),
               sem=sV)
            op_ev[f % 2] = k.mark(sV)
        k.barrier()

    def emit_store_x(target):
        for tc in range(NTC):
            for half in range(2):
                pp = (pA if half == 0 else pB)[:, (tc % 2) * 512:
                                               (tc % 2) * 512 + 512]
                for j in range(4):
                    g = half * 4 + j
                    mm(pp[:, j * 128:(j + 1) * 128],
                       xfm(g)[:, tc * 128:(tc + 1) * 128], a_idb[:, :],
                       start=True, stop=True, sem=sT if j == 3 else None)
            k.wait("v", sT)
            stg = a_f32[:, FU:FU + 1024]
            vo("v", lambda e, tc=tc, stg=stg: e.tensor_copy(
                stg[:, 0:512], pA[:, (tc % 2) * 512:(tc % 2) * 512 + 512]))
            vo("v", lambda e, tc=tc, stg=stg: e.tensor_copy(
                stg[:, 512:1024], pB[:, (tc % 2) * 512:(tc % 2) * 512 + 512]),
               sem=sV)
            k.wait("y", sV)
            dma("y", target[tc * 128:(tc + 1) * 128, :], stg)
            k.wait("v", sSY)
            k.wait("t", sV)
        k.barrier()

    emit_ffn(save=True, dbg_tag=1)
    if debug:
        emit_store_x(dbg["x1"])
    emit_attn()
    if debug:
        emit_store_x(dbg["x2"])
    emit_ffn(save=False)
    emit_store_x(d_out)
    k.barrier()

    with nc.Block() as block:
        @block.tensor
        def _(e):
            for fn in k.q["t"]:
                fn(e)

        @block.vector
        def _(e):
            for fn in k.q["v"]:
                fn(e)

        @block.scalar
        def _(e):
            for fn in k.q["s"]:
                fn(e)

        @block.gpsimd
        def _(e):
            for fn in k.q["g"]:
                fn(e)

        @block.sync
        def _(e):
            for fn in k.q["y"]:
                fn(e)

    for g in reversed(guards):
        g.__exit__(None, None, None)
    for g in reversed(k.guards):
        g.__exit__(None, None, None)
    return nc


def _host_consts():
    ident = np.eye(128, dtype=np.float32)
    DH = 64
    inv = (1.0 / 10000.0 ** (np.arange(0, DH, 2, dtype=np.float64) / DH))
    tpos = np.arange(T, dtype=np.float64)[:, None]
    fr = tpos * inv[None, :]
    emb = np.concatenate([fr, fr], axis=1)
    cos = np.cos(emb).astype(np.float32).T
    sin = np.sin(emb).astype(np.float32).T
    cos_fm = np.concatenate([cos, cos], axis=0)
    sin_fm = np.concatenate([sin, sin], axis=0)
    R = np.zeros((128, 128), dtype=np.float32)
    for blk in (0, 64):
        for d in range(32):
            R[blk + d, blk + d + 32] = -1.0
            R[blk + d + 32, blk + d] = 1.0
    rotT = R.T.copy()
    crows = np.zeros((128, 384), dtype=np.float32)
    crows[0, 0:128] = 1.0
    crows[64, 0:128] = 1.0
    crows[64, 128:192] = 1.0
    crows[64, 320:384] = 1.0
    crows[:, 300] = 1.0
    return ident, cos_fm, sin_fm, crows, rotT


_NC_CACHE = {}


def kernel(L0_attn_x, L0_attn_W_qkv, L0_attn_b_qkv, L0_attn_W_out, L0_attn_b_out,
           L0_ffn_clip_x, L0_ffn_clip_down, L0_ffn_w_gate, L0_ffn_w_up,
           L0_ffn_w_down, _debug=False):
    key = bool(_debug)
    if key not in _NC_CACHE:
        _NC_CACHE[key] = build_nc(debug=_debug)
    nc = _NC_CACHE[key]
    ident, cos_fm, sin_fm, crows, rotT = _host_consts()
    x = np.ascontiguousarray(np.asarray(L0_attn_x, dtype=np.float32))
    base = {
        "wqkv": np.ascontiguousarray(np.asarray(L0_attn_W_qkv, np.float32)),
        "bqkv": np.ascontiguousarray(np.asarray(L0_attn_b_qkv, np.float32)),
        "wout": np.ascontiguousarray(np.asarray(L0_attn_W_out, np.float32)),
        "bout": np.ascontiguousarray(np.asarray(L0_attn_b_out, np.float32)),
        "clipx": np.ascontiguousarray(np.asarray(L0_ffn_clip_x, np.float32)),
        "clipd": np.ascontiguousarray(np.asarray(L0_ffn_clip_down, np.float32)),
        "wg": np.ascontiguousarray(np.asarray(L0_ffn_w_gate, np.float32)),
        "wu": np.ascontiguousarray(np.asarray(L0_ffn_w_up, np.float32)),
        "wd": np.ascontiguousarray(np.asarray(L0_ffn_w_down, np.float32)),
        "cident": ident, "ccos": cos_fm, "csin": sin_fm, "crows": crows,
        "crot": rotT,
    }
    in_maps = []
    for b in range(8):
        m = dict(base)
        m["x"] = np.ascontiguousarray(x[b])
        in_maps.append(m)
    res = run_bass_kernel_spmd(nc, in_maps, core_ids=list(range(8)))
    out = np.stack([res.results[b]["out"] for b in range(8)], axis=0)
    if _debug:
        extra = {kk: np.stack([res.results[b]["dbg_" + kk] for b in range(8)])
                 for kk in ("x1", "x2", "sc")}
        return out, extra
    return out
